# version 25
# baseline (speedup 1.0000x reference)
import contextlib
import numpy as np

import concourse.bass as bass
import concourse.mybir as mybir
from concourse.bass_utils import run_bass_kernel_spmd

F32 = mybir.dt.float32
BF = mybir.dt.bfloat16
I32 = mybir.dt.int32
AF = mybir.ActivationFunctionType
ALU = mybir.AluOpType
AX = mybir.AxisListType

S = 8192
D = 1024
C = 128
NCH = S // C
ES = 256
FC = 8
BLK = 512
NBLK = S // BLK
ETA = 0.01 * 2.0 / 1024.0
GT = 512
NST = 4
TOK = 2048

ENGS = ("pe", "dve", "act", "sync", "pool")
SEMS = ("sPE", "sDVE", "sACT", "sPL", "cc",
        "qC", "qTh", "qXA", "qXB", "qXTa", "qXTb",
        "qSO00", "qSO01", "qSO10", "qSO11", "qSO20", "qSO21", "qSO30", "qSO31",
        "qW1a", "qW1b", "qW2a", "qW2b", "qW2c", "qSg", "qXwA", "qXwB", "qXf", "qTkA", "qTkB",
        "qKI", "qKTa", "qKTb", "qKTc", "ck",
        "qOutA", "qOutB")


class Em:

    def __init__(self, sems, active, eng):
        self.sems = sems
        self.active = active
        self.eng = eng
        self.c = {k: 0 for k in SEMS}
        self.lw = {}

    def on(self, engine, fn, wait=(), inc=None, n=1):
        if engine == self.active:
            for s, v in wait:
                if v and v > self.lw.get(s, 0):
                    self.eng.wait_ge(self.sems[s], v)
                    self.lw[s] = v
            inst = fn(self.eng)
            if inst is not None and inc is not None:
                inst.then_inc(self.sems[inc], n)
        if inc is not None:
            self.c[inc] += n
            return self.c[inc]
        return None


def build_nc():
    nc = bass.Bass("TRN2", target_bir_lowering=False)

    def inp(name, shape, dt=F32):
        return nc.dram_tensor(name, list(shape), dt, kind="ExternalInput")

    dram = dict(
        xT=inp("xT", (D, S)),
        xsl=inp("xsl", (S, ES)),
        TkT=inp("TkT", (D, ES)),
        TqT=inp("TqT", (D, ES)),
        TvTs=inp("TvTs", (D, ES)),
        W0Ts=inp("W0Ts", (D, ES)),
        b0s=inp("b0s", (1, ES)),
        sW1=inp("sW1", (D, 512)),
        sb1T=inp("sb1T", (128, 4)),
        sW2=inp("sW2", (512, D)),
        sb2=inp("sb2", (1, D)),
        rW1=inp("rW1", (8, D, 512)),
        rb1T=inp("rb1T", (128, 32)),
        rW2=inp("rW2", (8, 512, D)),
        rb2=inp("rb2", (8, D)),
        rWT=inp("rWT", (D, 8)),
        ebr=inp("ebr", (1, 8)),
        cIdent=inp("cIdent", (128, 128)),
        cMaskL=inp("cMaskL", (128, 128)),
        cMaskU=inp("cMaskU", (128, 128)),
        cMaskUI=inp("cMaskUI", (128, 128)),
        cIota=inp("cIota", (128, 8)),
        cOnes=inp("cOnes", (128, 128)),
        cSel=inp("cSel", (8, 8 * 128)),
        tokOff=inp("tokOff", (1, 1), I32),
        out=nc.dram_tensor("out", [TOK, D], F32, kind="ExternalOutput"),
        sOut=nc.dram_tensor("sOut", [S, ES], BF, kind="Internal"),
        sGat=nc.dram_tensor("sGat", [4 * S, ES], BF, kind="Internal"),
        x1Tmp=nc.dram_tensor("x1Tmp", [TOK, D], BF, kind="Internal"),
        kqI00=nc.dram_tensor("kqI00", [512, 512], BF, kind="Internal"),
        kqG00=nc.dram_tensor("kqG00", [2048, 512], BF, kind="Internal"),
        kqI0a=nc.dram_tensor("kqI0a", [512, 512], BF, kind="Internal"),
        kqI0b=nc.dram_tensor("kqI0b", [512, 1024], BF, kind="Internal"),
        kqI1=nc.dram_tensor("kqI1", [512, 2048], BF, kind="Internal"),
        kqI2=nc.dram_tensor("kqI2", [512, 2048], BF, kind="Internal"),
        kqI3=nc.dram_tensor("kqI3", [512, 2048], BF, kind="Internal"),
        kqG0a=nc.dram_tensor("kqG0a", [2048, 512], BF, kind="Internal"),
        kqG0b=nc.dram_tensor("kqG0b", [2048, 1024], BF, kind="Internal"),
        kqG1=nc.dram_tensor("kqG1", [2048, 2048], BF, kind="Internal"),
        kqG2=nc.dram_tensor("kqG2", [2048, 2048], BF, kind="Internal"),
        kqG3=nc.dram_tensor("kqG3", [2048, 2048], BF, kind="Internal"),
    )

    with contextlib.ExitStack() as st:
        sems = {k: st.enter_context(nc.semaphore(k)) for k in SEMS}

        def sb(name, shape, dt):
            return st.enter_context(nc.sbuf_tensor(name, list(shape), dt))

        def psm(name, shape, dt=F32):
            return st.enter_context(nc.psum_tensor(name, list(shape), dt))

        rBase = st.enter_context(nc.gpsimd.register("rBase"))
        rCur = st.enter_context(nc.gpsimd.register("rCur"))

        T = dict(
            rBase=rBase, rCur=rCur,
            identB=sb("identB", (128, 128), BF),
            maskL=sb("maskL", (128, 128), F32),
            maskU=sb("maskU", (128, 128), F32),
            maskUI=sb("maskUI", (128, 128), F32),
            iota8=sb("iota8", (128, 8), F32),
            onesR=sb("onesR", (1, 128), BF),
            onesC=sb("onesC", (128, 1), BF),
            tokI=sb("tokI", (1, 1), I32),
            selB=sb("selB", (8, 8 * 128), BF),
            TkTb=sb("TkTb", (128, FC * ES), BF),
            TqTb=sb("TqTb", (128, FC * ES), BF),
            TvTb=sb("TvTb", (128, FC * ES), BF),
            Wt=sb("Wt", (128, FC * ES), F32),
            WtB=sb("WtB", (128, FC * ES), BF),
            bF=sb("bF", (1, ES), F32),
            bB=sb("bB", (1, ES), BF),
            xTb=[sb("xTb0", (128, FC * BLK), BF), sb("xTb1", (128, FC * BLK), BF)],
            KQt=[sb("KQt0", (128, 16 * 128), BF), sb("KQt1", (128, 16 * 128), BF),
                 sb("KQt2", (128, 16 * 128), BF)],
            kqOwn=sb("kqOwn", (128, 4 * 2048), BF),
            VhS=[sb("VhS0", (128, 16 * ES), BF), sb("VhS1", (128, 16 * ES), BF)],
            xslT=[sb("xslT0", (128, ES), F32), sb("xslT1", (128, ES), F32)],
            KtS=[sb("KtS0", (128, FC * 128), BF), sb("KtS1", (128, FC * 128), BF)],
            Ml=[sb("Ml0", (128, 128), BF), sb("Ml1", (128, 128), BF)],
            Mu=[sb("Mu0", (128, 128), BF), sb("Mu1", (128, 128), BF)],
            U=[sb("U0", (128, 128), BF), sb("U1", (128, 128), BF)],
            LT=[sb("LT0", (128, 128), BF), sb("LT1", (128, 128), BF)],
            Pl=[sb("Pla", (128, 128), BF), sb("Plb", (128, 128), BF)],
            Pu=[sb("Pua", (128, 128), BF), sb("Pub", (128, 128), BF)],
            Rb=[sb("Rb0", (128, ES), BF), sb("Rb1", (128, ES), BF)],
            WmB=[sb("WmB0", (128, ES), BF), sb("WmB1", (128, ES), BF)],
            sTile=[sb("sT0", (128, ES), BF), sb("sT1", (128, ES), BF)],
            rw1b=[sb("rw1b0", (128, FC * 512), BF), sb("rw1b1", (128, FC * 512), BF)],
            rw2b=[sb("rw2b0", (128, 4 * D), BF), sb("rw2b1", (128, 4 * D), BF),
                  sb("rw2b2", (128, 4 * D), BF)],
            rWTb=sb("rWTb", (128, FC * 8), BF),
            sb1c=sb("sb1c", (128, 4), F32),
            rb1c=sb("rb1c", (128, 32), F32),
            sb2b=sb("sb2b", (1, D), BF),
            rb2b=sb("rb2b", (8, D), BF),
            ebB=sb("ebB", (1, 8), BF),
            ebBc=sb("ebBc", (128, 8), F32),
            sg=[sb("sg0", (128, D), BF)],
            x1f=sb("x1f", (128, FC * GT), BF),
            x1tok=[sb("x1t0", (128, D), BF), sb("x1t1", (128, D), BF)],
            hfA=sb("hfA", (128, 2 * 4 * GT), BF),
            bcT=sb("bcT", (128, GT), BF),
            yacc=sb("yacc", (128, NST * D), F32),
            outT=[sb("outT0", (128, D), F32)],
            LG=sb("LG", (128, NST * 8), F32),
            ACTs=sb("ACTs", (128, NST * 8), F32),
            mx8=sb("mx8", (128, 8), F32),
            idx8=sb("idx8", (128, 8), mybir.dt.uint32),
            idxF=sb("idxF", (128, 2), F32),
            f1=sb("f1", (128, 8), F32),
            f2=sb("f2", (128, 8), F32),
            CTpad=sb("CTpad", (128, 128), BF),
            CT=sb("CT", (8, NST * 128), BF),
            stats=sb("stats", (128, 6), F32),
            epsT=sb("epsT", (128, 1), F32),
            oneT=sb("oneT", (128, 1), F32),
            pPj=[psm("pPj0", (128, 512)), psm("pPj1", (128, 512))],
            pAB=psm("pAB", (128, 256)),
            pTa=psm("pTa", (128, 256)),
            pTb=psm("pTb", (128, 256)),
            pR=psm("pR", (128, 256)),
            pOut=psm("pOut", (128, 256)),
            pKt=psm("pKt", (128, 128), BF),
        )

        with nc.Block("mk") as blk:
            def run(name):
                def body(eng):
                    em = Em(sems, name, eng)
                    emit_all(em, dram, T)
                return body
            blk.tensor(run("pe"))
            blk.vector(run("dve"))
            blk.scalar(run("act"))
            blk.sync(run("sync"))
            blk.gpsimd(run("pool"))

    return nc


def emit_all(em, dram, T):
    identB = T["identB"].ap()
    onesR = T["onesR"].ap()
    onesC = T["onesC"].ap()
    maskL = T["maskL"].ap()
    maskU = T["maskU"].ap()
    maskUI = T["maskUI"].ap()
    iota8 = T["iota8"].ap()
    stats = T["stats"].ap()

    xT3 = dram["xT"].ap().rearrange("(c p) t -> p c t", p=128)
    TkT3 = dram["TkT"].ap().rearrange("(c p) e -> p c e", p=128)
    TqT3 = dram["TqT"].ap().rearrange("(c p) e -> p c e", p=128)
    KQIN = [dram["kqI00"], dram["kqI0a"], dram["kqI0b"],
            dram["kqI1"], dram["kqI2"], dram["kqI3"]]
    KQGT = [dram["kqG00"], dram["kqG0a"], dram["kqG0b"],
            dram["kqG1"], dram["kqG2"], dram["kqG3"]]

    def kq_src(c):
        if c < 4:
            return 0, c * 128
        if c < 8:
            return 1, (c - 4) * 128
        if c < 16:
            return 2, (c - 8) * 128
        return 3 + (c // 16 - 1), (c % 16) * 128
    TvT3 = dram["TvTs"].ap().rearrange("(c p) e -> p c e", p=128)
    W0T3 = dram["W0Ts"].ap().rearrange("(c p) e -> p c e", p=128)
    x1T = dram["x1Tmp"].ap()
    sGatH = dram["sGat"]

    St = {
        "xsl_rdy": {}, "xtb_rdy": {},
        "wtb_rdy": 0, "bB_rdy": 0, "dbadd": 0,
        "lastchain_out": 0, "last_revac": 0, "tbuild_prev": 0,
        "drip": [],
        "bankrel": {0: [], 1: []}, "bankrr": 0,
        "ffnq": [],
    }

    em.on("sync", lambda e: e.dma_start(out=maskL, in_=dram["cMaskL"].ap()), inc="qC", n=16)
    em.on("sync", lambda e: e.dma_start(out=maskU, in_=dram["cMaskU"].ap()), inc="qC", n=16)
    em.on("sync", lambda e: e.dma_start(out=maskUI, in_=dram["cMaskUI"].ap()), inc="qC", n=16)
    em.on("sync", lambda e: e.dma_start(out=iota8, in_=dram["cIota"].ap()), inc="qC", n=16)
    em.on("sync", lambda e: e.dma_start(
        out=T["Wt"].ap().rearrange("p (c e) -> p c e", c=FC), in_=W0T3), inc="qC", n=16)
    em.on("sync", lambda e: e.dma_start(out=T["bF"].ap(), in_=dram["b0s"].ap()), inc="qC", n=16)
    em.on("sync", lambda e: e.dma_start(out=T["tokI"].ap(), in_=dram["tokOff"].ap()), inc="qC", n=16)
    em.on("sync", lambda e: e.dma_start(out=T["sb1c"].ap(), in_=dram["sb1T"].ap()), inc="qC", n=16)
    em.on("sync", lambda e: e.dma_start(out=T["rb1c"].ap(), in_=dram["rb1T"].ap()), inc="qC", n=16)
    qs_consts = em.c["qC"]
    em.on("sync", lambda e: e.dma_start(out=T["xslT"][0].ap(), in_=dram["xsl"].ap()[0:128, :]), inc="qXA", n=16)
    St["xsl_rdy"][0] = ("qXA", em.c["qXA"])

    em.on("pool", lambda e: e.dma_start(out=identB, in_=dram["cIdent"].ap()), inc="qTh", n=16)
    em.on("pool", lambda e: e.dma_start(
        out=T["TkTb"].ap().rearrange("p (c e) -> p c e", c=FC), in_=TkT3), inc="qTh", n=16)
    em.on("pool", lambda e: e.dma_start(
        out=T["TqTb"].ap().rearrange("p (c e) -> p c e", c=FC), in_=TqT3), inc="qTh", n=16)
    em.on("pool", lambda e: e.dma_start(
        out=T["TvTb"].ap().rearrange("p (c e) -> p c e", c=FC), in_=TvT3), inc="qTh", n=16)
    St["qp_kqv"] = em.c["qTh"]
    em.on("pool", lambda e: e.dma_start(
        out=T["rWTb"].ap().rearrange("p (c e) -> p c e", c=FC),
        in_=dram["rWT"].ap().rearrange("(c p) e -> p c e", p=128)), inc="qTh", n=16)
    em.on("pool", lambda e: e.dma_start(out=T["sb2b"].ap(), in_=dram["sb2"].ap()), inc="qTh", n=16)
    em.on("pool", lambda e: e.dma_start(out=T["rb2b"].ap(), in_=dram["rb2"].ap()), inc="qTh", n=16)
    em.on("pool", lambda e: e.dma_start(out=T["ebB"].ap(), in_=dram["ebr"].ap()), inc="qTh", n=16)
    em.on("pool", lambda e: e.dma_start(out=T["selB"].ap(), in_=dram["cSel"].ap()), inc="qTh", n=16)
    St["qp_theta"] = em.c["qTh"]
    em.on("pool", lambda e: e.reg_load(T["rBase"], T["tokI"].ap()[0:1, 0:1]),
          wait=[("qC", qs_consts)])
    em.on("pool", lambda e: e.dma_start(
        out=T["xTb"][0].ap().rearrange("p (c t) -> p c t", c=FC),
        in_=xT3[:, :, 0:BLK]), inc="qXTa", n=16)
    St["xtb_rdy"][0] = ("qXTa", em.c["qXTa"])

    em.on("dve", lambda e: e.memset(onesR, 1.0), inc="sDVE")
    em.on("dve", lambda e: e.memset(onesC, 1.0), inc="sDVE")
    St["ones_rdy"] = em.c["sDVE"]
    for fc in range(FC):
        em.on("act", lambda e, fc=fc: e.copy(
            T["WtB"].ap()[:, fc * ES:(fc + 1) * ES],
            T["Wt"].ap()[:, fc * ES:(fc + 1) * ES]),
            wait=[("qC", qs_consts)], inc="sACT")
        St[("wtbfc", fc)] = ("sACT", em.c["sACT"])
    em.on("act", lambda e: e.copy(T["bB"].ap(), T["bF"].ap()), inc="sACT")
    St["wtb_rdy"] = em.c["sACT"]
    St["bB_rdy"] = em.c["sACT"]
    em.on("dve", lambda e: e.memset(T["CTpad"].ap(), 0.0), inc="sDVE")
    St["ctpad0"] = em.c["sDVE"]
    em.on("dve", lambda e: e.memset(T["epsT"].ap(), 1e-5), inc="sDVE")
    em.on("dve", lambda e: e.memset(T["oneT"].ap(), 1.0), inc="sDVE")
    St["eps_rdy"] = em.c["sDVE"]
    em.on("pe", lambda e: e.matmul(T["pAB"].ap()[:, 0:8], onesR, T["ebB"].ap(),
                                   start=True, stop=True),
          wait=[("qTh", St["qp_theta"]), ("sDVE", St["ones_rdy"])], inc="sPE")
    em.on("dve", lambda e: e.tensor_copy(T["ebBc"].ap(), T["pAB"].ap()[:, 0:8]),
          wait=[("sPE", em.c["sPE"])], inc="sDVE")
    St["ebbc"] = em.c["sDVE"]
    St["dbadd"] = em.c["sDVE"]

    def claim_bank():
        k = St["bankrr"]
        St["bankrr"] = 1 - k
        return k, list(St["bankrel"][k])

    def release_bank(k, conds):
        St["bankrel"][k] = list(conds)

    RG = [[0, 1, 2, 3], [4, 5, 6, 7]]

    def m_k(fc):
        return (fc // 2) * 4 + (fc % 2)

    def m_q(fc):
        return m_k(fc) + 2

    def make_proj_units(b):
        p = b % 2
        xtb = T["xTb"][p].ap()
        units = []
        war_dve = St["last_revac"]
        kqi = St.get("kqi_cnt", 0)
        xw = [St["xtb_rdy"][b], ("qTh", St["qp_kqv"])]
        vs = (b // 4) % 2
        for half, thb in ((0, T["TkTb"]), (1, T["TqTb"])):
            for ec in range(2):
                mms = []
                for dc in range(FC):
                    def mm(e, dc=dc, ec=ec, thb=thb, bank=None):
                        return e.matmul(
                            T["pPj"][bank].ap(),
                            thb.ap()[:, dc * ES + ec * 128: dc * ES + ec * 128 + 128],
                            xtb[:, dc * BLK:(dc + 1) * BLK],
                            start=(dc == 0), stop=(dc == FC - 1))
                    mms.append(mm)
                co = (half * 2 + ec) * 2048 + (b % 4) * BLK

                def ev(e, co=co, bank=None):
                    return e.copy(T["kqOwn"].ap()[:, co:co + BLK],
                                  T["pPj"][bank].ap())
                units.append(dict(mms=mms, ev=ev, xw=xw, war_dve=0, kqi=kqi))
        for tq in range(4):
            mms = []
            for dc in range(FC):
                def mmv(e, dc=dc, tq=tq, bank=None):
                    return e.matmul(
                        T["pPj"][bank].ap()[:, 0:ES],
                        xtb[:, dc * BLK + tq * 128: dc * BLK + tq * 128 + 128],
                        T["TvTb"].ap()[:, dc * ES:(dc + 1) * ES],
                        start=(dc == 0), stop=(dc == FC - 1))
                mms.append(mmv)
            vo = ((b % 4) * 4 + tq) * ES

            def evv(e, vo=vo, vs=vs, bank=None):
                return e.activation(
                    T["VhS"][vs].ap()[:, vo:vo + ES],
                    T["pPj"][bank].ap()[:, 0:ES], AF.Identity, scale=ETA)
            units.append(dict(mms=mms, ev=evv, xw=xw, war_dve=war_dve, kqi=0))
        return units

    def run_proj_unit(u):
        k, rel = claim_bank()
        w0 = list(u["xw"]) + rel
        for i, mm in enumerate(u["mms"]):
            em.on("pe", lambda e, mm=mm, k=k: mm(e, bank=k),
                  wait=w0 if i == 0 else [],
                  inc="sPE" if i == len(u["mms"]) - 1 else None)
        mmd = em.c["sPE"]
        wv = [("sPE", mmd)]
        if u["war_dve"]:
            wv.append(("sDVE", u["war_dve"]))
        if u["kqi"]:
            wv.append(("qKI", u["kqi"]))
        em.on("act", lambda e, k=k: u["ev"](e, bank=k), wait=wv, inc="sACT")
        release_bank(k, [("sACT", em.c["sACT"])])

    def drip(n):
        lst = St["drip"]
        k = 0
        while lst and k < n:
            run_proj_unit(lst.pop(0))
            k += 1
        if not lst and St.get("drip_block") is not None:
            St[("projmm", St["drip_block"])] = em.c["sPE"]
            St["drip_block"] = None

    def start_block(b):
        if b >= NBLK:
            return
        drip(10 ** 9)
        war = St.get(("projmm", b - 2), 0)

        def ld(e, b=b):
            return e.dma_start(
                out=T["xTb"][b % 2].ap().rearrange("p (c t) -> p c t", c=FC),
                in_=xT3[:, :, b * BLK:(b + 1) * BLK])
        qn = "qXTa" if b % 2 == 0 else "qXTb"
        em.on("pool", ld, wait=[("sPE", war)], inc=qn, n=16)
        St["xtb_rdy"][b] = (qn, em.c[qn])
        St["drip"] = make_proj_units(b)
        St["drip_block"] = b

    def kqO3(c0, w):
        return T["kqOwn"].ap().rearrange("p (q t) -> p q t", q=4)[:, :, c0:c0 + w]

    def stage_kq_dma(s):
        drip(10 ** 9)
        kq_ev = em.c["sACT"]
        St[("vhs", s)] = kq_ev
        em.on("sync", lambda e, s=s: e.dma_start(
            out=KQIN[s + 2].ap().rearrange("(q p) t -> p q t", p=128),
            in_=kqO3(0, 2048)),
            wait=[("sACT", kq_ev)], inc="qKI", n=16)
        St["kqi_cnt"] = em.c["qKI"]

    def stage_kq_ag(buf):
        em.on("pool", lambda e, buf=buf: e.collective_compute(
            "AllGather", ALU.bypass, replica_groups=RG,
            ins=[KQIN[buf].ap()], outs=[KQGT[buf].ap()]),
            wait=[("qKI", em.c["qKI"])], inc="ck", n=1)

    def tile_load(c):
        if c >= NCH:
            return
        buf, off = kq_src(c)
        pt = c % 3
        qn = ("qKTa", "qKTb", "qKTc")[pt]
        src = KQGT[buf].ap().rearrange("(m p) t -> p m t", p=128)
        em.on("sync", lambda e, src=src, off=off, pt=pt: e.dma_start(
            out=T["KQt"][pt].ap().rearrange("p (m t) -> p m t", m=16),
            in_=src[:, :, off:off + 128]),
            wait=[("ck", buf + 1), ("sPE", St.get(("kqt_war", pt), 0))],
            inc=qn, n=16)
        St[("kqt_rdy", c)] = (qn, em.c[qn])

    St["drip"] = make_proj_units(0)
    St["drip_block"] = 0
    drip(10 ** 9)
    em.on("sync", lambda e: e.dma_start(
        out=KQIN[0].ap().rearrange("(q p) t -> p q t", p=128),
        in_=kqO3(0, 512)),
        wait=[("sACT", em.c["sACT"])], inc="qKI", n=16)
    em.on("pool", lambda e: e.collective_compute(
        "AllGather", ALU.bypass, replica_groups=RG,
        ins=[KQIN[0].ap()], outs=[KQGT[0].ap()]),
        wait=[("qKI", em.c["qKI"])], inc="ck", n=1)
    start_block(1)
    drip(10 ** 9)
    kq_ev0 = em.c["sACT"]
    em.on("sync", lambda e: e.dma_start(
        out=KQIN[1].ap().rearrange("(q p) t -> p q t", p=128),
        in_=kqO3(512, 512)),
        wait=[("sACT", kq_ev0)], inc="qKI", n=16)
    em.on("pool", lambda e: e.collective_compute(
        "AllGather", ALU.bypass, replica_groups=RG,
        ins=[KQIN[1].ap()], outs=[KQGT[1].ap()]),
        wait=[("qKI", em.c["qKI"])], inc="ck", n=1)
    start_block(2)
    drip(10 ** 9)
    start_block(3)
    drip(10 ** 9)
    St[("vhs", 0)] = em.c["sACT"]
    em.on("sync", lambda e: e.dma_start(
        out=KQIN[2].ap().rearrange("(q p) t -> p q t", p=128),
        in_=kqO3(1024, 1024)),
        wait=[("sACT", St[("vhs", 0)])], inc="qKI", n=16)
    St["kqi_cnt"] = em.c["qKI"]
    em.on("pool", lambda e: e.collective_compute(
        "AllGather", ALU.bypass, replica_groups=RG,
        ins=[KQIN[2].ap()], outs=[KQGT[2].ap()]),
        wait=[("qKI", em.c["qKI"])], inc="ck", n=1)
    tile_load(0)
    tile_load(1)

    def pf_gram(c):
        if c >= NCH:
            return
        pi = c % 2
        kq = T["KQt"][c % 3]
        pAB = T["pAB"].ap()
        for fc in range(FC):
            w = [St[("kqt_rdy", c)], ("sDVE", St["dbadd"])] if fc == 0 else []
            off = m_k(fc) * 128
            em.on("pe", lambda e, fc=fc, off=off: e.matmul(
                pAB[:, 0:256],
                kq.ap()[:, off:off + 128],
                bass.AP(kq, off, [[16 * 128, 128], [256, 2], [1, 128]]),
                start=(fc == 0), stop=(fc == FC - 1)),
                wait=w, inc="sPE" if fc == FC - 1 else None)
        ab_done = em.c["sPE"]
        em.on("dve", lambda e: e.scalar_tensor_tensor(
            T["Ml"][pi].ap(), pAB[:, 0:128], 1.0, maskL, ALU.add, ALU.mult),
            wait=[("sPE", ab_done), ("sPE", St.get(("uread", pi), 0)),
                  ("sDVE", St.get(("maskwaw", pi), 0))], inc="sDVE")
        em.on("dve", lambda e: e.scalar_tensor_tensor(
            T["Mu"][pi].ap(), pAB[:, 0:128], 1.0, maskU, ALU.add, ALU.mult), inc="sDVE")
        masks_done = em.c["sDVE"]
        St[("maskwaw", pi)] = masks_done
        St[("masks", c)] = masks_done
        em.on("dve", lambda e: e.tensor_tensor(
            T["U"][pi].ap(), identB, T["Mu"][pi].ap(), ALU.subtract),
            wait=[("sDVE", masks_done), ("qTh", St["qp_theta"])], inc="sDVE")
        St[("uadd", c, 0)] = em.c["sDVE"]
        em.on("dve", lambda e: e.scalar_tensor_tensor(
            T["LT"][pi].ap(), pAB[:, 128:256], 1.0, maskUI, ALU.add, ALU.mult),
            wait=[("sPE", ab_done), ("sPE", St.get(("ltread", pi), 0)),
                  ("sDVE", St.get(("ltwaw", pi), 0))], inc="sDVE")
        St[("lt", c)] = em.c["sDVE"]
        St[("ltwaw", pi)] = em.c["sDVE"]

    def pf_steps(c):
        if c >= NCH:
            return
        pi = c % 2
        kq = T["KQt"][c % 3].ap()
        pTa, pTb, pKt = T["pTa"].ap(), T["pTb"].ap(), T["pKt"].ap()
        tb_war = St["tbuild_prev"]
        state = dict(cur_l=T["Ml"][pi].ap(), cur_u=T["Mu"][pi].ap(),
                     nxt=0, puev=0, plev=0)
        war_kt = St.get(("ktread", pi), 0)

        def t_round(j):
            cur_l, cur_u = state["cur_l"], state["cur_u"]
            w = [("sDVE", state["plev"])]
            if j == 1:
                w.append(("sDVE", St[("masks", c)]))
            em.on("pe", lambda e: e.matmul(
                pTb[:, 0:128], cur_u, cur_l, start=True, stop=True),
                wait=w, inc="sPE")
            sql_d = em.c["sPE"]
            npl, npu = T["Pl"][state["nxt"]].ap(), T["Pu"][state["nxt"]].ap()
            em.on("dve", lambda e: e.tensor_copy(npl, pTb[:, 0:128]),
                  wait=[("sPE", sql_d), ("sPE", tb_war),
                        ("sACT", St.get("puevwaw", 0)),
                        ("sDVE", St.get("pevwaw", 0))], inc="sDVE")
            plev = em.c["sDVE"]
            state["plev"] = plev
            St["pevwaw"] = plev
            if j < 6:
                em.on("pe", lambda e: e.matmul(
                    pTa[:, 128:256], cur_l, cur_u, start=True, stop=True),
                    wait=[("sDVE", St[("uadd", c, j - 1)]),
                          ("sACT", state["puev"])], inc="sPE")
                squ_d = em.c["sPE"]
                em.on("act", lambda e: e.copy(npu, pTa[:, 128:256]),
                      wait=[("sPE", squ_d), ("sACT", St.get("puevwaw", 0))],
                      inc="sACT")
                state["puev"] = em.c["sACT"]
                St["puevwaw"] = em.c["sACT"]
            state["cur_l"], state["cur_u"] = npl, npu
            state["nxt"] = 1 - state["nxt"]
            yield

            em.on("pe", lambda e, npl=npl: e.matmul(
                pTa[:, 0:128], npl, T["U"][pi].ap(), start=True, stop=True),
                wait=[("sDVE", plev), ("sDVE", St[("uadd", c, j - 1)]),
                      ("sACT", state["puev"])], inc="sPE")
            umm_d = em.c["sPE"]
            em.on("dve", lambda e: e.tensor_tensor(
                T["U"][pi].ap(), T["U"][pi].ap(), pTa[:, 0:128], ALU.add),
                wait=[("sPE", umm_d), ("sDVE", St[("uadd", c, j - 1)])], inc="sDVE")
            St[("uadd", c, j)] = em.c["sDVE"]
            if j == 6:
                St[("u_rdy", c)] = em.c["sDVE"]
                St["tbuild_prev"] = em.c["sPE"]
            yield

        def kt_step(fc):
            em.on("pe", lambda e, fc=fc: e.transpose(
                pKt, kq[:, m_k(fc) * 128:m_k(fc) * 128 + 128], identB),
                wait=[St[("kqt_rdy", c)], ("sACT", St.get("ktev_last", 0)),
                      ("qTh", St["qp_theta"])], inc="sPE")
            tr_d = em.c["sPE"]
            em.on("act", lambda e, fc=fc: e.copy(
                T["KtS"][pi].ap()[:, fc * 128:(fc + 1) * 128], pKt),
                wait=[("sPE", tr_d), ("sPE", war_kt),
                      ("sACT", St.get(("ktwaw", pi), 0))], inc="sACT")
            St["ktev_last"] = em.c["sACT"]
            if fc == FC - 1:
                St[("kt_rdy", c)] = ("sACT", em.c["sACT"])
                St[("ktwaw", pi)] = em.c["sACT"]
            yield

        for j in range(1, 7):
            yield from t_round(j)
            if j <= 4:
                yield from kt_step(j - 1)
                yield from kt_step(j + 3)

    def chain_ctx(c):
        return dict(
            pi=c % 2, b=c // 4, sub=c % 4,
            kq=T["KQt"][c % 3].ap(),
            pR=T["pR"].ap(), pOut=T["pOut"].ap(), pAB=T["pAB"].ap(),
            wtb=T["WtB"].ap(), wt=T["Wt"].ap())

    def chain_r(c, X):
        pi, b, pR, wtb = X["pi"], X["b"], X["pR"], X["wtb"]
        w0 = [("sACT", St["bB_rdy"]), St[("kqt_rdy", c)],
              ("sDVE", St.get(("sadd", c - 1, 6), 0)),
              ("sDVE", St["dbadd"]),
              ("sACT", St["wtb_rdy"]), ("sDVE", St.get("wtb_rdy_d", 0))]
        for fc in range(FC):
            w = w0 if fc == 0 else []
            em.on("pe", lambda e, fc=fc: e.matmul(
                pR, X["kq"][:, m_k(fc) * 128:m_k(fc) * 128 + 128],
                wtb[:, fc * ES:(fc + 1) * ES],
                start=(fc == 0), stop=False), wait=w)
        em.on("pe", lambda e: e.matmul(pR, onesR, T["bB"].ap(),
                                       start=False, stop=True),
              wait=[("sDVE", St["ones_rdy"])], inc="sPE")
        r_done = em.c["sPE"]
        em.on("dve", lambda e: e.scalar_tensor_tensor(
            T["Rb"][pi].ap(), pR, -ETA,
            T["VhS"][(c // 16) % 2].ap()[:, (c % 16) * ES:(c % 16 + 1) * ES],
            ALU.mult, ALU.add),
            wait=[("sPE", r_done), ("sDVE", St.get(("rbwaw", pi), 0)),
                  ("sACT", St[("vhs", c // 16)])], inc="sDVE")
        X["r_evac"] = em.c["sDVE"]
        St[("rbwaw", pi)] = X["r_evac"]
        St["last_revac"] = X["r_evac"]

    def chain_outq(c, X):
        pOut, wtb = X["pOut"], X["wtb"]
        wo = [("sDVE", St.get(("sadd", c - 1, 7), 0))]
        for fc in range(FC):
            w = wo if fc == 0 else []
            em.on("pe", lambda e, fc=fc: e.matmul(
                pOut, X["kq"][:, m_q(fc) * 128:m_q(fc) * 128 + 128],
                wtb[:, fc * ES:(fc + 1) * ES],
                start=(fc == 0), stop=False), wait=w)
        em.on("pe", lambda e: e.matmul(pOut, onesR, T["bB"].ap(),
                                       start=False, stop=False),
              wait=[("sDVE", St["ones_rdy"])])

    def chain_solve(c, X):
        pi, pR, pOut = X["pi"], X["pR"], X["pOut"]
        em.on("pe", lambda e: e.matmul(
            pR, T["U"][pi].ap(), T["Rb"][pi].ap(), start=True, stop=True),
            wait=[("sDVE", X["r_evac"]), ("sDVE", St[("u_rdy", c)])], inc="sPE")
        wm_done = em.c["sPE"]
        St[("uread", pi)] = wm_done
        em.on("dve", lambda e: e.tensor_copy(T["WmB"][pi].ap(), pR),
              wait=[("sPE", wm_done), ("sPE", St.get(("wmbread", pi), 0))],
              inc="sDVE")
        X["wmb_d"] = em.c["sDVE"]
        em.on("pe", lambda e: e.matmul(
            pOut, T["LT"][pi].ap(), T["WmB"][pi].ap(), start=False, stop=True),
            wait=[("sDVE", X["wmb_d"]), ("sDVE", St[("lt", c)])], inc="sPE")
        out_done = em.c["sPE"]
        X["out_done"] = out_done
        St["lastchain_out"] = out_done
        St[("ltread", pi)] = out_done
        St[("kqt_war", c % 3)] = out_done
        em.on("dve", lambda e: e.tensor_tensor(
            T["sTile"][pi].ap(), pOut, T["xslT"][pi].ap(), ALU.add),
            wait=[("sPE", out_done), St["xsl_rdy"][c],
                  ("sDVE", St.get(("stwaw", pi), 0)),
                  (f"qSO{(c - 2) // 16}{c % 2}",
                   em.c[f"qSO{(c - 2) // 16}{c % 2}"]) if c >= 2 else ("qC", 0)],
            inc="sDVE")
        s_evac = em.c["sDVE"]
        X["s_evac"] = s_evac
        St[("stwaw", pi)] = s_evac
        St[("sevac", c)] = s_evac
        qso = f"qSO{c // 16}{c % 2}"
        em.on("sync", lambda e, c=c: e.dma_start(
            out=dram["sOut"].ap()[c * 128:(c + 1) * 128, :], in_=T["sTile"][pi].ap()),
            wait=[("sDVE", s_evac)], inc=qso, n=16)
        if c + 1 < NCH:
            qx = "qXA" if (c + 1) % 2 == 0 else "qXB"
            em.on("sync", lambda e, c=c: e.dma_start(
                out=T["xslT"][1 - pi].ap(),
                in_=dram["xsl"].ap()[(c + 1) * 128:(c + 2) * 128, :]),
                wait=[("sDVE", St.get(("sevac", c - 1), 0))], inc=qx, n=16)
            St["xsl_rdy"][c + 1] = (qx, em.c[qx])

    def chain_dw(c, X, fcs):
        pi, pR, pOut, wtb, wt = X["pi"], X["pR"], X["pOut"], X["wtb"], X["wt"]
        kt = T["KtS"][pi].ap()
        for fc in fcs:
            bank = pR if fc % 2 == 0 else pOut
            w = [St[("kt_rdy", c)], ("sDVE", X["wmb_d"])]
            if fc == 1:
                w.append(("sDVE", X["s_evac"]))
            elif fc >= 2:
                w.append(("sDVE", St[("sadd", c, fc - 2)]))
            em.on("pe", lambda e, fc=fc, bank=bank: e.matmul(
                bank, kt[:, fc * 128:(fc + 1) * 128], T["WmB"][pi].ap(),
                start=True, stop=True), wait=w, inc="sPE")
            dw_d = em.c["sPE"]
            em.on("dve", lambda e, fc=fc, bank=bank: e.tensor_tensor(
                wt[:, fc * ES:(fc + 1) * ES], wt[:, fc * ES:(fc + 1) * ES],
                bank, ALU.add),
                wait=[("sPE", dw_d), ("sDVE", St.get(("sadd", c - 1, fc), 0))],
                inc="sDVE")
            St[("sadd", c, fc)] = em.c["sDVE"]
            if fc % 2 == 0:
                em.on("act", lambda e, fc=fc: e.copy(
                    wtb[:, fc * ES:(fc + 1) * ES], wt[:, fc * ES:(fc + 1) * ES]),
                    wait=[("sDVE", St[("sadd", c, fc)]), ("sPE", X["out_done"]),
                          ("sACT", St.get("wtbwaw_a", 0))], inc="sACT")
                St["wtbwaw_a"] = em.c["sACT"]
                St[("wtbfc", fc)] = ("sACT", em.c["sACT"])
            else:
                em.on("dve", lambda e, fc=fc: e.tensor_copy(
                    wtb[:, fc * ES:(fc + 1) * ES], wt[:, fc * ES:(fc + 1) * ES]),
                    wait=[("sDVE", St[("sadd", c, fc)]), ("sPE", X["out_done"])],
                    inc="sDVE")
                St[("wtbfc", fc)] = ("sDVE", em.c["sDVE"])
        if fcs[-1] == FC - 1:
            St["wtb_rdy"] = em.c["sACT"]
            St["wtb_rdy_d"] = em.c["sDVE"]
            St[("ktread", pi)] = em.c["sPE"]

    def chain_db(c, X):
        pi, pAB = X["pi"], X["pAB"]
        em.on("pe", lambda e: e.matmul(
            pAB[0:1, 0:ES], onesC, T["WmB"][pi].ap(), start=True, stop=True),
            wait=[("sDVE", St.get(("lt", c + 1), St[("lt", c)])),
                  ("sDVE", St["ones_rdy"])], inc="sPE")
        db_d = em.c["sPE"]
        St[("wmbread", pi)] = db_d
        em.on("dve", lambda e: e.tensor_tensor(
            T["bF"].ap(), T["bF"].ap(), pAB[0:1, 0:ES], ALU.add),
            wait=[("sPE", db_d), ("sDVE", St["dbadd"])], inc="sDVE")
        St["dbadd"] = em.c["sDVE"]
        em.on("act", lambda e: e.copy(T["bB"].ap(), T["bF"].ap()),
              wait=[("sDVE", St["dbadd"]), ("sPE", X["out_done"]),
                    ("sACT", St["bB_rdy"])], inc="sACT")
        St["bB_rdy"] = em.c["sACT"]

    def stage_collective(sg, q):
        em.on("pool", lambda e, sg=sg, q=q: e.collective_compute(
            "AllGather", ALU.bypass,
            replica_groups=[[0, 1, 2, 3], [4, 5, 6, 7]],
            ins=[dram["sOut"].ap()[sg * 2048 + q * 512:sg * 2048 + (q + 1) * 512, :]],
            outs=[dram["sGat"].ap()[sg * 8192 + q * 2048:sg * 8192 + (q + 1) * 2048, :]]),
            wait=[(f"qSO{sg}0", 32 * (q + 1)), (f"qSO{sg}1", 32 * (q + 1))],
            inc="cc", n=1)

    EXPQ = [[8], [0, 1], [2, 3], [4, 5], [6, 7]]

    def ffn_stage_units(s):
        q = St["ffnq"]
        base = 16 * s + 17 if s < 3 else 60

        def tag(off, fn):
            cap = NCH + 1 if s < 3 else 63
            q.append((min(base + off, cap) if s < 3 or off <= 5 else NCH + 1, fn))

        for st4 in range(NST):
            tag(st4 // 2, lambda e=None, st4=st4: u_sgln(s, st4))
        tag(2, lambda: u_x1f(s))
        for st4 in range(NST):
            tag(3, lambda st4=st4: u_router(s, st4))
        tag(3, lambda: u_softplus(s))
        for st4 in range(NST):
            tag(4, lambda st4=st4: u_top2(s, st4))
        off = 5
        for gi, grp in enumerate(EXPQ):
            for e8 in grp:
                tag(max(off - 2, 0), lambda s=s, e8=e8: u_wload(s, e8))
                tag(off, lambda s=s, e8=e8, gi=gi: u_layer1(s, e8, gi))
                off += 1
            for st4 in range(NST):
                for half in range(2):
                    tag(off + (st4 // 2), lambda s=s, gi=gi, st4=st4, half=half:
                        u_yacc(s, gi, st4, half))
            off += 2
        for st4 in range(NST):
            tag(off + st4 // 2, lambda s=s, st4=st4: u_ln2(s, st4))

    def u_sgln(s, st4):
        slot = st4 % 2
        sgs = 0
        em.on("pool", lambda e: e.reg_add(T["rCur"], T["rBase"],
                                          s * 4 * 2048 * ES + st4 * 128 * ES))
        em.on("pool", lambda e: e.dma_start(
            out=T["sg"][sgs].ap().rearrange("p (r x) -> p r x", r=4),
            in_=bass.AP(sGatH, T["rCur"], [[ES, 128], [512 * ES, 4], [1, ES]])),
            wait=[("cc", 4 * s + 4)] + St.get("sg_rel", []),
            inc="qSg", n=16)
        sg_rdy = em.c["qSg"]
        x1rel = St.get(("x1tok_rel", slot), [])
        strel = St.get("stats_rel", [])
        em.on("act", lambda e: e.activation(
            T["x1tok"][slot].ap(), T["sg"][sgs].ap(), AF.Square,
            accum_out=stats[:, 1:2]),
            wait=[("qSg", sg_rdy)] + x1rel + strel, inc="sACT")
        sq_d = em.c["sACT"]
        em.on("dve", lambda e: e.reduce_sum(stats[:, 0:1], T["sg"][sgs].ap(), AX.X),
              wait=[("qSg", sg_rdy)] + strel, inc="sDVE")
        em.on("dve", lambda e: e.tensor_scalar_mul(stats[:, 2:3], stats[:, 0:1], 1.0 / D),
              wait=[("sDVE", em.c["sDVE"])], inc="sDVE")
        em.on("dve", lambda e: e.tensor_mul(stats[:, 3:4], stats[:, 2:3], stats[:, 2:3]),
              wait=[("sDVE", em.c["sDVE"])], inc="sDVE")
        em.on("dve", lambda e: e.scalar_tensor_tensor(
            stats[:, 4:5], stats[:, 1:2], 1.0 / D, stats[:, 3:4], ALU.mult, ALU.subtract),
            wait=[("sACT", sq_d), ("sDVE", em.c["sDVE"])], inc="sDVE")
        var_d = em.c["sDVE"]
        em.on("act", lambda e: e.activation(stats[:, 5:6], stats[:, 4:5], AF.Ln,
                                            bias=T["epsT"].ap()),
              wait=[("sDVE", var_d), ("sDVE", St["eps_rdy"])], inc="sACT")
        em.on("act", lambda e: e.activation(stats[:, 5:6], stats[:, 5:6], AF.Exp,
                                            scale=-0.5),
              wait=[("sACT", em.c["sACT"])], inc="sACT")
        rstd_d = em.c["sACT"]
        em.on("dve", lambda e: e.tensor_scalar(
            T["x1tok"][slot].ap(), T["sg"][sgs].ap(), stats[:, 2:3], stats[:, 5:6],
            ALU.subtract, ALU.mult),
            wait=[("sACT", rstd_d), ("sDVE", var_d)], inc="sDVE")
        ap_d = em.c["sDVE"]
        St["sg_rel"] = [("sACT", sq_d), ("sDVE", ap_d)]
        St["stats_rel"] = [("sACT", rstd_d), ("sDVE", ap_d)]
        qxw = "qXwA" if slot == 0 else "qXwB"
        em.on("sync", lambda e: e.dma_start(
            out=x1T[(s * NST + st4) * 128:(s * NST + st4) * 128 + 128, :],
            in_=T["x1tok"][slot].ap()),
            wait=[("sDVE", ap_d)], inc=qxw, n=16)
        St[("x1tok_rel", slot)] = [(qxw, em.c[qxw])]
        St[("x1w_done", s, st4)] = [("qXwA", em.c["qXwA"]), ("qXwB", em.c["qXwB"])]

    def u_x1f(s):
        war = St.get("x1f_war", 0)
        for dc in range(FC):
            em.on("sync", lambda e, dc=dc: e.dma_start(
                out=T["x1f"].ap()[:, dc * GT:(dc + 1) * GT],
                in_=x1T[s * GT:(s + 1) * GT, dc * 128:(dc + 1) * 128],
                transpose=True),
                wait=St[("x1w_done", s, NST - 1)] + [("sPE", war)],
                inc="qXf", n=16)
        St["x1f_rdy"] = em.c["qXf"]

    def u_router(s, st4):
        return _g_router(s, st4)

    def _g_router(s, st4):
        k, rel = claim_bank()
        pb = T["pPj"][k].ap()
        for dc in range(FC):
            w = ([("qXf", St["x1f_rdy"])] + rel) if dc == 0 else []
            em.on("pe", lambda e, dc=dc, st4=st4, pb=pb: e.matmul(
                pb[:, 0:8],
                T["x1f"].ap()[:, dc * GT + st4 * 128: dc * GT + st4 * 128 + 128],
                T["rWTb"].ap()[:, dc * 8:(dc + 1) * 8],
                start=(dc == 0), stop=(dc == FC - 1)),
                wait=w, inc="sPE" if dc == FC - 1 else None)
        rt_d = em.c["sPE"]
        em.on("dve", lambda e, st4=st4, pb=pb: e.tensor_tensor(
            T["LG"].ap()[:, st4 * 8:(st4 + 1) * 8], pb[:, 0:8],
            T["ebBc"].ap(), ALU.add),
            wait=[("sPE", rt_d), ("sDVE", St["ebbc"]),
                  ("sDVE", St.get("lg_free", 0))], inc="sDVE")
        St[("lg", s, st4)] = em.c["sDVE"]
        release_bank(k, [("sDVE", em.c["sDVE"])])
        yield

    def u_softplus(s):
        lg = T["LG"].ap()
        ac = T["ACTs"].ap()
        lgd = St[("lg", s, NST - 1)]
        em.on("act", lambda e: e.activation(ac, lg, AF.Abs),
              wait=[("sDVE", lgd), ("sDVE", St.get("acts_free", 0))], inc="sACT")
        abs_d = em.c["sACT"]
        em.on("act", lambda e: e.activation(ac, ac, AF.Exp, scale=-1.0),
              wait=[("sACT", em.c["sACT"])], inc="sACT")
        em.on("act", lambda e: e.activation(ac, ac, AF.Ln, bias=T["oneT"].ap()),
              wait=[("sACT", em.c["sACT"]), ("sDVE", St["eps_rdy"])], inc="sACT")
        ln_d = em.c["sACT"]
        em.on("dve", lambda e: e.tensor_scalar_max(lg, lg, 0.0),
              wait=[("sACT", abs_d)], inc="sDVE")
        relu_d = em.c["sDVE"]
        em.on("dve", lambda e: e.tensor_tensor(ac, ac, lg, ALU.add),
              wait=[("sACT", ln_d), ("sDVE", relu_d)], inc="sDVE")
        sp_d = em.c["sDVE"]
        St["lg_free"] = sp_d
        em.on("act", lambda e: e.activation(ac, ac, AF.Ln),
              wait=[("sDVE", sp_d)], inc="sACT")
        em.on("act", lambda e: e.activation(ac, ac, AF.Exp, scale=0.5),
              wait=[("sACT", em.c["sACT"])], inc="sACT")
        St["acts_rdy"] = em.c["sACT"]

    def u_top2(s, st4):
        a = T["ACTs"].ap()[:, st4 * 8:(st4 + 1) * 8]
        em.on("dve", lambda e, a=a: e.max(T["mx8"].ap(), a),
              wait=[("sACT", St["acts_rdy"]), ("sDVE", St.get("top2_free", 0))],
              inc="sDVE")
        em.on("dve", lambda e, a=a: e.max_index(T["idx8"].ap(), T["mx8"].ap(), a),
              wait=[("sDVE", em.c["sDVE"])], inc="sDVE")
        em.on("dve", lambda e: e.tensor_copy(T["idxF"].ap(), T["idx8"].ap()[:, 0:2]),
              wait=[("sDVE", em.c["sDVE"])], inc="sDVE")
        em.on("dve", lambda e: e.tensor_scalar(
            T["f1"].ap(), iota8, T["idxF"].ap()[:, 0:1], None, ALU.is_equal),
            wait=[("sDVE", em.c["sDVE"])], inc="sDVE")
        em.on("dve", lambda e: e.tensor_scalar(
            T["f2"].ap(), iota8, T["idxF"].ap()[:, 1:2], None, ALU.is_equal),
            wait=[("sDVE", em.c["sDVE"])], inc="sDVE")
        em.on("dve", lambda e: e.tensor_tensor(
            T["f1"].ap(), T["f1"].ap(), T["f2"].ap(), ALU.add),
            wait=[("sDVE", em.c["sDVE"])], inc="sDVE")
        em.on("dve", lambda e, a=a: e.tensor_tensor(
            T["CTpad"].ap()[:, 0:8], a, T["f1"].ap(), ALU.mult),
            wait=[("sDVE", em.c["sDVE"]), ("sDVE", St["ctpad0"]),
                  ("sPE", St.get("ctp_prev", 0))], inc="sDVE")
        cm_d = em.c["sDVE"]
        St["top2_free"] = cm_d
        if st4 == NST - 1:
            St["acts_free"] = cm_d
        em.on("pe", lambda e: e.transpose(T["pKt"].ap(), T["CTpad"].ap(), identB),
              wait=[("sDVE", cm_d), ("sACT", St.get("ktev_last", 0)),
                    ("qTh", St["qp_theta"])], inc="sPE")
        St["ctp_prev"] = em.c["sPE"]
        em.on("act", lambda e, st4=st4: e.copy(
            T["CT"].ap()[0:8, st4 * 128:(st4 + 1) * 128], T["pKt"].ap()[0:8, :]),
            wait=[("sPE", St["ctp_prev"]), ("sPE", St.get("ct_read_pe", 0))],
            inc="sACT")
        St["ktev_last"] = em.c["sACT"]
        if st4 == NST - 1:
            St["ct_rdy"] = em.c["sACT"]

    def u_wload(s, e8):
        s1 = St.get("w1_slot", 0)
        St["w1_slot"] = 1 - s1
        s2 = St.get("w2_slot", 0)
        St["w2_slot"] = (s2 + 1) % 3
        q1 = "qW1a" if s1 == 0 else "qW1b"
        q2 = ("qW2a", "qW2b", "qW2c")[s2]
        if e8 == 8:
            src1 = dram["sW1"].ap().rearrange("(c p) h -> p c h", p=128)
            src2 = dram["sW2"].ap().rearrange("(c p) d -> p c d", p=128)
        else:
            src1 = dram["rW1"].ap()[e8].rearrange("(c p) h -> p c h", p=128)
            src2 = dram["rW2"].ap()[e8].rearrange("(c p) d -> p c d", p=128)
        em.on("pool", lambda e, src1=src1, s1=s1: e.dma_start(
            out=T["rw1b"][s1].ap().rearrange("p (c h) -> p c h", c=FC),
            in_=src1), wait=[("sPE", St.get(("w1_war", s1), 0))], inc=q1, n=16)
        em.on("pool", lambda e, src2=src2, s2=s2: e.dma_start(
            out=T["rw2b"][s2].ap().rearrange("p (c d) -> p c d", c=4),
            in_=src2), wait=[("sPE", St.get(("w2_war", s2), 0))], inc=q2, n=16)
        St[("w_rdy", s, e8)] = [(q1, em.c[q1]), (q2, em.c[q2])]
        St[("w1_at", s, e8)] = s1
        St[("w2_at", s, e8)] = s2

    def u_layer1(s, e8, gi):
        return _g_layer1(s, e8, gi)

    def _g_layer1(s, e8, gi):
        s1 = St[("w1_at", s, e8)]
        wrdy = St[("w_rdy", s, e8)]
        eslot = EXPQ[gi].index(e8)
        w1 = T["rw1b"][s1].ap()
        shared = (e8 == 8)
        if not shared:
            k, rel = claim_bank()
            pb = T["pPj"][k].ap()
            em.on("pe", lambda e, e8=e8, pb=pb: e.matmul(
                pb, T["selB"].ap()[:, e8 * 128:(e8 + 1) * 128],
                T["CT"].ap()[0:8, :], start=True, stop=True),
                wait=[("sACT", St["ct_rdy"])] + rel, inc="sPE")
            bc_d = em.c["sPE"]
            em.on("act", lambda e, pb=pb: e.copy(T["bcT"].ap(), pb),
                  wait=[("sPE", bc_d), ("sDVE", St.get("bct_free", 0))], inc="sACT")
            bct_rdy = em.c["sACT"]
            release_bank(k, [("sACT", bct_rdy)])
            yield
        else:
            bct_rdy = 0
        hread = St.get("hfA_free", 0)
        scl_d = 0
        for hc in range(4):
            k, rel = claim_bank()
            pb = T["pPj"][k].ap()
            for dc in range(FC):
                w = (list(wrdy) + [("qXf", St["x1f_rdy"])] + rel) if dc == 0 else []
                em.on("pe", lambda e, dc=dc, hc=hc, w1=w1, pb=pb: e.matmul(
                    pb, w1[:, dc * 512 + hc * 128: dc * 512 + hc * 128 + 128],
                    T["x1f"].ap()[:, dc * GT:(dc + 1) * GT],
                    start=(dc == 0), stop=(dc == FC - 1)),
                    wait=w, inc="sPE" if dc == FC - 1 else None)
            hf_d = em.c["sPE"]
            bias = (T["sb1c"].ap()[:, hc:hc + 1] if shared
                    else T["rb1c"].ap()[:, e8 * 4 + hc: e8 * 4 + hc + 1])
            hsl = T["hfA"].ap()[:, (eslot * 4 + hc) * GT:(eslot * 4 + hc + 1) * GT]
            em.on("act", lambda e, hsl=hsl, bias=bias, pb=pb: e.activation(
                hsl, pb, AF.Silu, bias=bias),
                wait=[("sPE", hf_d), ("sPE", hread), ("qC", qs_consts),
                      ("sDVE", St.get(("hf_waw", eslot, hc), 0))], inc="sACT")
            silu_d = em.c["sACT"]
            if not shared:
                em.on("dve", lambda e, hsl=hsl: e.tensor_tensor(
                    hsl, hsl, T["bcT"].ap(), ALU.mult),
                    wait=[("sACT", silu_d), ("sACT", bct_rdy)], inc="sDVE")
                scl_d = em.c["sDVE"]
                St[("hf_waw", eslot, hc)] = scl_d
            else:
                St[("hf_waw", eslot, hc)] = 0
            release_bank(k, [("sACT", silu_d)])
            if hc < 3:
                yield
        St[("hf_rdy", gi, e8)] = (em.c["sACT"], scl_d)
        St[("w1_war", s1)] = em.c["sPE"]
        if not shared:
            St["bct_free"] = scl_d
        if e8 == EXPQ[-1][-1]:
            St["x1f_war"] = em.c["sPE"]

    def u_yacc(s, gi, st4, half):
        return _g_yacc(s, gi, st4, half)

    def _g_yacc(s, gi, st4, half):
        grp = EXPQ[gi]
        k, rel = claim_bank()
        pb = T["pPj"][k].ap()
        first = True
        if gi == 0:
            em.on("pe", lambda e, half=half, pb=pb: e.matmul(
                pb, onesR, T["sb2b"].ap()[0:1, half * 512:(half + 1) * 512],
                start=True, stop=False),
                wait=rel + [("sACT", St["ct_rdy"]), ("qTh", St["qp_theta"]),
                            ("sDVE", St["ones_rdy"])])
            em.on("pe", lambda e, st4=st4, half=half, pb=pb: e.matmul(
                pb, T["CT"].ap()[0:8, st4 * 128:(st4 + 1) * 128],
                T["rb2b"].ap()[0:8, half * 512:(half + 1) * 512],
                start=False, stop=False))
            first = False
        for e8 in grp:
            eslot = grp.index(e8)
            s2 = St[("w2_at", s, e8)]
            w2 = T["rw2b"][s2].ap()
            wrdy = St[("hf_rdy", gi, e8)]
            wq = St[("w_rdy", s, e8)][1]
            for hc in range(4):
                lastf = (e8 == grp[-1] and hc == 3)
                hsl = T["hfA"].ap()[:, (eslot * 4 + hc) * GT + st4 * 128:
                                    (eslot * 4 + hc) * GT + st4 * 128 + 128]
                w = []
                if first:
                    w = rel + [("sACT", wrdy[0]), ("sDVE", wrdy[1]), wq]
                elif hc == 0:
                    w = [("sACT", wrdy[0]), ("sDVE", wrdy[1]), wq]
                em.on("pe", lambda e, hsl=hsl, w2=w2, hc=hc, half=half,
                      lastf=lastf, first=first, pb=pb: e.matmul(
                          pb, hsl,
                          w2[:, hc * D + half * 512: hc * D + half * 512 + 512],
                          start=first, stop=lastf),
                      wait=w, inc="sPE" if lastf else None)
                first = False
            if e8 != grp[-1]:
                yield
        y_d = em.c["sPE"]
        ysl = T["yacc"].ap()[:, st4 * D + half * 512: st4 * D + half * 512 + 512]
        if gi == 0:
            em.on("dve", lambda e, ysl=ysl, pb=pb: e.tensor_copy(ysl, pb),
                  wait=[("sPE", y_d)] + St.get(("y_rel", st4, half), []),
                  inc="sDVE")
        else:
            em.on("dve", lambda e, ysl=ysl, pb=pb: e.tensor_tensor(
                ysl, ysl, pb, ALU.add),
                wait=[("sPE", y_d), ("sDVE", St.get(("yw", st4, half), 0))],
                inc="sDVE")
        St[("yw", st4, half)] = em.c["sDVE"]
        release_bank(k, [("sDVE", em.c["sDVE"])])
        if st4 == NST - 1 and half == 1:
            St["hfA_free"] = y_d
            for e8 in grp:
                St[("w2_war", St[("w2_at", s, e8)])] = y_d
            if gi == len(EXPQ) - 1:
                St["ct_read_pe"] = y_d

    def u_ln2(s, st4):
        slot = st4 % 2
        x1rel = St.get(("x1tok_rel", slot), [])
        em.on("sync", lambda e, st4=st4: e.dma_start(
            out=T["x1tok"][slot].ap(),
            in_=x1T[(s * NST + st4) * 128:(s * NST + st4) * 128 + 128, :]),
            wait=x1rel + St[("x1w_done", s, NST - 1)], inc=("qTkA" if slot == 0 else "qTkB"), n=16)
        xr = ("qTkA" if slot == 0 else "qTkB", em.c["qTkA" if slot == 0 else "qTkB"])
        ysl = T["yacc"].ap()[:, st4 * D:(st4 + 1) * D]
        em.on("dve", lambda e, ysl=ysl, slot=slot: e.tensor_tensor(
            ysl, ysl, T["x1tok"][slot].ap(), ALU.add),
            wait=[xr, ("sDVE", St[("yw", st4, 0)]),
                  ("sDVE", St[("yw", st4, 1)])], inc="sDVE")
        s2_d = em.c["sDVE"]
        strel = St.get("stats_rel", [])
        em.on("act", lambda e, ysl=ysl, slot=slot: e.activation(
            T["x1tok"][slot].ap(), ysl, AF.Square, accum_out=stats[:, 1:2]),
            wait=[("sDVE", s2_d)] + strel, inc="sACT")
        sq_d = em.c["sACT"]
        em.on("dve", lambda e, ysl=ysl: e.reduce_sum(stats[:, 0:1], ysl, AX.X),
              wait=[("sDVE", s2_d)] + strel, inc="sDVE")
        em.on("dve", lambda e: e.tensor_scalar_mul(stats[:, 2:3], stats[:, 0:1], 1.0 / D),
              wait=[("sDVE", em.c["sDVE"])], inc="sDVE")
        em.on("dve", lambda e: e.tensor_mul(stats[:, 3:4], stats[:, 2:3], stats[:, 2:3]),
              wait=[("sDVE", em.c["sDVE"])], inc="sDVE")
        em.on("dve", lambda e: e.scalar_tensor_tensor(
            stats[:, 4:5], stats[:, 1:2], 1.0 / D, stats[:, 3:4], ALU.mult, ALU.subtract),
            wait=[("sACT", sq_d), ("sDVE", em.c["sDVE"])], inc="sDVE")
        var_d = em.c["sDVE"]
        em.on("act", lambda e: e.activation(stats[:, 5:6], stats[:, 4:5], AF.Ln,
                                            bias=T["epsT"].ap()),
              wait=[("sDVE", var_d)], inc="sACT")
        em.on("act", lambda e: e.activation(stats[:, 5:6], stats[:, 5:6], AF.Exp,
                                            scale=-0.5),
              wait=[("sACT", em.c["sACT"])], inc="sACT")
        rstd_d = em.c["sACT"]
        oslot = 0
        qo = "qOutA" if st4 % 2 == 0 else "qOutB"
        em.on("dve", lambda e, ysl=ysl, oslot=oslot: e.tensor_scalar(
            T["outT"][oslot].ap(), ysl, stats[:, 2:3], stats[:, 5:6],
            ALU.subtract, ALU.mult),
            wait=[("sACT", rstd_d), ("qOutA", em.c["qOutA"]),
                  ("qOutB", em.c["qOutB"]), ("sDVE", var_d)], inc="sDVE")
        ow = em.c["sDVE"]
        St["stats_rel"] = [("sACT", rstd_d), ("sDVE", ow)]
        St[("y_rel", st4, 0)] = [("sDVE", ow)]
        St[("y_rel", st4, 1)] = [("sDVE", ow)]
        St[("x1tok_rel", slot)] = [("sACT", sq_d), ("sDVE", s2_d)]
        em.on("sync", lambda e, s=s, st4=st4, oslot=oslot: e.dma_start(
            out=dram["out"].ap()[(s * NST + st4) * 128:(s * NST + st4) * 128 + 128, :],
            in_=T["outT"][oslot].ap()),
            wait=[("sDVE", ow)], inc=qo, n=16)

    def pump(c, budget):
        q = St["ffnq"]
        k = 0
        while k < budget:
            gen = St.get("ffngen")
            if gen is None:
                if not (q and q[0][0] <= c):
                    return
                _, fn = q.pop(0)
                r = fn()
                if hasattr(r, "__next__"):
                    St["ffngen"] = r
                    gen = r
                else:
                    k += 1
                    continue
            try:
                next(gen)
                k += 1
            except StopIteration:
                St["ffngen"] = None

    St["x1f_war"] = 0

    for s in range(4):
        ffn_stage_units(s)

    pf_gram(0)
    for _ in pf_steps(0):
        pass
    for c in range(NCH):
        tile_load(c + 2)
        if c % 16 < 4:
            start_block(4 * (c // 16) + 4 + c % 16)
        X = chain_ctx(c)
        chain_r(c, X)
        pf_gram(c + 1)
        chain_outq(c, X)
        pump(c, 3)
        chain_solve(c, X)
        steps = pf_steps(c + 1)
        for fc4 in range(4):
            if c < NCH - 1:
                chain_dw(c, X, [2 * fc4, 2 * fc4 + 1])
            next(steps, None)
            next(steps, None)
            if fc4 % 2 == 1:
                pump(c, 2)
            drip(1)
        for si, _ in enumerate(steps):
            if si % 2 == 0:
                pump(c, 2)
            drip(1)
        if c < NCH - 1:
            chain_db(c, X)
        pump(c, 3)
        drip(1)
        if c % 16 == 5 and c // 16 < 3:
            stage_kq_dma(c // 16 + 1)
        if c % 16 == 6 and c // 16 < 3:
            stage_kq_ag(c // 16 + 3)
        if c % 4 == 3:
            stage_collective(c // 16, (c % 16) // 4)
    drip(10 ** 9)
    pump(NCH + 2, 10 ** 9)

    finals = [("qOutA", em.c["qOutA"]), ("qOutB", em.c["qOutB"])]
    em.on("sync", lambda e: e.nop(), wait=finals)
    em.on("pool", lambda e: e.nop(), wait=finals)


def kernel(L0_attn_in_seq, L0_attn_theta_K, L0_attn_theta_V, L0_attn_theta_Q,
           L0_attn_W0, L0_attn_b0, L0_ffn_routing_W, L0_ffn_expert_bias,
           L0_ffn_sW1, L0_ffn_sb1, L0_ffn_sW2, L0_ffn_sb2, L0_ffn_rW1,
           L0_ffn_rb1, L0_ffn_rW2, L0_ffn_rb2):
    f32 = np.float32
    x = np.asarray(L0_attn_in_seq, f32)
    Tk = np.asarray(L0_attn_theta_K, f32)
    Tv = np.asarray(L0_attn_theta_V, f32)
    Tq = np.asarray(L0_attn_theta_Q, f32)
    W0 = np.asarray(L0_attn_W0, f32)
    b0 = np.asarray(L0_attn_b0, f32)
    rW = np.asarray(L0_ffn_routing_W, f32)
    eb = np.asarray(L0_ffn_expert_bias, f32)
    sb1 = np.asarray(L0_ffn_sb1, f32)
    sb2 = np.asarray(L0_ffn_sb2, f32)
    rb1 = np.asarray(L0_ffn_rb1, f32)
    eta = np.float32(ETA)
    TkTf = np.ascontiguousarray(Tk.T)
    TqTf = np.ascontiguousarray(Tq.T)
    shared = dict(
        sW1=np.ascontiguousarray(np.asarray(L0_ffn_sW1, f32)),
        sb1T=np.ascontiguousarray(sb1.reshape(4, 128).T),
        sW2=np.ascontiguousarray(np.asarray(L0_ffn_sW2, f32)),
        sb2=np.ascontiguousarray(sb2.reshape(1, D)),
        rW1=np.ascontiguousarray(np.asarray(L0_ffn_rW1, f32)),
        rb1T=np.ascontiguousarray(rb1.reshape(8, 4, 128).transpose(2, 0, 1).reshape(128, 32)),
        rW2=np.ascontiguousarray(np.asarray(L0_ffn_rW2, f32)),
        rb2=np.ascontiguousarray(np.asarray(L0_ffn_rb2, f32)),
        rWT=np.ascontiguousarray(rW.T), ebr=np.ascontiguousarray(eb.reshape(1, 8)),
        cIdent=np.eye(128, dtype=f32),
        cMaskL=np.ascontiguousarray(eta * np.tril(np.ones((128, 128), f32), -1)),
        cMaskU=np.ascontiguousarray(eta * np.triu(np.ones((128, 128), f32), 1)),
        cMaskUI=np.ascontiguousarray(np.triu(np.ones((128, 128), f32), 0)),
        cIota=np.ascontiguousarray(np.tile(np.arange(8, dtype=f32), (128, 1))),
        cOnes=np.ones((128, 128), f32),
        cSel=np.ascontiguousarray(
            (np.arange(8)[:, None] == (np.arange(8 * 128) // 128)[None, :]).astype(f32)),
    )
    TvT = np.ascontiguousarray(Tv.T)
    W0T = np.ascontiguousarray(W0.T)
    xTb = [np.ascontiguousarray(x[b].T) for b in range(2)]

    in_maps = []
    for i in range(8):
        b, r = i // 4, i % 4
        sl = slice(r * ES, (r + 1) * ES)
        m = dict(shared)
        m["xT"] = xTb[b]
        m["TkT"] = np.ascontiguousarray(TkTf[:, sl])
        m["TqT"] = np.ascontiguousarray(TqTf[:, sl])
        m["xsl"] = np.ascontiguousarray(x[b][:, sl])
        m["TvTs"] = np.ascontiguousarray(TvT[:, sl])
        m["W0Ts"] = np.ascontiguousarray(W0T[:, sl])
        m["b0s"] = np.ascontiguousarray(b0[sl].reshape(1, ES))
        m["tokOff"] = np.array([[r * 4 * GT * ES]], dtype=np.int32)
        in_maps.append(m)

    nc = build_nc()
    res = run_bass_kernel_spmd(nc, in_maps, core_ids=list(range(8)))
    outp = np.empty((2, S, D), dtype=np.float32)
    for i in range(8):
        b, r = i // 4, i % 4
        o = res.results[i]["out"]
        for s in range(4):
            outp[b, 2048 * s + GT * r: 2048 * s + GT * (r + 1), :] = \
                o[s * GT:(s + 1) * GT, :]
    return outp


# version 26
# speedup vs baseline: 1.0942x; 1.0942x over previous
import contextlib
import numpy as np

import concourse.bass as bass
import concourse.mybir as mybir
from concourse.bass_utils import run_bass_kernel_spmd

F32 = mybir.dt.float32
BF = mybir.dt.bfloat16
I32 = mybir.dt.int32
AF = mybir.ActivationFunctionType
ALU = mybir.AluOpType
AX = mybir.AxisListType

S = 8192
D = 1024
C = 128
NCH = S // C
ES = 256
FC = 8
BLK = 512
NBLK = S // BLK
ETA = 0.01 * 2.0 / 1024.0
GT = 512
NST = 4
TOK = 2048

ENGS = ("pe", "dve", "act", "sync", "pool")
SEMS = ("sPE", "sDVE", "sACT", "sPL", "cc",
        "qC", "qTh", "qXA", "qXB", "qXTa", "qXTb",
        "qSO00", "qSO01", "qSO10", "qSO11", "qSO20", "qSO21", "qSO30", "qSO31",
        "qW1a", "qW1b", "qW2a", "qW2b", "qW2c", "qSg", "qXwA", "qXwB", "qXf", "qTkA", "qTkB",
        "qKI", "qKTa", "qKTb", "qKTc", "ck",
        "qOutA", "qOutB")


class Em:

    def __init__(self, sems, active, eng):
        self.sems = sems
        self.active = active
        self.eng = eng
        self.c = {k: 0 for k in SEMS}
        self.lw = {}

    def on(self, engine, fn, wait=(), inc=None, n=1):
        if engine == self.active:
            for s, v in wait:
                if v and v > self.lw.get(s, 0):
                    self.eng.wait_ge(self.sems[s], v)
                    self.lw[s] = v
            inst = fn(self.eng)
            if inst is not None and inc is not None:
                inst.then_inc(self.sems[inc], n)
        if inc is not None:
            self.c[inc] += n
            return self.c[inc]
        return None


def build_nc():
    nc = bass.Bass("TRN2", target_bir_lowering=False)

    def inp(name, shape, dt=F32):
        return nc.dram_tensor(name, list(shape), dt, kind="ExternalInput")

    dram = dict(
        xT=inp("xT", (D, S)),
        xsl=inp("xsl", (S, ES)),
        TkT=inp("TkT", (D, ES)),
        TqT=inp("TqT", (D, ES)),
        TvTs=inp("TvTs", (D, ES)),
        W0Ts=inp("W0Ts", (D, ES)),
        b0s=inp("b0s", (1, ES)),
        sW1=inp("sW1", (D, 512)),
        sb1T=inp("sb1T", (128, 4)),
        sW2=inp("sW2", (512, D)),
        sb2=inp("sb2", (1, D)),
        rW1=inp("rW1", (8, D, 512)),
        rb1T=inp("rb1T", (128, 32)),
        rW2=inp("rW2", (8, 512, D)),
        rb2=inp("rb2", (8, D)),
        rWT=inp("rWT", (D, 8)),
        ebr=inp("ebr", (1, 8)),
        cIdent=inp("cIdent", (128, 128)),
        cMaskL=inp("cMaskL", (128, 128)),
        cMaskU=inp("cMaskU", (128, 128)),
        cMaskUI=inp("cMaskUI", (128, 128)),
        cIota=inp("cIota", (128, 8)),
        cOnes=inp("cOnes", (128, 128)),
        cSel=inp("cSel", (8, 8 * 128)),
        tokOff=inp("tokOff", (1, 1), I32),
        out=nc.dram_tensor("out", [TOK, D], F32, kind="ExternalOutput"),
        sOut=nc.dram_tensor("sOut", [S, ES], BF, kind="Internal"),
        sGat=nc.dram_tensor("sGat", [4 * S, ES], BF, kind="Internal"),
        x1Tmp=nc.dram_tensor("x1Tmp", [TOK, D], BF, kind="Internal"),
        kqI00=nc.dram_tensor("kqI00", [512, 512], BF, kind="Internal"),
        kqG00=nc.dram_tensor("kqG00", [2048, 512], BF, kind="Internal"),
        kqI0a=nc.dram_tensor("kqI0a", [512, 512], BF, kind="Internal"),
        kqI0b=nc.dram_tensor("kqI0b", [512, 1024], BF, kind="Internal"),
        kqI1=nc.dram_tensor("kqI1", [512, 2048], BF, kind="Internal"),
        kqI2=nc.dram_tensor("kqI2", [512, 2048], BF, kind="Internal"),
        kqI3=nc.dram_tensor("kqI3", [512, 2048], BF, kind="Internal"),
        kqG0a=nc.dram_tensor("kqG0a", [2048, 512], BF, kind="Internal"),
        kqG0b=nc.dram_tensor("kqG0b", [2048, 1024], BF, kind="Internal"),
        kqG1=nc.dram_tensor("kqG1", [2048, 2048], BF, kind="Internal"),
        kqG2=nc.dram_tensor("kqG2", [2048, 2048], BF, kind="Internal"),
        kqG3=nc.dram_tensor("kqG3", [2048, 2048], BF, kind="Internal"),
    )

    with contextlib.ExitStack() as st:
        sems = {k: st.enter_context(nc.semaphore(k)) for k in SEMS}

        def sb(name, shape, dt):
            return st.enter_context(nc.sbuf_tensor(name, list(shape), dt))

        def psm(name, shape, dt=F32):
            return st.enter_context(nc.psum_tensor(name, list(shape), dt))

        rBase = st.enter_context(nc.gpsimd.register("rBase"))
        rCur = st.enter_context(nc.gpsimd.register("rCur"))

        T = dict(
            rBase=rBase, rCur=rCur,
            identB=sb("identB", (128, 128), BF),
            maskL=sb("maskL", (128, 128), F32),
            maskU=sb("maskU", (128, 128), F32),
            maskUI=sb("maskUI", (128, 128), F32),
            iota8=sb("iota8", (128, 8), F32),
            onesR=sb("onesR", (1, 128), BF),
            onesC=sb("onesC", (128, 1), BF),
            tokI=sb("tokI", (1, 1), I32),
            selB=sb("selB", (8, 8 * 128), BF),
            TkTb=sb("TkTb", (128, FC * ES), BF),
            TqTb=sb("TqTb", (128, FC * ES), BF),
            TvTb=sb("TvTb", (128, FC * ES), BF),
            Wt=sb("Wt", (128, FC * ES), F32),
            WtB=sb("WtB", (128, FC * ES), BF),
            bF=sb("bF", (1, ES), F32),
            bB=sb("bB", (1, ES), BF),
            xTb=[sb("xTb0", (128, FC * BLK), BF), sb("xTb1", (128, FC * BLK), BF)],
            KQt=[sb("KQt0", (128, 16 * 128), BF), sb("KQt1", (128, 16 * 128), BF),
                 sb("KQt2", (128, 16 * 128), BF)],
            kqOwn=sb("kqOwn", (128, 4 * 2048), BF),
            VhS=[sb("VhS0", (128, 16 * ES), BF), sb("VhS1", (128, 16 * ES), BF)],
            xslT=[sb("xslT0", (128, ES), F32), sb("xslT1", (128, ES), F32)],
            KtS=[sb("KtS0", (128, FC * 128), BF), sb("KtS1", (128, FC * 128), BF)],
            Ml=[sb("Ml0", (128, 128), BF), sb("Ml1", (128, 128), BF)],
            Mu=[sb("Mu0", (128, 128), BF), sb("Mu1", (128, 128), BF)],
            U=[sb("U0", (128, 128), BF), sb("U1", (128, 128), BF)],
            LT=[sb("LT0", (128, 128), BF), sb("LT1", (128, 128), BF)],
            Pl=[sb("Pla", (128, 128), BF), sb("Plb", (128, 128), BF)],
            Pu=[sb("Pua", (128, 128), BF), sb("Pub", (128, 128), BF)],
            Rb=[sb("Rb0", (128, ES), BF), sb("Rb1", (128, ES), BF)],
            WmB=[sb("WmB0", (128, ES), BF), sb("WmB1", (128, ES), BF)],
            sTile=[sb("sT0", (128, ES), BF), sb("sT1", (128, ES), BF)],
            rw1b=[sb("rw1b0", (128, FC * 512), BF), sb("rw1b1", (128, FC * 512), BF)],
            rw2b=[sb("rw2b0", (128, 4 * D), BF), sb("rw2b1", (128, 4 * D), BF),
                  sb("rw2b2", (128, 4 * D), BF)],
            rWTb=sb("rWTb", (128, FC * 8), BF),
            sb1c=sb("sb1c", (128, 4), F32),
            rb1c=sb("rb1c", (128, 32), F32),
            sb2b=sb("sb2b", (1, D), BF),
            rb2b=sb("rb2b", (8, D), BF),
            ebB=sb("ebB", (1, 8), BF),
            ebBc=sb("ebBc", (128, 8), F32),
            sg=[sb("sg0", (128, D), BF)],
            x1f=sb("x1f", (128, FC * GT), BF),
            x1tok=[sb("x1t0", (128, D), BF), sb("x1t1", (128, D), BF)],
            hfA=sb("hfA", (128, 2 * 4 * GT), BF),
            bcT=sb("bcT", (128, GT), BF),
            yacc=sb("yacc", (128, NST * D), F32),
            outT=[sb("outT0", (128, D), F32)],
            LG=sb("LG", (128, NST * 8), F32),
            ACTs=sb("ACTs", (128, NST * 8), F32),
            mx8=sb("mx8", (128, 8), F32),
            idx8=sb("idx8", (128, 8), mybir.dt.uint32),
            idxF=sb("idxF", (128, 2), F32),
            f1=sb("f1", (128, 8), F32),
            f2=sb("f2", (128, 8), F32),
            CTpad=sb("CTpad", (128, 128), BF),
            CT=sb("CT", (8, NST * 128), BF),
            stats=sb("stats", (128, 6), F32),
            epsT=sb("epsT", (128, 1), F32),
            oneT=sb("oneT", (128, 1), F32),
            pPj=[psm("pPj0", (128, 512)), psm("pPj1", (128, 512))],
            pAB=psm("pAB", (128, 256)),
            pTa=psm("pTa", (128, 256)),
            pTb=psm("pTb", (128, 256)),
            pR=psm("pR", (128, 256)),
            pOut=psm("pOut", (128, 256)),
            pKt=psm("pKt", (128, 128), BF),
        )

        with nc.Block("mk") as blk:
            def run(name):
                def body(eng):
                    em = Em(sems, name, eng)
                    emit_all(em, dram, T)
                return body
            blk.tensor(run("pe"))
            blk.vector(run("dve"))
            blk.scalar(run("act"))
            blk.sync(run("sync"))
            blk.gpsimd(run("pool"))

    return nc


def emit_all(em, dram, T):
    identB = T["identB"].ap()
    onesR = T["onesR"].ap()
    onesC = T["onesC"].ap()
    maskL = T["maskL"].ap()
    maskU = T["maskU"].ap()
    maskUI = T["maskUI"].ap()
    iota8 = T["iota8"].ap()
    stats = T["stats"].ap()

    xT3 = dram["xT"].ap().rearrange("(c p) t -> p c t", p=128)
    TkT3 = dram["TkT"].ap().rearrange("(c p) e -> p c e", p=128)
    TqT3 = dram["TqT"].ap().rearrange("(c p) e -> p c e", p=128)
    KQIN = [dram["kqI00"], dram["kqI0a"], dram["kqI0b"],
            dram["kqI1"], dram["kqI2"], dram["kqI3"]]
    KQGT = [dram["kqG00"], dram["kqG0a"], dram["kqG0b"],
            dram["kqG1"], dram["kqG2"], dram["kqG3"]]

    def kq_src(c):
        if c < 4:
            return 0, c * 128
        if c < 8:
            return 1, (c - 4) * 128
        if c < 16:
            return 2, (c - 8) * 128
        return 3 + (c // 16 - 1), (c % 16) * 128
    TvT3 = dram["TvTs"].ap().rearrange("(c p) e -> p c e", p=128)
    W0T3 = dram["W0Ts"].ap().rearrange("(c p) e -> p c e", p=128)
    x1T = dram["x1Tmp"].ap()
    sGatH = dram["sGat"]

    St = {
        "xsl_rdy": {}, "xtb_rdy": {},
        "wtb_rdy": 0, "bB_rdy": 0, "dbadd": 0,
        "lastchain_out": 0, "last_revac": 0, "tbuild_prev": 0,
        "drip": [],
        "bankrel": {0: [], 1: []}, "bankrr": 0,
        "ffnq": [],
    }

    em.on("sync", lambda e: e.dma_start(out=maskL, in_=dram["cMaskL"].ap()), inc="qC", n=16)
    em.on("sync", lambda e: e.dma_start(out=maskU, in_=dram["cMaskU"].ap()), inc="qC", n=16)
    em.on("sync", lambda e: e.dma_start(out=maskUI, in_=dram["cMaskUI"].ap()), inc="qC", n=16)
    em.on("sync", lambda e: e.dma_start(out=iota8, in_=dram["cIota"].ap()), inc="qC", n=16)
    em.on("sync", lambda e: e.dma_start(
        out=T["Wt"].ap().rearrange("p (c e) -> p c e", c=FC), in_=W0T3), inc="qC", n=16)
    em.on("sync", lambda e: e.dma_start(out=T["bF"].ap(), in_=dram["b0s"].ap()), inc="qC", n=16)
    em.on("sync", lambda e: e.dma_start(out=T["tokI"].ap(), in_=dram["tokOff"].ap()), inc="qC", n=16)
    em.on("sync", lambda e: e.dma_start(out=T["sb1c"].ap(), in_=dram["sb1T"].ap()), inc="qC", n=16)
    em.on("sync", lambda e: e.dma_start(out=T["rb1c"].ap(), in_=dram["rb1T"].ap()), inc="qC", n=16)
    qs_consts = em.c["qC"]
    em.on("sync", lambda e: e.dma_start(out=T["xslT"][0].ap(), in_=dram["xsl"].ap()[0:128, :]), inc="qXA", n=16)
    St["xsl_rdy"][0] = ("qXA", em.c["qXA"])

    em.on("pool", lambda e: e.dma_start(out=identB, in_=dram["cIdent"].ap()), inc="qTh", n=16)
    em.on("pool", lambda e: e.dma_start(
        out=T["TkTb"].ap().rearrange("p (c e) -> p c e", c=FC), in_=TkT3), inc="qTh", n=16)
    em.on("pool", lambda e: e.dma_start(
        out=T["TqTb"].ap().rearrange("p (c e) -> p c e", c=FC), in_=TqT3), inc="qTh", n=16)
    em.on("pool", lambda e: e.dma_start(
        out=T["TvTb"].ap().rearrange("p (c e) -> p c e", c=FC), in_=TvT3), inc="qTh", n=16)
    St["qp_kqv"] = em.c["qTh"]
    em.on("pool", lambda e: e.dma_start(
        out=T["rWTb"].ap().rearrange("p (c e) -> p c e", c=FC),
        in_=dram["rWT"].ap().rearrange("(c p) e -> p c e", p=128)), inc="qTh", n=16)
    em.on("pool", lambda e: e.dma_start(out=T["sb2b"].ap(), in_=dram["sb2"].ap()), inc="qTh", n=16)
    em.on("pool", lambda e: e.dma_start(out=T["rb2b"].ap(), in_=dram["rb2"].ap()), inc="qTh", n=16)
    em.on("pool", lambda e: e.dma_start(out=T["ebB"].ap(), in_=dram["ebr"].ap()), inc="qTh", n=16)
    em.on("pool", lambda e: e.dma_start(out=T["selB"].ap(), in_=dram["cSel"].ap()), inc="qTh", n=16)
    St["qp_theta"] = em.c["qTh"]
    em.on("pool", lambda e: e.reg_load(T["rBase"], T["tokI"].ap()[0:1, 0:1]),
          wait=[("qC", qs_consts)])
    em.on("pool", lambda e: e.dma_start(
        out=T["xTb"][0].ap().rearrange("p (c t) -> p c t", c=FC),
        in_=xT3[:, :, 0:BLK]), inc="qXTa", n=16)
    St["xtb_rdy"][0] = ("qXTa", em.c["qXTa"])

    em.on("dve", lambda e: e.memset(onesR, 1.0), inc="sDVE")
    em.on("dve", lambda e: e.memset(onesC, 1.0), inc="sDVE")
    St["ones_rdy"] = em.c["sDVE"]
    for fc in range(FC):
        em.on("act", lambda e, fc=fc: e.copy(
            T["WtB"].ap()[:, fc * ES:(fc + 1) * ES],
            T["Wt"].ap()[:, fc * ES:(fc + 1) * ES]),
            wait=[("qC", qs_consts)], inc="sACT")
        St[("wtbfc", fc)] = ("sACT", em.c["sACT"])
    em.on("act", lambda e: e.copy(T["bB"].ap(), T["bF"].ap()), inc="sACT")
    St["wtb_rdy"] = em.c["sACT"]
    St["bB_rdy"] = em.c["sACT"]
    em.on("dve", lambda e: e.memset(T["CTpad"].ap(), 0.0), inc="sDVE")
    St["ctpad0"] = em.c["sDVE"]
    em.on("dve", lambda e: e.memset(T["epsT"].ap(), 1e-5), inc="sDVE")
    em.on("dve", lambda e: e.memset(T["oneT"].ap(), 1.0), inc="sDVE")
    St["eps_rdy"] = em.c["sDVE"]
    em.on("pe", lambda e: e.matmul(T["pAB"].ap()[:, 0:8], onesR, T["ebB"].ap(),
                                   start=True, stop=True),
          wait=[("qTh", St["qp_theta"]), ("sDVE", St["ones_rdy"])], inc="sPE")
    em.on("dve", lambda e: e.tensor_copy(T["ebBc"].ap(), T["pAB"].ap()[:, 0:8]),
          wait=[("sPE", em.c["sPE"])], inc="sDVE")
    St["ebbc"] = em.c["sDVE"]
    St["dbadd"] = em.c["sDVE"]

    def claim_bank():
        k = St["bankrr"]
        St["bankrr"] = 1 - k
        return k, list(St["bankrel"][k])

    def release_bank(k, conds):
        St["bankrel"][k] = list(conds)

    RG = [[0, 1, 2, 3], [4, 5, 6, 7]]

    def m_k(fc):
        return (fc // 2) * 4 + (fc % 2)

    def m_q(fc):
        return m_k(fc) + 2

    def make_proj_units(b):
        p = b % 2
        xtb = T["xTb"][p].ap()
        units = []
        war_dve = St["last_revac"]
        kqi = St.get("kqi_cnt", 0)
        xw = [St["xtb_rdy"][b], ("qTh", St["qp_kqv"])]
        vs = (b // 4) % 2
        for half, thb in ((0, T["TkTb"]), (1, T["TqTb"])):
            for ec in range(2):
                mms = []
                for dc in range(FC):
                    def mm(e, dc=dc, ec=ec, thb=thb, bank=None):
                        return e.matmul(
                            T["pPj"][bank].ap(),
                            thb.ap()[:, dc * ES + ec * 128: dc * ES + ec * 128 + 128],
                            xtb[:, dc * BLK:(dc + 1) * BLK],
                            start=(dc == 0), stop=(dc == FC - 1))
                    mms.append(mm)
                co = (half * 2 + ec) * 2048 + (b % 4) * BLK

                def ev(e, co=co, bank=None):
                    return e.copy(T["kqOwn"].ap()[:, co:co + BLK],
                                  T["pPj"][bank].ap())
                units.append(dict(mms=mms, ev=ev, xw=xw, war_dve=0, kqi=kqi))
        for tq in range(4):
            mms = []
            for dc in range(FC):
                def mmv(e, dc=dc, tq=tq, bank=None):
                    return e.matmul(
                        T["pPj"][bank].ap()[:, 0:ES],
                        xtb[:, dc * BLK + tq * 128: dc * BLK + tq * 128 + 128],
                        T["TvTb"].ap()[:, dc * ES:(dc + 1) * ES],
                        start=(dc == 0), stop=(dc == FC - 1))
                mms.append(mmv)
            vo = ((b % 4) * 4 + tq) * ES

            def evv(e, vo=vo, vs=vs, bank=None):
                return e.activation(
                    T["VhS"][vs].ap()[:, vo:vo + ES],
                    T["pPj"][bank].ap()[:, 0:ES], AF.Identity, scale=ETA)
            units.append(dict(mms=mms, ev=evv, xw=xw, war_dve=war_dve, kqi=0))
        return units

    def run_proj_unit(u):
        k, rel = claim_bank()
        w0 = list(u["xw"]) + rel
        for i, mm in enumerate(u["mms"]):
            em.on("pe", lambda e, mm=mm, k=k: mm(e, bank=k),
                  wait=w0 if i == 0 else [],
                  inc="sPE" if i == len(u["mms"]) - 1 else None)
        mmd = em.c["sPE"]
        wv = [("sPE", mmd)]
        if u["war_dve"]:
            wv.append(("sDVE", u["war_dve"]))
        if u["kqi"]:
            wv.append(("qKI", u["kqi"]))
        em.on("act", lambda e, k=k: u["ev"](e, bank=k), wait=wv, inc="sACT")
        release_bank(k, [("sACT", em.c["sACT"])])

    def drip(n):
        lst = St["drip"]
        k = 0
        while lst and k < n:
            run_proj_unit(lst.pop(0))
            k += 1
        if not lst and St.get("drip_block") is not None:
            St[("projmm", St["drip_block"])] = em.c["sPE"]
            St["drip_block"] = None

    def start_block(b):
        if b >= NBLK:
            return
        drip(10 ** 9)
        war = St.get(("projmm", b - 2), 0)

        def ld(e, b=b):
            return e.dma_start(
                out=T["xTb"][b % 2].ap().rearrange("p (c t) -> p c t", c=FC),
                in_=xT3[:, :, b * BLK:(b + 1) * BLK])
        qn = "qXTa" if b % 2 == 0 else "qXTb"
        em.on("pool", ld, wait=[("sPE", war)], inc=qn, n=16)
        St["xtb_rdy"][b] = (qn, em.c[qn])
        St["drip"] = make_proj_units(b)
        St["drip_block"] = b

    def kqO3(c0, w):
        return T["kqOwn"].ap().rearrange("p (q t) -> p q t", q=4)[:, :, c0:c0 + w]

    def stage_kq_dma(s):
        drip(10 ** 9)
        kq_ev = em.c["sACT"]
        St[("vhs", s)] = kq_ev
        em.on("sync", lambda e, s=s: e.dma_start(
            out=KQIN[s + 2].ap().rearrange("(q p) t -> p q t", p=128),
            in_=kqO3(0, 2048)),
            wait=[("sACT", kq_ev)], inc="qKI", n=16)
        St["kqi_cnt"] = em.c["qKI"]

    def stage_kq_ag(buf):
        em.on("pool", lambda e, buf=buf: e.collective_compute(
            "AllGather", ALU.bypass, replica_groups=RG,
            ins=[KQIN[buf].ap()], outs=[KQGT[buf].ap()]),
            wait=[("qKI", em.c["qKI"])], inc="ck", n=1)

    def tile_load(c):
        if c >= NCH:
            return
        buf, off = kq_src(c)
        pt = c % 3
        qn = ("qKTa", "qKTb", "qKTc")[pt]
        src = KQGT[buf].ap().rearrange("(m p) t -> p m t", p=128)
        em.on("sync", lambda e, src=src, off=off, pt=pt: e.dma_start(
            out=T["KQt"][pt].ap().rearrange("p (m t) -> p m t", m=16),
            in_=src[:, :, off:off + 128]),
            wait=[("ck", buf + 1), ("sPE", St.get(("kqt_war", pt), 0))],
            inc=qn, n=16)
        St[("kqt_rdy", c)] = (qn, em.c[qn])

    St["drip"] = make_proj_units(0)
    St["drip_block"] = 0
    drip(10 ** 9)
    em.on("sync", lambda e: e.dma_start(
        out=KQIN[0].ap().rearrange("(q p) t -> p q t", p=128),
        in_=kqO3(0, 512)),
        wait=[("sACT", em.c["sACT"])], inc="qKI", n=16)
    em.on("pool", lambda e: e.collective_compute(
        "AllGather", ALU.bypass, replica_groups=RG,
        ins=[KQIN[0].ap()], outs=[KQGT[0].ap()]),
        wait=[("qKI", em.c["qKI"])], inc="ck", n=1)
    start_block(1)
    drip(10 ** 9)
    kq_ev0 = em.c["sACT"]
    em.on("sync", lambda e: e.dma_start(
        out=KQIN[1].ap().rearrange("(q p) t -> p q t", p=128),
        in_=kqO3(512, 512)),
        wait=[("sACT", kq_ev0)], inc="qKI", n=16)
    em.on("pool", lambda e: e.collective_compute(
        "AllGather", ALU.bypass, replica_groups=RG,
        ins=[KQIN[1].ap()], outs=[KQGT[1].ap()]),
        wait=[("qKI", em.c["qKI"])], inc="ck", n=1)
    start_block(2)
    drip(10 ** 9)
    start_block(3)
    drip(10 ** 9)
    St[("vhs", 0)] = em.c["sACT"]
    em.on("sync", lambda e: e.dma_start(
        out=KQIN[2].ap().rearrange("(q p) t -> p q t", p=128),
        in_=kqO3(1024, 1024)),
        wait=[("sACT", St[("vhs", 0)])], inc="qKI", n=16)
    St["kqi_cnt"] = em.c["qKI"]
    em.on("pool", lambda e: e.collective_compute(
        "AllGather", ALU.bypass, replica_groups=RG,
        ins=[KQIN[2].ap()], outs=[KQGT[2].ap()]),
        wait=[("qKI", em.c["qKI"])], inc="ck", n=1)
    tile_load(0)
    tile_load(1)

    def pf_gram(c):
        if c >= NCH:
            return
        pi = c % 2
        kq = T["KQt"][c % 3]
        pAB = T["pAB"].ap()
        for fc in range(FC):
            w = [St[("kqt_rdy", c)], ("sDVE", St["dbadd"])] if fc == 0 else []
            off = m_k(fc) * 128
            em.on("pe", lambda e, fc=fc, off=off: e.matmul(
                pAB[:, 0:256],
                kq.ap()[:, off:off + 128],
                bass.AP(kq, off, [[16 * 128, 128], [256, 2], [1, 128]]),
                start=(fc == 0), stop=(fc == FC - 1)),
                wait=w, inc="sPE" if fc == FC - 1 else None)
        ab_done = em.c["sPE"]
        em.on("dve", lambda e: e.scalar_tensor_tensor(
            T["Ml"][pi].ap(), pAB[:, 0:128], 1.0, maskL, ALU.add, ALU.mult),
            wait=[("sPE", ab_done), ("sPE", St.get(("uread", pi), 0)),
                  ("sDVE", St.get(("maskwaw", pi), 0))], inc="sDVE")
        em.on("dve", lambda e: e.scalar_tensor_tensor(
            T["Mu"][pi].ap(), pAB[:, 0:128], 1.0, maskU, ALU.add, ALU.mult), inc="sDVE")
        masks_done = em.c["sDVE"]
        St[("maskwaw", pi)] = masks_done
        St[("masks", c)] = masks_done
        em.on("dve", lambda e: e.tensor_tensor(
            T["U"][pi].ap(), identB, T["Mu"][pi].ap(), ALU.subtract),
            wait=[("sDVE", masks_done), ("qTh", St["qp_theta"])], inc="sDVE")
        St[("uadd", c, 0)] = em.c["sDVE"]
        em.on("dve", lambda e: e.scalar_tensor_tensor(
            T["LT"][pi].ap(), pAB[:, 128:256], 1.0, maskUI, ALU.add, ALU.mult),
            wait=[("sPE", ab_done), ("sPE", St.get(("ltread", pi), 0)),
                  ("sDVE", St.get(("ltwaw", pi), 0))], inc="sDVE")
        St[("lt", c)] = em.c["sDVE"]
        St[("ltwaw", pi)] = em.c["sDVE"]

    def pf_steps(c):
        if c >= NCH:
            return
        pi = c % 2
        kq = T["KQt"][c % 3].ap()
        pTa, pTb, pKt = T["pTa"].ap(), T["pTb"].ap(), T["pKt"].ap()
        tb_war = St["tbuild_prev"]
        state = dict(cur_l=T["Ml"][pi].ap(), cur_u=T["Mu"][pi].ap(),
                     nxt=0, puev=0, plev=0)
        war_kt = St.get(("ktread", pi), 0)

        def t_round(j):
            cur_l, cur_u = state["cur_l"], state["cur_u"]
            w = [("sDVE", state["plev"])]
            if j == 1:
                w.append(("sDVE", St[("masks", c)]))
            em.on("pe", lambda e: e.matmul(
                pTb[:, 0:128], cur_u, cur_l, start=True, stop=True),
                wait=w, inc="sPE")
            sql_d = em.c["sPE"]
            npl, npu = T["Pl"][state["nxt"]].ap(), T["Pu"][state["nxt"]].ap()
            em.on("dve", lambda e: e.tensor_copy(npl, pTb[:, 0:128]),
                  wait=[("sPE", sql_d), ("sPE", tb_war),
                        ("sACT", St.get("puevwaw", 0)),
                        ("sDVE", St.get("pevwaw", 0))], inc="sDVE")
            plev = em.c["sDVE"]
            state["plev"] = plev
            St["pevwaw"] = plev
            if j < 6:
                em.on("pe", lambda e: e.matmul(
                    pTa[:, 128:256], cur_l, cur_u, start=True, stop=True),
                    wait=[("sDVE", St[("uadd", c, j - 1)]),
                          ("sACT", state["puev"])], inc="sPE")
                squ_d = em.c["sPE"]
                em.on("act", lambda e: e.copy(npu, pTa[:, 128:256]),
                      wait=[("sPE", squ_d), ("sACT", St.get("puevwaw", 0))],
                      inc="sACT")
                state["puev"] = em.c["sACT"]
                St["puevwaw"] = em.c["sACT"]
            state["cur_l"], state["cur_u"] = npl, npu
            state["nxt"] = 1 - state["nxt"]
            yield

            em.on("pe", lambda e, npl=npl: e.matmul(
                pTa[:, 0:128], npl, T["U"][pi].ap(), start=True, stop=True),
                wait=[("sDVE", plev), ("sDVE", St[("uadd", c, j - 1)]),
                      ("sACT", state["puev"])], inc="sPE")
            umm_d = em.c["sPE"]
            em.on("dve", lambda e: e.tensor_tensor(
                T["U"][pi].ap(), T["U"][pi].ap(), pTa[:, 0:128], ALU.add),
                wait=[("sPE", umm_d), ("sDVE", St[("uadd", c, j - 1)])], inc="sDVE")
            St[("uadd", c, j)] = em.c["sDVE"]
            if j == 6:
                St[("u_rdy", c)] = em.c["sDVE"]
                St["tbuild_prev"] = em.c["sPE"]
            yield

        def kt_step(fc):
            em.on("pe", lambda e, fc=fc: e.transpose(
                pKt, kq[:, m_k(fc) * 128:m_k(fc) * 128 + 128], identB),
                wait=[St[("kqt_rdy", c)], ("sACT", St.get("ktev_last", 0)),
                      ("qTh", St["qp_theta"])], inc="sPE")
            tr_d = em.c["sPE"]
            em.on("act", lambda e, fc=fc: e.copy(
                T["KtS"][pi].ap()[:, fc * 128:(fc + 1) * 128], pKt),
                wait=[("sPE", tr_d), ("sPE", war_kt),
                      ("sACT", St.get(("ktwaw", pi), 0))], inc="sACT")
            St["ktev_last"] = em.c["sACT"]
            if fc == FC - 1:
                St[("kt_rdy", c)] = ("sACT", em.c["sACT"])
                St[("ktwaw", pi)] = em.c["sACT"]
            yield

        for j in range(1, 7):
            yield from t_round(j)
            if j <= 4:
                yield from kt_step(j - 1)
                yield from kt_step(j + 3)

    def chain_ctx(c):
        return dict(
            pi=c % 2, b=c // 4, sub=c % 4,
            kq=T["KQt"][c % 3].ap(),
            pR=T["pR"].ap(), pOut=T["pOut"].ap(), pAB=T["pAB"].ap(),
            wtb=T["WtB"].ap(), wt=T["Wt"].ap())

    def chain_r(c, X):
        pi, b, pR, wtb = X["pi"], X["b"], X["pR"], X["wtb"]
        w0 = [("sACT", St["bB_rdy"]), St[("kqt_rdy", c)],
              ("sDVE", St.get(("sadd", c - 1, 6), 0)),
              ("sDVE", St["dbadd"]),
              ("sACT", St["wtb_rdy"]), ("sDVE", St.get("wtb_rdy_d", 0))]
        for fc in range(FC):
            w = w0 if fc == 0 else []
            em.on("pe", lambda e, fc=fc: e.matmul(
                pR, X["kq"][:, m_k(fc) * 128:m_k(fc) * 128 + 128],
                wtb[:, fc * ES:(fc + 1) * ES],
                start=(fc == 0), stop=False), wait=w)
        em.on("pe", lambda e: e.matmul(pR, onesR, T["bB"].ap(),
                                       start=False, stop=True),
              wait=[("sDVE", St["ones_rdy"])], inc="sPE")
        r_done = em.c["sPE"]
        em.on("dve", lambda e: e.scalar_tensor_tensor(
            T["Rb"][pi].ap(), pR, -ETA,
            T["VhS"][(c // 16) % 2].ap()[:, (c % 16) * ES:(c % 16 + 1) * ES],
            ALU.mult, ALU.add),
            wait=[("sPE", r_done), ("sDVE", St.get(("rbwaw", pi), 0)),
                  ("sACT", St[("vhs", c // 16)])], inc="sDVE")
        X["r_evac"] = em.c["sDVE"]
        St[("rbwaw", pi)] = X["r_evac"]
        St["last_revac"] = X["r_evac"]

    def chain_outq(c, X):
        pOut, wtb = X["pOut"], X["wtb"]
        wo = [("sDVE", St.get(("sadd", c - 1, 7), 0))]
        for fc in range(FC):
            w = wo if fc == 0 else []
            em.on("pe", lambda e, fc=fc: e.matmul(
                pOut, X["kq"][:, m_q(fc) * 128:m_q(fc) * 128 + 128],
                wtb[:, fc * ES:(fc + 1) * ES],
                start=(fc == 0), stop=False), wait=w)
        em.on("pe", lambda e: e.matmul(pOut, onesR, T["bB"].ap(),
                                       start=False, stop=False),
              wait=[("sDVE", St["ones_rdy"])])

    def chain_solve(c, X):
        pi, pR, pOut = X["pi"], X["pR"], X["pOut"]
        em.on("pe", lambda e: e.matmul(
            pR, T["U"][pi].ap(), T["Rb"][pi].ap(), start=True, stop=True),
            wait=[("sDVE", X["r_evac"]), ("sDVE", St[("u_rdy", c)])], inc="sPE")
        wm_done = em.c["sPE"]
        St[("uread", pi)] = wm_done
        em.on("dve", lambda e: e.tensor_copy(T["WmB"][pi].ap(), pR),
              wait=[("sPE", wm_done), ("sPE", St.get(("wmbread", pi), 0))],
              inc="sDVE")
        X["wmb_d"] = em.c["sDVE"]
        em.on("pe", lambda e: e.matmul(
            pOut, T["LT"][pi].ap(), T["WmB"][pi].ap(), start=False, stop=True),
            wait=[("sDVE", X["wmb_d"]), ("sDVE", St[("lt", c)])], inc="sPE")
        out_done = em.c["sPE"]
        X["out_done"] = out_done
        St["lastchain_out"] = out_done
        St[("ltread", pi)] = out_done
        St[("kqt_war", c % 3)] = out_done
        em.on("dve", lambda e: e.tensor_tensor(
            T["sTile"][pi].ap(), pOut, T["xslT"][pi].ap(), ALU.add),
            wait=[("sPE", out_done), St["xsl_rdy"][c],
                  ("sDVE", St.get(("stwaw", pi), 0)),
                  (f"qSO{(c - 2) // 16}{c % 2}",
                   em.c[f"qSO{(c - 2) // 16}{c % 2}"]) if c >= 2 else ("qC", 0)],
            inc="sDVE")
        s_evac = em.c["sDVE"]
        X["s_evac"] = s_evac
        St[("stwaw", pi)] = s_evac
        St[("sevac", c)] = s_evac
        qso = f"qSO{c // 16}{c % 2}"
        em.on("sync", lambda e, c=c: e.dma_start(
            out=dram["sOut"].ap()[c * 128:(c + 1) * 128, :], in_=T["sTile"][pi].ap()),
            wait=[("sDVE", s_evac)], inc=qso, n=16)
        if c + 1 < NCH:
            qx = "qXA" if (c + 1) % 2 == 0 else "qXB"
            em.on("sync", lambda e, c=c: e.dma_start(
                out=T["xslT"][1 - pi].ap(),
                in_=dram["xsl"].ap()[(c + 1) * 128:(c + 2) * 128, :]),
                wait=[("sDVE", St.get(("sevac", c - 1), 0))], inc=qx, n=16)
            St["xsl_rdy"][c + 1] = (qx, em.c[qx])

    def chain_dw(c, X, fcs):
        pi, pR, pOut, wtb, wt = X["pi"], X["pR"], X["pOut"], X["wtb"], X["wt"]
        kt = T["KtS"][pi].ap()
        for fc in fcs:
            bank = pR if fc % 2 == 0 else pOut
            w = [St[("kt_rdy", c)], ("sDVE", X["wmb_d"])]
            if fc == 1:
                w.append(("sDVE", X["s_evac"]))
            elif fc >= 2:
                w.append(("sDVE", St[("sadd", c, fc - 2)]))
            em.on("pe", lambda e, fc=fc, bank=bank: e.matmul(
                bank, kt[:, fc * 128:(fc + 1) * 128], T["WmB"][pi].ap(),
                start=True, stop=True), wait=w, inc="sPE")
            dw_d = em.c["sPE"]
            em.on("dve", lambda e, fc=fc, bank=bank: e.tensor_tensor(
                wt[:, fc * ES:(fc + 1) * ES], wt[:, fc * ES:(fc + 1) * ES],
                bank, ALU.add),
                wait=[("sPE", dw_d), ("sDVE", St.get(("sadd", c - 1, fc), 0))],
                inc="sDVE")
            St[("sadd", c, fc)] = em.c["sDVE"]
            if fc % 2 == 0:
                em.on("act", lambda e, fc=fc: e.copy(
                    wtb[:, fc * ES:(fc + 1) * ES], wt[:, fc * ES:(fc + 1) * ES]),
                    wait=[("sDVE", St[("sadd", c, fc)]), ("sPE", X["out_done"]),
                          ("sACT", St.get("wtbwaw_a", 0))], inc="sACT")
                St["wtbwaw_a"] = em.c["sACT"]
                St[("wtbfc", fc)] = ("sACT", em.c["sACT"])
            else:
                em.on("dve", lambda e, fc=fc: e.tensor_copy(
                    wtb[:, fc * ES:(fc + 1) * ES], wt[:, fc * ES:(fc + 1) * ES]),
                    wait=[("sDVE", St[("sadd", c, fc)]), ("sPE", X["out_done"])],
                    inc="sDVE")
                St[("wtbfc", fc)] = ("sDVE", em.c["sDVE"])
        if fcs[-1] == FC - 1:
            St["wtb_rdy"] = em.c["sACT"]
            St["wtb_rdy_d"] = em.c["sDVE"]
            St[("ktread", pi)] = em.c["sPE"]

    def chain_db(c, X):
        pi, pAB = X["pi"], X["pAB"]
        em.on("pe", lambda e: e.matmul(
            pAB[0:1, 0:ES], onesC, T["WmB"][pi].ap(), start=True, stop=True),
            wait=[("sDVE", St.get(("lt", c + 1), St[("lt", c)])),
                  ("sDVE", St["ones_rdy"])], inc="sPE")
        db_d = em.c["sPE"]
        St[("wmbread", pi)] = db_d
        em.on("dve", lambda e: e.tensor_tensor(
            T["bF"].ap(), T["bF"].ap(), pAB[0:1, 0:ES], ALU.add),
            wait=[("sPE", db_d), ("sDVE", St["dbadd"])], inc="sDVE")
        St["dbadd"] = em.c["sDVE"]
        em.on("act", lambda e: e.copy(T["bB"].ap(), T["bF"].ap()),
              wait=[("sDVE", St["dbadd"]), ("sPE", X["out_done"]),
                    ("sACT", St["bB_rdy"])], inc="sACT")
        St["bB_rdy"] = em.c["sACT"]

    def stage_collective(sg, q):
        em.on("pool", lambda e, sg=sg, q=q: e.collective_compute(
            "AllGather", ALU.bypass,
            replica_groups=[[0, 1, 2, 3], [4, 5, 6, 7]],
            ins=[dram["sOut"].ap()[sg * 2048 + q * 512:sg * 2048 + (q + 1) * 512, :]],
            outs=[dram["sGat"].ap()[sg * 8192 + q * 2048:sg * 8192 + (q + 1) * 2048, :]]),
            wait=[(f"qSO{sg}0", 32 * (q + 1)), (f"qSO{sg}1", 32 * (q + 1))],
            inc="cc", n=1)

    EXPQ = [[0, 1], [2, 3], [4, 5], [6, 7], [8]]

    def ffn_stage_units(s):
        q = St["ffnq"]
        base = 16 * s + 17 if s < 3 else 60

        def tag(off, fn):
            cap = NCH + 1 if s < 3 else 63
            q.append((min(base + off, cap) if s < 3 or off <= 5 else NCH + 1, fn))

        for st4 in range(NST):
            tag(st4 // 2, lambda e=None, st4=st4: u_sgln(s, st4))
        tag(2, lambda: u_x1f(s))
        for st4 in range(NST):
            tag(3, lambda st4=st4: u_router(s, st4))
        tag(3, lambda: u_softplus(s))
        for st4 in range(NST):
            tag(4, lambda st4=st4: u_top2(s, st4))
        off = 5
        for gi, grp in enumerate(EXPQ):
            for e8 in grp:
                tag(max(off - 2, 0), lambda s=s, e8=e8: u_wload(s, e8))
                tag(off, lambda s=s, e8=e8, gi=gi: u_layer1(s, e8, gi))
                off += 1
            for st4 in range(NST):
                for half in range(2):
                    tag(off + (st4 // 2), lambda s=s, gi=gi, st4=st4, half=half:
                        u_yacc(s, gi, st4, half))
            off += 2
        for st4 in range(NST):
            tag(off + st4 // 2, lambda s=s, st4=st4: u_ln2(s, st4))

    def u_sgln(s, st4):
        slot = st4 % 2
        sgs = 0
        em.on("pool", lambda e: e.reg_add(T["rCur"], T["rBase"],
                                          s * 4 * 2048 * ES + st4 * 128 * ES))
        em.on("pool", lambda e: e.dma_start(
            out=T["sg"][sgs].ap().rearrange("p (r x) -> p r x", r=4),
            in_=bass.AP(sGatH, T["rCur"], [[ES, 128], [512 * ES, 4], [1, ES]])),
            wait=[("cc", 4 * s + 4)] + St.get("sg_rel", []),
            inc="qSg", n=16)
        sg_rdy = em.c["qSg"]
        x1rel = St.get(("x1tok_rel", slot), [])
        strel = St.get("stats_rel", [])
        em.on("act", lambda e: e.activation(
            T["x1tok"][slot].ap(), T["sg"][sgs].ap(), AF.Square,
            accum_out=stats[:, 1:2]),
            wait=[("qSg", sg_rdy)] + x1rel + strel, inc="sACT")
        sq_d = em.c["sACT"]
        em.on("dve", lambda e: e.reduce_sum(stats[:, 0:1], T["sg"][sgs].ap(), AX.X),
              wait=[("qSg", sg_rdy)] + strel, inc="sDVE")
        em.on("dve", lambda e: e.tensor_scalar_mul(stats[:, 2:3], stats[:, 0:1], 1.0 / D),
              wait=[("sDVE", em.c["sDVE"])], inc="sDVE")
        em.on("dve", lambda e: e.tensor_mul(stats[:, 3:4], stats[:, 2:3], stats[:, 2:3]),
              wait=[("sDVE", em.c["sDVE"])], inc="sDVE")
        em.on("dve", lambda e: e.scalar_tensor_tensor(
            stats[:, 4:5], stats[:, 1:2], 1.0 / D, stats[:, 3:4], ALU.mult, ALU.subtract),
            wait=[("sACT", sq_d), ("sDVE", em.c["sDVE"])], inc="sDVE")
        var_d = em.c["sDVE"]
        em.on("act", lambda e: e.activation(stats[:, 5:6], stats[:, 4:5], AF.Ln,
                                            bias=T["epsT"].ap()),
              wait=[("sDVE", var_d), ("sDVE", St["eps_rdy"])], inc="sACT")
        em.on("act", lambda e: e.activation(stats[:, 5:6], stats[:, 5:6], AF.Exp,
                                            scale=-0.5),
              wait=[("sACT", em.c["sACT"])], inc="sACT")
        rstd_d = em.c["sACT"]
        em.on("dve", lambda e: e.tensor_scalar(
            T["x1tok"][slot].ap(), T["sg"][sgs].ap(), stats[:, 2:3], stats[:, 5:6],
            ALU.subtract, ALU.mult),
            wait=[("sACT", rstd_d), ("sDVE", var_d)], inc="sDVE")
        ap_d = em.c["sDVE"]
        St["sg_rel"] = [("sACT", sq_d), ("sDVE", ap_d)]
        St["stats_rel"] = [("sACT", rstd_d), ("sDVE", ap_d)]
        qxw = "qXwA" if slot == 0 else "qXwB"
        em.on("sync", lambda e: e.dma_start(
            out=x1T[(s * NST + st4) * 128:(s * NST + st4) * 128 + 128, :],
            in_=T["x1tok"][slot].ap()),
            wait=[("sDVE", ap_d)], inc=qxw, n=16)
        St[("x1tok_rel", slot)] = [(qxw, em.c[qxw])]
        St[("x1w_done", s, st4)] = [("qXwA", em.c["qXwA"]), ("qXwB", em.c["qXwB"])]

    def u_x1f(s):
        war = St.get("x1f_war", 0)
        for dc in range(FC):
            em.on("sync", lambda e, dc=dc: e.dma_start(
                out=T["x1f"].ap()[:, dc * GT:(dc + 1) * GT],
                in_=x1T[s * GT:(s + 1) * GT, dc * 128:(dc + 1) * 128],
                transpose=True),
                wait=St[("x1w_done", s, NST - 1)] + [("sPE", war)],
                inc="qXf", n=16)
        St["x1f_rdy"] = em.c["qXf"]

    def u_router(s, st4):
        return _g_router(s, st4)

    def _g_router(s, st4):
        k, rel = claim_bank()
        pb = T["pPj"][k].ap()
        for dc in range(FC):
            w = ([("qXf", St["x1f_rdy"])] + rel) if dc == 0 else []
            em.on("pe", lambda e, dc=dc, st4=st4, pb=pb: e.matmul(
                pb[:, 0:8],
                T["x1f"].ap()[:, dc * GT + st4 * 128: dc * GT + st4 * 128 + 128],
                T["rWTb"].ap()[:, dc * 8:(dc + 1) * 8],
                start=(dc == 0), stop=(dc == FC - 1)),
                wait=w, inc="sPE" if dc == FC - 1 else None)
        rt_d = em.c["sPE"]
        em.on("dve", lambda e, st4=st4, pb=pb: e.tensor_tensor(
            T["LG"].ap()[:, st4 * 8:(st4 + 1) * 8], pb[:, 0:8],
            T["ebBc"].ap(), ALU.add),
            wait=[("sPE", rt_d), ("sDVE", St["ebbc"]),
                  ("sDVE", St.get("lg_free", 0))], inc="sDVE")
        St[("lg", s, st4)] = em.c["sDVE"]
        release_bank(k, [("sDVE", em.c["sDVE"])])
        yield

    def u_softplus(s):
        lg = T["LG"].ap()
        ac = T["ACTs"].ap()
        lgd = St[("lg", s, NST - 1)]
        em.on("act", lambda e: e.activation(ac, lg, AF.Abs),
              wait=[("sDVE", lgd), ("sDVE", St.get("acts_free", 0))], inc="sACT")
        abs_d = em.c["sACT"]
        em.on("act", lambda e: e.activation(ac, ac, AF.Exp, scale=-1.0),
              wait=[("sACT", em.c["sACT"])], inc="sACT")
        em.on("act", lambda e: e.activation(ac, ac, AF.Ln, bias=T["oneT"].ap()),
              wait=[("sACT", em.c["sACT"]), ("sDVE", St["eps_rdy"])], inc="sACT")
        ln_d = em.c["sACT"]
        em.on("dve", lambda e: e.tensor_scalar_max(lg, lg, 0.0),
              wait=[("sACT", abs_d)], inc="sDVE")
        relu_d = em.c["sDVE"]
        em.on("dve", lambda e: e.tensor_tensor(ac, ac, lg, ALU.add),
              wait=[("sACT", ln_d), ("sDVE", relu_d)], inc="sDVE")
        sp_d = em.c["sDVE"]
        St["lg_free"] = sp_d
        em.on("act", lambda e: e.activation(ac, ac, AF.Ln),
              wait=[("sDVE", sp_d)], inc="sACT")
        em.on("act", lambda e: e.activation(ac, ac, AF.Exp, scale=0.5),
              wait=[("sACT", em.c["sACT"])], inc="sACT")
        St["acts_rdy"] = em.c["sACT"]

    def u_top2(s, st4):
        a = T["ACTs"].ap()[:, st4 * 8:(st4 + 1) * 8]
        em.on("dve", lambda e, a=a: e.max(T["mx8"].ap(), a),
              wait=[("sACT", St["acts_rdy"]), ("sDVE", St.get("top2_free", 0))],
              inc="sDVE")
        em.on("dve", lambda e, a=a: e.max_index(T["idx8"].ap(), T["mx8"].ap(), a),
              wait=[("sDVE", em.c["sDVE"])], inc="sDVE")
        em.on("dve", lambda e: e.tensor_copy(T["idxF"].ap(), T["idx8"].ap()[:, 0:2]),
              wait=[("sDVE", em.c["sDVE"])], inc="sDVE")
        em.on("dve", lambda e: e.tensor_scalar(
            T["f1"].ap(), iota8, T["idxF"].ap()[:, 0:1], None, ALU.is_equal),
            wait=[("sDVE", em.c["sDVE"])], inc="sDVE")
        em.on("dve", lambda e: e.tensor_scalar(
            T["f2"].ap(), iota8, T["idxF"].ap()[:, 1:2], None, ALU.is_equal),
            wait=[("sDVE", em.c["sDVE"])], inc="sDVE")
        em.on("dve", lambda e: e.tensor_tensor(
            T["f1"].ap(), T["f1"].ap(), T["f2"].ap(), ALU.add),
            wait=[("sDVE", em.c["sDVE"])], inc="sDVE")
        em.on("dve", lambda e, a=a: e.tensor_tensor(
            T["CTpad"].ap()[:, 0:8], a, T["f1"].ap(), ALU.mult),
            wait=[("sDVE", em.c["sDVE"]), ("sDVE", St["ctpad0"]),
                  ("sPE", St.get("ctp_prev", 0))], inc="sDVE")
        cm_d = em.c["sDVE"]
        St["top2_free"] = cm_d
        if st4 == NST - 1:
            St["acts_free"] = cm_d
        em.on("pe", lambda e: e.transpose(T["pKt"].ap(), T["CTpad"].ap(), identB),
              wait=[("sDVE", cm_d), ("sACT", St.get("ktev_last", 0)),
                    ("qTh", St["qp_theta"])], inc="sPE")
        St["ctp_prev"] = em.c["sPE"]
        em.on("act", lambda e, st4=st4: e.copy(
            T["CT"].ap()[0:8, st4 * 128:(st4 + 1) * 128], T["pKt"].ap()[0:8, :]),
            wait=[("sPE", St["ctp_prev"]), ("sPE", St.get("ct_read_pe", 0))],
            inc="sACT")
        St["ktev_last"] = em.c["sACT"]
        if st4 == NST - 1:
            St["ct_rdy"] = em.c["sACT"]

    def u_wload(s, e8):
        s1 = St.get("w1_slot", 0)
        St["w1_slot"] = 1 - s1
        s2 = St.get("w2_slot", 0)
        St["w2_slot"] = (s2 + 1) % 3
        q1 = "qW1a" if s1 == 0 else "qW1b"
        q2 = ("qW2a", "qW2b", "qW2c")[s2]
        if e8 == 8:
            src1 = dram["sW1"].ap().rearrange("(c p) h -> p c h", p=128)
            src2 = dram["sW2"].ap().rearrange("(c p) d -> p c d", p=128)
        else:
            src1 = dram["rW1"].ap()[e8].rearrange("(c p) h -> p c h", p=128)
            src2 = dram["rW2"].ap()[e8].rearrange("(c p) d -> p c d", p=128)
        em.on("pool", lambda e, src1=src1, s1=s1: e.dma_start(
            out=T["rw1b"][s1].ap().rearrange("p (c h) -> p c h", c=FC),
            in_=src1), wait=[("sPE", St.get(("w1_war", s1), 0))], inc=q1, n=16)
        em.on("pool", lambda e, src2=src2, s2=s2: e.dma_start(
            out=T["rw2b"][s2].ap().rearrange("p (c d) -> p c d", c=4),
            in_=src2), wait=[("sPE", St.get(("w2_war", s2), 0))], inc=q2, n=16)
        St[("w_rdy", s, e8)] = [(q1, em.c[q1]), (q2, em.c[q2])]
        St[("w1_at", s, e8)] = s1
        St[("w2_at", s, e8)] = s2

    def u_layer1(s, e8, gi):
        return _g_layer1(s, e8, gi)

    def _g_layer1(s, e8, gi):
        s1 = St[("w1_at", s, e8)]
        wrdy = St[("w_rdy", s, e8)]
        eslot = EXPQ[gi].index(e8)
        w1 = T["rw1b"][s1].ap()
        shared = (e8 == 8)
        if not shared:
            k, rel = claim_bank()
            pb = T["pPj"][k].ap()
            em.on("pe", lambda e, e8=e8, pb=pb: e.matmul(
                pb, T["selB"].ap()[:, e8 * 128:(e8 + 1) * 128],
                T["CT"].ap()[0:8, :], start=True, stop=True),
                wait=[("sACT", St["ct_rdy"])] + rel, inc="sPE")
            bc_d = em.c["sPE"]
            em.on("act", lambda e, pb=pb: e.copy(T["bcT"].ap(), pb),
                  wait=[("sPE", bc_d), ("sDVE", St.get("bct_free", 0))], inc="sACT")
            bct_rdy = em.c["sACT"]
            release_bank(k, [("sACT", bct_rdy)])
            yield
        else:
            bct_rdy = 0
        hread = St.get("hfA_free", 0)
        scl_d = 0
        for hc in range(4):
            k, rel = claim_bank()
            pb = T["pPj"][k].ap()
            for dc in range(FC):
                w = (list(wrdy) + [("qXf", St["x1f_rdy"])] + rel) if dc == 0 else []
                em.on("pe", lambda e, dc=dc, hc=hc, w1=w1, pb=pb: e.matmul(
                    pb, w1[:, dc * 512 + hc * 128: dc * 512 + hc * 128 + 128],
                    T["x1f"].ap()[:, dc * GT:(dc + 1) * GT],
                    start=(dc == 0), stop=(dc == FC - 1)),
                    wait=w, inc="sPE" if dc == FC - 1 else None)
            hf_d = em.c["sPE"]
            bias = (T["sb1c"].ap()[:, hc:hc + 1] if shared
                    else T["rb1c"].ap()[:, e8 * 4 + hc: e8 * 4 + hc + 1])
            hsl = T["hfA"].ap()[:, (eslot * 4 + hc) * GT:(eslot * 4 + hc + 1) * GT]
            em.on("act", lambda e, hsl=hsl, bias=bias, pb=pb: e.activation(
                hsl, pb, AF.Silu, bias=bias),
                wait=[("sPE", hf_d), ("sPE", hread), ("qC", qs_consts),
                      ("sDVE", St.get(("hf_waw", eslot, hc), 0))], inc="sACT")
            silu_d = em.c["sACT"]
            if not shared:
                em.on("dve", lambda e, hsl=hsl: e.tensor_tensor(
                    hsl, hsl, T["bcT"].ap(), ALU.mult),
                    wait=[("sACT", silu_d), ("sACT", bct_rdy)], inc="sDVE")
                scl_d = em.c["sDVE"]
                St[("hf_waw", eslot, hc)] = scl_d
            else:
                St[("hf_waw", eslot, hc)] = 0
            release_bank(k, [("sACT", silu_d)])
            if hc < 3:
                yield
        St[("hf_rdy", gi, e8)] = (em.c["sACT"], scl_d)
        St[("w1_war", s1)] = em.c["sPE"]
        if not shared:
            St["bct_free"] = scl_d
        if e8 == 8:
            St["x1f_war"] = em.c["sPE"]

    def u_yacc(s, gi, st4, half):
        return _g_yacc(s, gi, st4, half)

    def _g_yacc(s, gi, st4, half):
        grp = EXPQ[gi]
        k, rel = claim_bank()
        pb = T["pPj"][k].ap()
        first = True
        if gi == 0:
            em.on("pe", lambda e, half=half, pb=pb: e.matmul(
                pb, onesR, T["sb2b"].ap()[0:1, half * 512:(half + 1) * 512],
                start=True, stop=False),
                wait=rel + [("sACT", St["ct_rdy"]), ("qTh", St["qp_theta"]),
                            ("sDVE", St["ones_rdy"])])
            em.on("pe", lambda e, st4=st4, half=half, pb=pb: e.matmul(
                pb, T["CT"].ap()[0:8, st4 * 128:(st4 + 1) * 128],
                T["rb2b"].ap()[0:8, half * 512:(half + 1) * 512],
                start=False, stop=False))
            first = False
        for e8 in grp:
            eslot = grp.index(e8)
            s2 = St[("w2_at", s, e8)]
            w2 = T["rw2b"][s2].ap()
            wrdy = St[("hf_rdy", gi, e8)]
            wq = St[("w_rdy", s, e8)][1]
            for hc in range(4):
                lastf = (e8 == grp[-1] and hc == 3)
                hsl = T["hfA"].ap()[:, (eslot * 4 + hc) * GT + st4 * 128:
                                    (eslot * 4 + hc) * GT + st4 * 128 + 128]
                w = []
                if first:
                    w = rel + [("sACT", wrdy[0]), ("sDVE", wrdy[1]), wq]
                elif hc == 0:
                    w = [("sACT", wrdy[0]), ("sDVE", wrdy[1]), wq]
                em.on("pe", lambda e, hsl=hsl, w2=w2, hc=hc, half=half,
                      lastf=lastf, first=first, pb=pb: e.matmul(
                          pb, hsl,
                          w2[:, hc * D + half * 512: hc * D + half * 512 + 512],
                          start=first, stop=lastf),
                      wait=w, inc="sPE" if lastf else None)
                first = False
            if e8 != grp[-1]:
                yield
        y_d = em.c["sPE"]
        ysl = T["yacc"].ap()[:, st4 * D + half * 512: st4 * D + half * 512 + 512]
        if gi == 0:
            em.on("dve", lambda e, ysl=ysl, pb=pb: e.tensor_copy(ysl, pb),
                  wait=[("sPE", y_d)] + St.get(("y_rel", st4, half), []),
                  inc="sDVE")
        else:
            em.on("dve", lambda e, ysl=ysl, pb=pb: e.tensor_tensor(
                ysl, ysl, pb, ALU.add),
                wait=[("sPE", y_d), ("sDVE", St.get(("yw", st4, half), 0))],
                inc="sDVE")
        St[("yw", st4, half)] = em.c["sDVE"]
        release_bank(k, [("sDVE", em.c["sDVE"])])
        if st4 == NST - 1 and half == 1:
            St["hfA_free"] = y_d
            for e8 in grp:
                St[("w2_war", St[("w2_at", s, e8)])] = y_d
            if gi == len(EXPQ) - 1:
                St["ct_read_pe"] = y_d

    def u_ln2(s, st4):
        slot = st4 % 2
        x1rel = St.get(("x1tok_rel", slot), [])
        em.on("sync", lambda e, st4=st4: e.dma_start(
            out=T["x1tok"][slot].ap(),
            in_=x1T[(s * NST + st4) * 128:(s * NST + st4) * 128 + 128, :]),
            wait=x1rel + St[("x1w_done", s, NST - 1)], inc=("qTkA" if slot == 0 else "qTkB"), n=16)
        xr = ("qTkA" if slot == 0 else "qTkB", em.c["qTkA" if slot == 0 else "qTkB"])
        ysl = T["yacc"].ap()[:, st4 * D:(st4 + 1) * D]
        em.on("dve", lambda e, ysl=ysl, slot=slot: e.tensor_tensor(
            ysl, ysl, T["x1tok"][slot].ap(), ALU.add),
            wait=[xr, ("sDVE", St[("yw", st4, 0)]),
                  ("sDVE", St[("yw", st4, 1)])], inc="sDVE")
        s2_d = em.c["sDVE"]
        strel = St.get("stats_rel", [])
        em.on("act", lambda e, ysl=ysl, slot=slot: e.activation(
            T["x1tok"][slot].ap(), ysl, AF.Square, accum_out=stats[:, 1:2]),
            wait=[("sDVE", s2_d)] + strel, inc="sACT")
        sq_d = em.c["sACT"]
        em.on("dve", lambda e, ysl=ysl: e.reduce_sum(stats[:, 0:1], ysl, AX.X),
              wait=[("sDVE", s2_d)] + strel, inc="sDVE")
        em.on("dve", lambda e: e.tensor_scalar_mul(stats[:, 2:3], stats[:, 0:1], 1.0 / D),
              wait=[("sDVE", em.c["sDVE"])], inc="sDVE")
        em.on("dve", lambda e: e.tensor_mul(stats[:, 3:4], stats[:, 2:3], stats[:, 2:3]),
              wait=[("sDVE", em.c["sDVE"])], inc="sDVE")
        em.on("dve", lambda e: e.scalar_tensor_tensor(
            stats[:, 4:5], stats[:, 1:2], 1.0 / D, stats[:, 3:4], ALU.mult, ALU.subtract),
            wait=[("sACT", sq_d), ("sDVE", em.c["sDVE"])], inc="sDVE")
        var_d = em.c["sDVE"]
        em.on("act", lambda e: e.activation(stats[:, 5:6], stats[:, 4:5], AF.Ln,
                                            bias=T["epsT"].ap()),
              wait=[("sDVE", var_d)], inc="sACT")
        em.on("act", lambda e: e.activation(stats[:, 5:6], stats[:, 5:6], AF.Exp,
                                            scale=-0.5),
              wait=[("sACT", em.c["sACT"])], inc="sACT")
        rstd_d = em.c["sACT"]
        oslot = 0
        qo = "qOutA" if st4 % 2 == 0 else "qOutB"
        em.on("dve", lambda e, ysl=ysl, oslot=oslot: e.tensor_scalar(
            T["outT"][oslot].ap(), ysl, stats[:, 2:3], stats[:, 5:6],
            ALU.subtract, ALU.mult),
            wait=[("sACT", rstd_d), ("qOutA", em.c["qOutA"]),
                  ("qOutB", em.c["qOutB"]), ("sDVE", var_d)], inc="sDVE")
        ow = em.c["sDVE"]
        St["stats_rel"] = [("sACT", rstd_d), ("sDVE", ow)]
        St[("y_rel", st4, 0)] = [("sDVE", ow)]
        St[("y_rel", st4, 1)] = [("sDVE", ow)]
        St[("x1tok_rel", slot)] = [("sACT", sq_d), ("sDVE", s2_d)]
        em.on("sync", lambda e, s=s, st4=st4, oslot=oslot: e.dma_start(
            out=dram["out"].ap()[(s * NST + st4) * 128:(s * NST + st4) * 128 + 128, :],
            in_=T["outT"][oslot].ap()),
            wait=[("sDVE", ow)], inc=qo, n=16)

    def pump(c, budget):
        q = St["ffnq"]
        k = 0
        while k < budget:
            gen = St.get("ffngen")
            if gen is None:
                if not (q and q[0][0] <= c):
                    return
                _, fn = q.pop(0)
                r = fn()
                if hasattr(r, "__next__"):
                    St["ffngen"] = r
                    gen = r
                else:
                    k += 1
                    continue
            try:
                next(gen)
                k += 1
            except StopIteration:
                St["ffngen"] = None

    St["x1f_war"] = 0

    for s in range(4):
        ffn_stage_units(s)

    pf_gram(0)
    for _ in pf_steps(0):
        pass
    for c in range(NCH):
        tile_load(c + 2)
        if c % 16 < 4:
            start_block(4 * (c // 16) + 4 + c % 16)
        X = chain_ctx(c)
        chain_r(c, X)
        pf_gram(c + 1)
        chain_outq(c, X)
        pump(c, 3)
        chain_solve(c, X)
        steps = pf_steps(c + 1)
        for fc4 in range(4):
            next(steps, None)
            if c < NCH - 1:
                chain_dw(c, X, [2 * fc4])
            next(steps, None)
            if c < NCH - 1:
                chain_dw(c, X, [2 * fc4 + 1])
            if fc4 % 2 == 1:
                pump(c, 2)
            drip(1)
        for si, _ in enumerate(steps):
            if si % 2 == 0:
                pump(c, 2)
            drip(1)
        if c < NCH - 1:
            chain_db(c, X)
        pump(c, 3)
        drip(1)
        if c % 16 == 5 and c // 16 < 3:
            stage_kq_dma(c // 16 + 1)
        if c % 16 == 6 and c // 16 < 3:
            stage_kq_ag(c // 16 + 3)
        if c % 4 == 3:
            stage_collective(c // 16, (c % 16) // 4)
    drip(10 ** 9)
    pump(NCH + 2, 10 ** 9)

    finals = [("qOutA", em.c["qOutA"]), ("qOutB", em.c["qOutB"])]
    em.on("sync", lambda e: e.nop(), wait=finals)
    em.on("pool", lambda e: e.nop(), wait=finals)


def kernel(L0_attn_in_seq, L0_attn_theta_K, L0_attn_theta_V, L0_attn_theta_Q,
           L0_attn_W0, L0_attn_b0, L0_ffn_routing_W, L0_ffn_expert_bias,
           L0_ffn_sW1, L0_ffn_sb1, L0_ffn_sW2, L0_ffn_sb2, L0_ffn_rW1,
           L0_ffn_rb1, L0_ffn_rW2, L0_ffn_rb2):
    f32 = np.float32
    x = np.asarray(L0_attn_in_seq, f32)
    Tk = np.asarray(L0_attn_theta_K, f32)
    Tv = np.asarray(L0_attn_theta_V, f32)
    Tq = np.asarray(L0_attn_theta_Q, f32)
    W0 = np.asarray(L0_attn_W0, f32)
    b0 = np.asarray(L0_attn_b0, f32)
    rW = np.asarray(L0_ffn_routing_W, f32)
    eb = np.asarray(L0_ffn_expert_bias, f32)
    sb1 = np.asarray(L0_ffn_sb1, f32)
    sb2 = np.asarray(L0_ffn_sb2, f32)
    rb1 = np.asarray(L0_ffn_rb1, f32)
    eta = np.float32(ETA)
    TkTf = np.ascontiguousarray(Tk.T)
    TqTf = np.ascontiguousarray(Tq.T)
    shared = dict(
        sW1=np.ascontiguousarray(np.asarray(L0_ffn_sW1, f32)),
        sb1T=np.ascontiguousarray(sb1.reshape(4, 128).T),
        sW2=np.ascontiguousarray(np.asarray(L0_ffn_sW2, f32)),
        sb2=np.ascontiguousarray(sb2.reshape(1, D)),
        rW1=np.ascontiguousarray(np.asarray(L0_ffn_rW1, f32)),
        rb1T=np.ascontiguousarray(rb1.reshape(8, 4, 128).transpose(2, 0, 1).reshape(128, 32)),
        rW2=np.ascontiguousarray(np.asarray(L0_ffn_rW2, f32)),
        rb2=np.ascontiguousarray(np.asarray(L0_ffn_rb2, f32)),
        rWT=np.ascontiguousarray(rW.T), ebr=np.ascontiguousarray(eb.reshape(1, 8)),
        cIdent=np.eye(128, dtype=f32),
        cMaskL=np.ascontiguousarray(eta * np.tril(np.ones((128, 128), f32), -1)),
        cMaskU=np.ascontiguousarray(eta * np.triu(np.ones((128, 128), f32), 1)),
        cMaskUI=np.ascontiguousarray(np.triu(np.ones((128, 128), f32), 0)),
        cIota=np.ascontiguousarray(np.tile(np.arange(8, dtype=f32), (128, 1))),
        cOnes=np.ones((128, 128), f32),
        cSel=np.ascontiguousarray(
            (np.arange(8)[:, None] == (np.arange(8 * 128) // 128)[None, :]).astype(f32)),
    )
    TvT = np.ascontiguousarray(Tv.T)
    W0T = np.ascontiguousarray(W0.T)
    xTb = [np.ascontiguousarray(x[b].T) for b in range(2)]

    in_maps = []
    for i in range(8):
        b, r = i // 4, i % 4
        sl = slice(r * ES, (r + 1) * ES)
        m = dict(shared)
        m["xT"] = xTb[b]
        m["TkT"] = np.ascontiguousarray(TkTf[:, sl])
        m["TqT"] = np.ascontiguousarray(TqTf[:, sl])
        m["xsl"] = np.ascontiguousarray(x[b][:, sl])
        m["TvTs"] = np.ascontiguousarray(TvT[:, sl])
        m["W0Ts"] = np.ascontiguousarray(W0T[:, sl])
        m["b0s"] = np.ascontiguousarray(b0[sl].reshape(1, ES))
        m["tokOff"] = np.array([[r * 4 * GT * ES]], dtype=np.int32)
        in_maps.append(m)

    nc = build_nc()
    res = run_bass_kernel_spmd(nc, in_maps, core_ids=list(range(8)))
    outp = np.empty((2, S, D), dtype=np.float32)
    for i in range(8):
        b, r = i // 4, i % 4
        o = res.results[i]["out"]
        for s in range(4):
            outp[b, 2048 * s + GT * r: 2048 * s + GT * (r + 1), :] = \
                o[s * GT:(s + 1) * GT, :]
    return outp


# version 27 (final)
# speedup vs baseline: 1.1145x; 1.0186x over previous
import contextlib
import numpy as np

import concourse.bass as bass
import concourse.mybir as mybir
from concourse.bass_utils import run_bass_kernel_spmd

F32 = mybir.dt.float32
BF = mybir.dt.bfloat16
I32 = mybir.dt.int32
AF = mybir.ActivationFunctionType
ALU = mybir.AluOpType
AX = mybir.AxisListType

S = 8192
D = 1024
C = 128
NCH = S // C
ES = 256
FC = 8
BLK = 512
NBLK = S // BLK
ETA = 0.01 * 2.0 / 1024.0
GT = 512
NST = 4
TOK = 2048

ENGS = ("pe", "dve", "act", "sync", "pool")
SEMS = ("sPE", "sDVE", "sACT", "sPL", "cc",
        "qC", "qTh", "qXA", "qXB", "qXTa", "qXTb",
        "qSO00", "qSO01", "qSO10", "qSO11", "qSO20", "qSO21", "qSO30", "qSO31",
        "qW1a", "qW1b", "qW2a", "qW2b", "qW2c", "qSg", "qXwA", "qXwB", "qXf", "qTkA", "qTkB",
        "qKI", "qKTa", "qKTb", "qKTc", "ck",
        "qOutA", "qOutB")


class Em:

    def __init__(self, sems, active, eng):
        self.sems = sems
        self.active = active
        self.eng = eng
        self.c = {k: 0 for k in SEMS}
        self.lw = {}

    def on(self, engine, fn, wait=(), inc=None, n=1):
        if engine == self.active:
            for s, v in wait:
                if v and v > self.lw.get(s, 0):
                    self.eng.wait_ge(self.sems[s], v)
                    self.lw[s] = v
            inst = fn(self.eng)
            if inst is not None and inc is not None:
                inst.then_inc(self.sems[inc], n)
        if inc is not None:
            self.c[inc] += n
            return self.c[inc]
        return None


def build_nc():
    nc = bass.Bass("TRN2", target_bir_lowering=False)

    def inp(name, shape, dt=F32):
        return nc.dram_tensor(name, list(shape), dt, kind="ExternalInput")

    dram = dict(
        xT=inp("xT", (D, S)),
        xsl=inp("xsl", (S, ES)),
        TkT=inp("TkT", (D, ES)),
        TqT=inp("TqT", (D, ES)),
        TvTs=inp("TvTs", (D, ES)),
        W0Ts=inp("W0Ts", (D, ES)),
        b0s=inp("b0s", (1, ES)),
        sW1=inp("sW1", (D, 512)),
        sb1T=inp("sb1T", (128, 4)),
        sW2=inp("sW2", (512, D)),
        sb2=inp("sb2", (1, D)),
        rW1=inp("rW1", (8, D, 512)),
        rb1T=inp("rb1T", (128, 32)),
        rW2=inp("rW2", (8, 512, D)),
        rb2=inp("rb2", (8, D)),
        rWT=inp("rWT", (D, 8)),
        ebr=inp("ebr", (1, 8)),
        cIdent=inp("cIdent", (128, 128)),
        cMaskL=inp("cMaskL", (128, 128)),
        cMaskU=inp("cMaskU", (128, 128)),
        cMaskUI=inp("cMaskUI", (128, 128)),
        cIota=inp("cIota", (128, 8)),
        cOnes=inp("cOnes", (128, 128)),
        cSel=inp("cSel", (8, 8 * 128)),
        tokOff=inp("tokOff", (1, 1), I32),
        out=nc.dram_tensor("out", [TOK, D], F32, kind="ExternalOutput"),
        sOut=nc.dram_tensor("sOut", [S, ES], BF, kind="Internal"),
        sGat=nc.dram_tensor("sGat", [4 * S, ES], BF, kind="Internal"),
        x1Tmp=nc.dram_tensor("x1Tmp", [TOK, D], BF, kind="Internal"),
        kqI00=nc.dram_tensor("kqI00", [512, 512], BF, kind="Internal"),
        kqG00=nc.dram_tensor("kqG00", [2048, 512], BF, kind="Internal"),
        kqI0a=nc.dram_tensor("kqI0a", [512, 512], BF, kind="Internal"),
        kqI0b=nc.dram_tensor("kqI0b", [512, 1024], BF, kind="Internal"),
        kqI1=nc.dram_tensor("kqI1", [512, 2048], BF, kind="Internal"),
        kqI2=nc.dram_tensor("kqI2", [512, 2048], BF, kind="Internal"),
        kqI3=nc.dram_tensor("kqI3", [512, 2048], BF, kind="Internal"),
        kqG0a=nc.dram_tensor("kqG0a", [2048, 512], BF, kind="Internal"),
        kqG0b=nc.dram_tensor("kqG0b", [2048, 1024], BF, kind="Internal"),
        kqG1=nc.dram_tensor("kqG1", [2048, 2048], BF, kind="Internal"),
        kqG2=nc.dram_tensor("kqG2", [2048, 2048], BF, kind="Internal"),
        kqG3=nc.dram_tensor("kqG3", [2048, 2048], BF, kind="Internal"),
    )

    with contextlib.ExitStack() as st:
        sems = {k: st.enter_context(nc.semaphore(k)) for k in SEMS}

        def sb(name, shape, dt):
            return st.enter_context(nc.sbuf_tensor(name, list(shape), dt))

        def psm(name, shape, dt=F32):
            return st.enter_context(nc.psum_tensor(name, list(shape), dt))

        rBase = st.enter_context(nc.gpsimd.register("rBase"))
        rCur = st.enter_context(nc.gpsimd.register("rCur"))

        T = dict(
            rBase=rBase, rCur=rCur,
            identB=sb("identB", (128, 128), BF),
            maskL=sb("maskL", (128, 128), F32),
            maskU=sb("maskU", (128, 128), F32),
            maskUI=sb("maskUI", (128, 128), F32),
            iota8=sb("iota8", (128, 8), F32),
            onesR=sb("onesR", (1, 128), BF),
            onesC=sb("onesC", (128, 1), BF),
            tokI=sb("tokI", (1, 1), I32),
            selB=sb("selB", (8, 8 * 128), BF),
            TkTb=sb("TkTb", (128, FC * ES), BF),
            TqTb=sb("TqTb", (128, FC * ES), BF),
            TvTb=sb("TvTb", (128, FC * ES), BF),
            Wt=sb("Wt", (128, FC * ES), F32),
            WtB=sb("WtB", (128, FC * ES), BF),
            bF=sb("bF", (1, ES), F32),
            bB=sb("bB", (1, ES), BF),
            xTb=[sb("xTb0", (128, FC * BLK), BF), sb("xTb1", (128, FC * BLK), BF)],
            KQt=[sb("KQt0", (128, 16 * 128), BF), sb("KQt1", (128, 16 * 128), BF),
                 sb("KQt2", (128, 16 * 128), BF)],
            kqOwn=sb("kqOwn", (128, 4 * 2048), BF),
            VhS=[sb("VhS0", (128, 16 * ES), BF), sb("VhS1", (128, 16 * ES), BF)],
            xslT=[sb("xslT0", (128, ES), F32), sb("xslT1", (128, ES), F32)],
            KtS=[sb("KtS0", (128, FC * 128), BF), sb("KtS1", (128, FC * 128), BF)],
            Ml=[sb("Ml0", (128, 128), BF), sb("Ml1", (128, 128), BF)],
            Mu=[sb("Mu0", (128, 128), BF), sb("Mu1", (128, 128), BF)],
            U=[sb("U0", (128, 128), BF), sb("U1", (128, 128), BF)],
            LT=[sb("LT0", (128, 128), BF), sb("LT1", (128, 128), BF)],
            Pl=[sb("Pla", (128, 128), BF), sb("Plb", (128, 128), BF)],
            Pu=[sb("Pua", (128, 128), BF), sb("Pub", (128, 128), BF)],
            Rb=[sb("Rb0", (128, ES), BF), sb("Rb1", (128, ES), BF)],
            WmB=[sb("WmB0", (128, ES), BF), sb("WmB1", (128, ES), BF)],
            sTile=[sb("sT0", (128, ES), BF), sb("sT1", (128, ES), BF)],
            rw1b=[sb("rw1b0", (128, FC * 512), BF), sb("rw1b1", (128, FC * 512), BF)],
            rw2b=[sb("rw2b0", (128, 4 * D), BF), sb("rw2b1", (128, 4 * D), BF),
                  sb("rw2b2", (128, 4 * D), BF)],
            rWTb=sb("rWTb", (128, FC * 8), BF),
            sb1c=sb("sb1c", (128, 4), F32),
            rb1c=sb("rb1c", (128, 32), F32),
            sb2b=sb("sb2b", (1, D), BF),
            rb2b=sb("rb2b", (8, D), BF),
            ebB=sb("ebB", (1, 8), BF),
            ebBc=sb("ebBc", (128, 8), F32),
            sg=[sb("sg0", (128, D), BF)],
            x1f=sb("x1f", (128, FC * GT), BF),
            x1tok=[sb("x1t0", (128, D), BF), sb("x1t1", (128, D), BF)],
            hfA=sb("hfA", (128, 2 * 4 * GT), BF),
            bcT=sb("bcT", (128, GT), BF),
            yacc=sb("yacc", (128, NST * D), F32),
            outT=[sb("outT0", (128, D), F32)],
            LG=sb("LG", (128, NST * 8), F32),
            ACTs=sb("ACTs", (128, NST * 8), F32),
            mx8=sb("mx8", (128, 8), F32),
            idx8=sb("idx8", (128, 8), mybir.dt.uint32),
            idxF=sb("idxF", (128, 2), F32),
            f1=sb("f1", (128, 8), F32),
            f2=sb("f2", (128, 8), F32),
            CTpad=sb("CTpad", (128, 128), BF),
            CT=sb("CT", (8, NST * 128), BF),
            stats=sb("stats", (128, 6), F32),
            epsT=sb("epsT", (128, 1), F32),
            oneT=sb("oneT", (128, 1), F32),
            pPj=[psm("pPj0", (128, 512)), psm("pPj1", (128, 512))],
            pAB=psm("pAB", (128, 256)),
            pTa=psm("pTa", (128, 256)),
            pTb=psm("pTb", (128, 256)),
            pR=psm("pR", (128, 256)),
            pOut=psm("pOut", (128, 256)),
            pKt=psm("pKt", (128, 128), BF),
        )

        with nc.Block("mk") as blk:
            def run(name):
                def body(eng):
                    em = Em(sems, name, eng)
                    emit_all(em, dram, T)
                return body
            blk.tensor(run("pe"))
            blk.vector(run("dve"))
            blk.scalar(run("act"))
            blk.sync(run("sync"))
            blk.gpsimd(run("pool"))

    return nc


def emit_all(em, dram, T):
    identB = T["identB"].ap()
    onesR = T["onesR"].ap()
    onesC = T["onesC"].ap()
    maskL = T["maskL"].ap()
    maskU = T["maskU"].ap()
    maskUI = T["maskUI"].ap()
    iota8 = T["iota8"].ap()
    stats = T["stats"].ap()

    xT3 = dram["xT"].ap().rearrange("(c p) t -> p c t", p=128)
    TkT3 = dram["TkT"].ap().rearrange("(c p) e -> p c e", p=128)
    TqT3 = dram["TqT"].ap().rearrange("(c p) e -> p c e", p=128)
    KQIN = [dram["kqI00"], dram["kqI0a"], dram["kqI0b"],
            dram["kqI1"], dram["kqI2"], dram["kqI3"]]
    KQGT = [dram["kqG00"], dram["kqG0a"], dram["kqG0b"],
            dram["kqG1"], dram["kqG2"], dram["kqG3"]]

    def kq_src(c):
        if c < 4:
            return 0, c * 128
        if c < 8:
            return 1, (c - 4) * 128
        if c < 16:
            return 2, (c - 8) * 128
        return 3 + (c // 16 - 1), (c % 16) * 128
    TvT3 = dram["TvTs"].ap().rearrange("(c p) e -> p c e", p=128)
    W0T3 = dram["W0Ts"].ap().rearrange("(c p) e -> p c e", p=128)
    x1T = dram["x1Tmp"].ap()
    sGatH = dram["sGat"]

    St = {
        "xsl_rdy": {}, "xtb_rdy": {},
        "wtb_rdy": 0, "bB_rdy": 0, "dbadd": 0,
        "lastchain_out": 0, "last_revac": 0, "tbuild_prev": 0,
        "drip": [],
        "bankrel": {0: [], 1: []}, "bankrr": 0,
        "ffnq": [],
    }

    em.on("sync", lambda e: e.dma_start(out=maskL, in_=dram["cMaskL"].ap()), inc="qC", n=16)
    em.on("sync", lambda e: e.dma_start(out=maskU, in_=dram["cMaskU"].ap()), inc="qC", n=16)
    em.on("sync", lambda e: e.dma_start(out=maskUI, in_=dram["cMaskUI"].ap()), inc="qC", n=16)
    em.on("sync", lambda e: e.dma_start(out=iota8, in_=dram["cIota"].ap()), inc="qC", n=16)
    em.on("sync", lambda e: e.dma_start(
        out=T["Wt"].ap().rearrange("p (c e) -> p c e", c=FC), in_=W0T3), inc="qC", n=16)
    em.on("sync", lambda e: e.dma_start(out=T["bF"].ap(), in_=dram["b0s"].ap()), inc="qC", n=16)
    em.on("sync", lambda e: e.dma_start(out=T["tokI"].ap(), in_=dram["tokOff"].ap()), inc="qC", n=16)
    em.on("sync", lambda e: e.dma_start(out=T["sb1c"].ap(), in_=dram["sb1T"].ap()), inc="qC", n=16)
    em.on("sync", lambda e: e.dma_start(out=T["rb1c"].ap(), in_=dram["rb1T"].ap()), inc="qC", n=16)
    qs_consts = em.c["qC"]
    em.on("sync", lambda e: e.dma_start(out=T["xslT"][0].ap(), in_=dram["xsl"].ap()[0:128, :]), inc="qXA", n=16)
    St["xsl_rdy"][0] = ("qXA", em.c["qXA"])

    em.on("pool", lambda e: e.dma_start(out=identB, in_=dram["cIdent"].ap()), inc="qTh", n=16)
    em.on("pool", lambda e: e.dma_start(
        out=T["TkTb"].ap().rearrange("p (c e) -> p c e", c=FC), in_=TkT3), inc="qTh", n=16)
    em.on("pool", lambda e: e.dma_start(
        out=T["TqTb"].ap().rearrange("p (c e) -> p c e", c=FC), in_=TqT3), inc="qTh", n=16)
    em.on("pool", lambda e: e.dma_start(
        out=T["TvTb"].ap().rearrange("p (c e) -> p c e", c=FC), in_=TvT3), inc="qTh", n=16)
    St["qp_kqv"] = em.c["qTh"]
    em.on("pool", lambda e: e.dma_start(
        out=T["rWTb"].ap().rearrange("p (c e) -> p c e", c=FC),
        in_=dram["rWT"].ap().rearrange("(c p) e -> p c e", p=128)), inc="qTh", n=16)
    em.on("pool", lambda e: e.dma_start(out=T["sb2b"].ap(), in_=dram["sb2"].ap()), inc="qTh", n=16)
    em.on("pool", lambda e: e.dma_start(out=T["rb2b"].ap(), in_=dram["rb2"].ap()), inc="qTh", n=16)
    em.on("pool", lambda e: e.dma_start(out=T["ebB"].ap(), in_=dram["ebr"].ap()), inc="qTh", n=16)
    em.on("pool", lambda e: e.dma_start(out=T["selB"].ap(), in_=dram["cSel"].ap()), inc="qTh", n=16)
    St["qp_theta"] = em.c["qTh"]
    em.on("pool", lambda e: e.reg_load(T["rBase"], T["tokI"].ap()[0:1, 0:1]),
          wait=[("qC", qs_consts)])
    em.on("pool", lambda e: e.dma_start(
        out=T["xTb"][0].ap().rearrange("p (c t) -> p c t", c=FC),
        in_=xT3[:, :, 0:BLK]), inc="qXTa", n=16)
    St["xtb_rdy"][0] = ("qXTa", em.c["qXTa"])

    em.on("dve", lambda e: e.memset(onesR, 1.0), inc="sDVE")
    em.on("dve", lambda e: e.memset(onesC, 1.0), inc="sDVE")
    St["ones_rdy"] = em.c["sDVE"]
    for fc in range(FC):
        em.on("act", lambda e, fc=fc: e.copy(
            T["WtB"].ap()[:, fc * ES:(fc + 1) * ES],
            T["Wt"].ap()[:, fc * ES:(fc + 1) * ES]),
            wait=[("qC", qs_consts)], inc="sACT")
        St[("wtbfc", fc)] = ("sACT", em.c["sACT"])
    em.on("act", lambda e: e.copy(T["bB"].ap(), T["bF"].ap()), inc="sACT")
    St["wtb_rdy"] = em.c["sACT"]
    St["bB_rdy"] = em.c["sACT"]
    em.on("dve", lambda e: e.memset(T["CTpad"].ap(), 0.0), inc="sDVE")
    St["ctpad0"] = em.c["sDVE"]
    em.on("dve", lambda e: e.memset(T["epsT"].ap(), 1e-5), inc="sDVE")
    em.on("dve", lambda e: e.memset(T["oneT"].ap(), 1.0), inc="sDVE")
    St["eps_rdy"] = em.c["sDVE"]
    em.on("pe", lambda e: e.matmul(T["pAB"].ap()[:, 0:8], onesR, T["ebB"].ap(),
                                   start=True, stop=True),
          wait=[("qTh", St["qp_theta"]), ("sDVE", St["ones_rdy"])], inc="sPE")
    em.on("dve", lambda e: e.tensor_copy(T["ebBc"].ap(), T["pAB"].ap()[:, 0:8]),
          wait=[("sPE", em.c["sPE"])], inc="sDVE")
    St["ebbc"] = em.c["sDVE"]
    St["dbadd"] = em.c["sDVE"]

    def claim_bank():
        k = St["bankrr"]
        St["bankrr"] = 1 - k
        return k, list(St["bankrel"][k])

    def release_bank(k, conds):
        St["bankrel"][k] = list(conds)

    RG = [[0, 1, 2, 3], [4, 5, 6, 7]]

    def m_k(fc):
        return (fc // 2) * 4 + (fc % 2)

    def m_q(fc):
        return m_k(fc) + 2

    def make_proj_units(b):
        p = b % 2
        xtb = T["xTb"][p].ap()
        units = []
        war_dve = St["last_revac"]
        kqi = St.get("kqi_cnt", 0)
        xw = [St["xtb_rdy"][b], ("qTh", St["qp_kqv"])]
        vs = (b // 4) % 2
        for half, thb in ((0, T["TkTb"]), (1, T["TqTb"])):
            for ec in range(2):
                mms = []
                for dc in range(FC):
                    def mm(e, dc=dc, ec=ec, thb=thb, bank=None):
                        return e.matmul(
                            T["pPj"][bank].ap(),
                            thb.ap()[:, dc * ES + ec * 128: dc * ES + ec * 128 + 128],
                            xtb[:, dc * BLK:(dc + 1) * BLK],
                            start=(dc == 0), stop=(dc == FC - 1))
                    mms.append(mm)
                co = (half * 2 + ec) * 2048 + (b % 4) * BLK

                def ev(e, co=co, bank=None):
                    return e.copy(T["kqOwn"].ap()[:, co:co + BLK],
                                  T["pPj"][bank].ap())
                units.append(dict(mms=mms, ev=ev, xw=xw, war_dve=0, kqi=kqi))
        for tq in range(4):
            mms = []
            for dc in range(FC):
                def mmv(e, dc=dc, tq=tq, bank=None):
                    return e.matmul(
                        T["pPj"][bank].ap()[:, 0:ES],
                        xtb[:, dc * BLK + tq * 128: dc * BLK + tq * 128 + 128],
                        T["TvTb"].ap()[:, dc * ES:(dc + 1) * ES],
                        start=(dc == 0), stop=(dc == FC - 1))
                mms.append(mmv)
            vo = ((b % 4) * 4 + tq) * ES

            def evv(e, vo=vo, vs=vs, bank=None):
                return e.activation(
                    T["VhS"][vs].ap()[:, vo:vo + ES],
                    T["pPj"][bank].ap()[:, 0:ES], AF.Identity, scale=ETA)
            units.append(dict(mms=mms, ev=evv, xw=xw, war_dve=war_dve, kqi=0))
        return units

    def run_proj_unit(u):
        k, rel = claim_bank()
        w0 = list(u["xw"]) + rel
        for i, mm in enumerate(u["mms"]):
            em.on("pe", lambda e, mm=mm, k=k: mm(e, bank=k),
                  wait=w0 if i == 0 else [],
                  inc="sPE" if i == len(u["mms"]) - 1 else None)
        mmd = em.c["sPE"]
        wv = [("sPE", mmd)]
        if u["war_dve"]:
            wv.append(("sDVE", u["war_dve"]))
        if u["kqi"]:
            wv.append(("qKI", u["kqi"]))
        em.on("act", lambda e, k=k: u["ev"](e, bank=k), wait=wv, inc="sACT")
        release_bank(k, [("sACT", em.c["sACT"])])

    def drip(n):
        lst = St["drip"]
        k = 0
        while lst and k < n:
            run_proj_unit(lst.pop(0))
            k += 1
        if not lst and St.get("drip_block") is not None:
            St[("projmm", St["drip_block"])] = em.c["sPE"]
            St["drip_block"] = None

    def start_block(b):
        if b >= NBLK:
            return
        drip(10 ** 9)
        war = St.get(("projmm", b - 2), 0)

        def ld(e, b=b):
            return e.dma_start(
                out=T["xTb"][b % 2].ap().rearrange("p (c t) -> p c t", c=FC),
                in_=xT3[:, :, b * BLK:(b + 1) * BLK])
        qn = "qXTa" if b % 2 == 0 else "qXTb"
        em.on("pool", ld, wait=[("sPE", war)], inc=qn, n=16)
        St["xtb_rdy"][b] = (qn, em.c[qn])
        St["drip"] = make_proj_units(b)
        St["drip_block"] = b

    def kqO3(c0, w):
        return T["kqOwn"].ap().rearrange("p (q t) -> p q t", q=4)[:, :, c0:c0 + w]

    def stage_kq_dma(s):
        drip(10 ** 9)
        kq_ev = em.c["sACT"]
        St[("vhs", s)] = kq_ev
        em.on("sync", lambda e, s=s: e.dma_start(
            out=KQIN[s + 2].ap().rearrange("(q p) t -> p q t", p=128),
            in_=kqO3(0, 2048)),
            wait=[("sACT", kq_ev)], inc="qKI", n=16)
        St["kqi_cnt"] = em.c["qKI"]

    def stage_kq_ag(buf):
        em.on("pool", lambda e, buf=buf: e.collective_compute(
            "AllGather", ALU.bypass, replica_groups=RG,
            ins=[KQIN[buf].ap()], outs=[KQGT[buf].ap()]),
            wait=[("qKI", em.c["qKI"])], inc="ck", n=1)

    def tile_load(c):
        if c >= NCH:
            return
        buf, off = kq_src(c)
        pt = c % 3
        qn = ("qKTa", "qKTb", "qKTc")[pt]
        src = KQGT[buf].ap().rearrange("(m p) t -> p m t", p=128)
        em.on("sync", lambda e, src=src, off=off, pt=pt: e.dma_start(
            out=T["KQt"][pt].ap().rearrange("p (m t) -> p m t", m=16),
            in_=src[:, :, off:off + 128]),
            wait=[("ck", buf + 1), ("sPE", St.get(("kqt_war", pt), 0))],
            inc=qn, n=16)
        St[("kqt_rdy", c)] = (qn, em.c[qn])

    St["drip"] = make_proj_units(0)
    St["drip_block"] = 0
    drip(10 ** 9)
    em.on("sync", lambda e: e.dma_start(
        out=KQIN[0].ap().rearrange("(q p) t -> p q t", p=128),
        in_=kqO3(0, 512)),
        wait=[("sACT", em.c["sACT"])], inc="qKI", n=16)
    em.on("pool", lambda e: e.collective_compute(
        "AllGather", ALU.bypass, replica_groups=RG,
        ins=[KQIN[0].ap()], outs=[KQGT[0].ap()]),
        wait=[("qKI", em.c["qKI"])], inc="ck", n=1)
    start_block(1)
    drip(10 ** 9)
    kq_ev0 = em.c["sACT"]
    em.on("sync", lambda e: e.dma_start(
        out=KQIN[1].ap().rearrange("(q p) t -> p q t", p=128),
        in_=kqO3(512, 512)),
        wait=[("sACT", kq_ev0)], inc="qKI", n=16)
    em.on("pool", lambda e: e.collective_compute(
        "AllGather", ALU.bypass, replica_groups=RG,
        ins=[KQIN[1].ap()], outs=[KQGT[1].ap()]),
        wait=[("qKI", em.c["qKI"])], inc="ck", n=1)
    start_block(2)
    drip(10 ** 9)
    start_block(3)
    drip(10 ** 9)
    St[("vhs", 0)] = em.c["sACT"]
    em.on("sync", lambda e: e.dma_start(
        out=KQIN[2].ap().rearrange("(q p) t -> p q t", p=128),
        in_=kqO3(1024, 1024)),
        wait=[("sACT", St[("vhs", 0)])], inc="qKI", n=16)
    St["kqi_cnt"] = em.c["qKI"]
    em.on("pool", lambda e: e.collective_compute(
        "AllGather", ALU.bypass, replica_groups=RG,
        ins=[KQIN[2].ap()], outs=[KQGT[2].ap()]),
        wait=[("qKI", em.c["qKI"])], inc="ck", n=1)
    tile_load(0)
    tile_load(1)

    def pf_gram(c):
        if c >= NCH:
            return
        pi = c % 2
        kq = T["KQt"][c % 3]
        pAB = T["pAB"].ap()
        for fc in range(FC):
            w = [St[("kqt_rdy", c)], ("sDVE", St["dbadd"])] if fc == 0 else []
            off = m_k(fc) * 128
            em.on("pe", lambda e, fc=fc, off=off: e.matmul(
                pAB[:, 0:256],
                kq.ap()[:, off:off + 128],
                bass.AP(kq, off, [[16 * 128, 128], [256, 2], [1, 128]]),
                start=(fc == 0), stop=(fc == FC - 1)),
                wait=w, inc="sPE" if fc == FC - 1 else None)
        ab_done = em.c["sPE"]
        em.on("dve", lambda e: e.scalar_tensor_tensor(
            T["Ml"][pi].ap(), pAB[:, 0:128], 1.0, maskL, ALU.add, ALU.mult),
            wait=[("sPE", ab_done), ("sPE", St.get(("uread", pi), 0)),
                  ("sDVE", St.get(("maskwaw", pi), 0))], inc="sDVE")
        em.on("dve", lambda e: e.scalar_tensor_tensor(
            T["Mu"][pi].ap(), pAB[:, 0:128], 1.0, maskU, ALU.add, ALU.mult), inc="sDVE")
        masks_done = em.c["sDVE"]
        St[("maskwaw", pi)] = masks_done
        St[("masks", c)] = masks_done
        em.on("dve", lambda e: e.tensor_tensor(
            T["U"][pi].ap(), identB, T["Mu"][pi].ap(), ALU.subtract),
            wait=[("sDVE", masks_done), ("qTh", St["qp_theta"])], inc="sDVE")
        St[("uadd", c, 0)] = em.c["sDVE"]
        em.on("dve", lambda e: e.scalar_tensor_tensor(
            T["LT"][pi].ap(), pAB[:, 128:256], 1.0, maskUI, ALU.add, ALU.mult),
            wait=[("sPE", ab_done), ("sPE", St.get(("ltread", pi), 0)),
                  ("sDVE", St.get(("ltwaw", pi), 0))], inc="sDVE")
        St[("lt", c)] = em.c["sDVE"]
        St[("ltwaw", pi)] = em.c["sDVE"]

    def pf_steps(c):
        if c >= NCH:
            return
        pi = c % 2
        kq = T["KQt"][c % 3].ap()
        pTa, pTb, pKt = T["pTa"].ap(), T["pTb"].ap(), T["pKt"].ap()
        tb_war = St["tbuild_prev"]
        state = dict(cur_l=T["Ml"][pi].ap(), cur_u=T["Mu"][pi].ap(),
                     nxt=0, puev=0, plev=0)
        war_kt = St.get(("ktread", pi), 0)

        def t_round(j):
            cur_l, cur_u = state["cur_l"], state["cur_u"]
            w = [("sDVE", state["plev"])]
            if j == 1:
                w.append(("sDVE", St[("masks", c)]))
            em.on("pe", lambda e: e.matmul(
                pTb[:, 0:128], cur_u, cur_l, start=True, stop=True),
                wait=w, inc="sPE")
            sql_d = em.c["sPE"]
            npl, npu = T["Pl"][state["nxt"]].ap(), T["Pu"][state["nxt"]].ap()
            em.on("dve", lambda e: e.tensor_copy(npl, pTb[:, 0:128]),
                  wait=[("sPE", sql_d), ("sPE", tb_war),
                        ("sACT", St.get("puevwaw", 0)),
                        ("sDVE", St.get("pevwaw", 0))], inc="sDVE")
            plev = em.c["sDVE"]
            state["plev"] = plev
            St["pevwaw"] = plev
            if j < 6:
                em.on("pe", lambda e: e.matmul(
                    pTa[:, 128:256], cur_l, cur_u, start=True, stop=True),
                    wait=[("sDVE", St[("uadd", c, j - 1)]),
                          ("sACT", state["puev"])], inc="sPE")
                squ_d = em.c["sPE"]
                em.on("act", lambda e: e.copy(npu, pTa[:, 128:256]),
                      wait=[("sPE", squ_d), ("sACT", St.get("puevwaw", 0))],
                      inc="sACT")
                state["puev"] = em.c["sACT"]
                St["puevwaw"] = em.c["sACT"]
            state["cur_l"], state["cur_u"] = npl, npu
            state["nxt"] = 1 - state["nxt"]
            yield

            em.on("pe", lambda e, npl=npl: e.matmul(
                pTa[:, 0:128], npl, T["U"][pi].ap(), start=True, stop=True),
                wait=[("sDVE", plev), ("sDVE", St[("uadd", c, j - 1)]),
                      ("sACT", state["puev"])], inc="sPE")
            umm_d = em.c["sPE"]
            em.on("dve", lambda e: e.tensor_tensor(
                T["U"][pi].ap(), T["U"][pi].ap(), pTa[:, 0:128], ALU.add),
                wait=[("sPE", umm_d), ("sDVE", St[("uadd", c, j - 1)])], inc="sDVE")
            St[("uadd", c, j)] = em.c["sDVE"]
            if j == 6:
                St[("u_rdy", c)] = em.c["sDVE"]
                St["tbuild_prev"] = em.c["sPE"]
            yield

        def kt_step(fc):
            em.on("pe", lambda e, fc=fc: e.transpose(
                pKt, kq[:, m_k(fc) * 128:m_k(fc) * 128 + 128], identB),
                wait=[St[("kqt_rdy", c)], ("sACT", St.get("ktev_last", 0)),
                      ("qTh", St["qp_theta"])], inc="sPE")
            tr_d = em.c["sPE"]
            em.on("act", lambda e, fc=fc: e.copy(
                T["KtS"][pi].ap()[:, fc * 128:(fc + 1) * 128], pKt),
                wait=[("sPE", tr_d), ("sPE", war_kt),
                      ("sACT", St.get(("ktwaw", pi), 0))], inc="sACT")
            St["ktev_last"] = em.c["sACT"]
            if fc == FC - 1:
                St[("kt_rdy", c)] = ("sACT", em.c["sACT"])
                St[("ktwaw", pi)] = em.c["sACT"]
            yield

        for j in range(1, 7):
            yield from t_round(j)
            if j <= 4:
                yield from kt_step(j - 1)
                yield from kt_step(j + 3)

    def chain_ctx(c):
        return dict(
            pi=c % 2, b=c // 4, sub=c % 4,
            kq=T["KQt"][c % 3].ap(),
            pR=T["pR"].ap(), pOut=T["pOut"].ap(), pAB=T["pAB"].ap(),
            wtb=T["WtB"].ap(), wt=T["Wt"].ap())

    def chain_r(c, X):
        pi, b, pR, wtb = X["pi"], X["b"], X["pR"], X["wtb"]
        w0 = [("sACT", St["bB_rdy"]), St[("kqt_rdy", c)],
              ("sDVE", St.get(("sadd", c - 1, 6), 0)),
              ("sDVE", St["dbadd"]),
              ("sACT", St["wtb_rdy"]), ("sDVE", St.get("wtb_rdy_d", 0))]
        for fc in range(FC):
            w = w0 if fc == 0 else []
            em.on("pe", lambda e, fc=fc: e.matmul(
                pR, X["kq"][:, m_k(fc) * 128:m_k(fc) * 128 + 128],
                wtb[:, fc * ES:(fc + 1) * ES],
                start=(fc == 0), stop=False), wait=w)
        em.on("pe", lambda e: e.matmul(pR, onesR, T["bB"].ap(),
                                       start=False, stop=True),
              wait=[("sDVE", St["ones_rdy"])], inc="sPE")
        r_done = em.c["sPE"]
        em.on("dve", lambda e: e.scalar_tensor_tensor(
            T["Rb"][pi].ap(), pR, -ETA,
            T["VhS"][(c // 16) % 2].ap()[:, (c % 16) * ES:(c % 16 + 1) * ES],
            ALU.mult, ALU.add),
            wait=[("sPE", r_done), ("sDVE", St.get(("rbwaw", pi), 0)),
                  ("sACT", St[("vhs", c // 16)])], inc="sDVE")
        X["r_evac"] = em.c["sDVE"]
        St[("rbwaw", pi)] = X["r_evac"]
        St["last_revac"] = X["r_evac"]

    def chain_outq(c, X):
        pOut, wtb = X["pOut"], X["wtb"]
        wo = [("sDVE", St.get(("sadd", c - 1, 7), 0))]
        for fc in range(FC):
            w = wo if fc == 0 else []
            em.on("pe", lambda e, fc=fc: e.matmul(
                pOut, X["kq"][:, m_q(fc) * 128:m_q(fc) * 128 + 128],
                wtb[:, fc * ES:(fc + 1) * ES],
                start=(fc == 0), stop=False), wait=w)
        em.on("pe", lambda e: e.matmul(pOut, onesR, T["bB"].ap(),
                                       start=False, stop=False),
              wait=[("sDVE", St["ones_rdy"])])

    def chain_solve(c, X):
        pi, pR, pOut = X["pi"], X["pR"], X["pOut"]
        em.on("pe", lambda e: e.matmul(
            pR, T["U"][pi].ap(), T["Rb"][pi].ap(), start=True, stop=True),
            wait=[("sDVE", X["r_evac"]), ("sDVE", St[("u_rdy", c)])], inc="sPE")
        wm_done = em.c["sPE"]
        St[("uread", pi)] = wm_done
        em.on("dve", lambda e: e.tensor_copy(T["WmB"][pi].ap(), pR),
              wait=[("sPE", wm_done), ("sPE", St.get(("wmbread", pi), 0))],
              inc="sDVE")
        X["wmb_d"] = em.c["sDVE"]
        yield
        em.on("pe", lambda e: e.matmul(
            pOut, T["LT"][pi].ap(), T["WmB"][pi].ap(), start=False, stop=True),
            wait=[("sDVE", X["wmb_d"]), ("sDVE", St[("lt", c)])], inc="sPE")
        out_done = em.c["sPE"]
        X["out_done"] = out_done
        St["lastchain_out"] = out_done
        St[("ltread", pi)] = out_done
        St[("kqt_war", c % 3)] = out_done
        em.on("dve", lambda e: e.tensor_tensor(
            T["sTile"][pi].ap(), pOut, T["xslT"][pi].ap(), ALU.add),
            wait=[("sPE", out_done), St["xsl_rdy"][c],
                  ("sDVE", St.get(("stwaw", pi), 0)),
                  (f"qSO{(c - 2) // 16}{c % 2}",
                   em.c[f"qSO{(c - 2) // 16}{c % 2}"]) if c >= 2 else ("qC", 0)],
            inc="sDVE")
        s_evac = em.c["sDVE"]
        X["s_evac"] = s_evac
        St[("stwaw", pi)] = s_evac
        St[("sevac", c)] = s_evac
        qso = f"qSO{c // 16}{c % 2}"
        em.on("sync", lambda e, c=c: e.dma_start(
            out=dram["sOut"].ap()[c * 128:(c + 1) * 128, :], in_=T["sTile"][pi].ap()),
            wait=[("sDVE", s_evac)], inc=qso, n=16)
        if c + 1 < NCH:
            qx = "qXA" if (c + 1) % 2 == 0 else "qXB"
            em.on("sync", lambda e, c=c: e.dma_start(
                out=T["xslT"][1 - pi].ap(),
                in_=dram["xsl"].ap()[(c + 1) * 128:(c + 2) * 128, :]),
                wait=[("sDVE", St.get(("sevac", c - 1), 0))], inc=qx, n=16)
            St["xsl_rdy"][c + 1] = (qx, em.c[qx])

    def chain_dw(c, X, fcs):
        pi, pR, pOut, wtb, wt = X["pi"], X["pR"], X["pOut"], X["wtb"], X["wt"]
        kt = T["KtS"][pi].ap()
        for fc in fcs:
            bank = pR if fc % 2 == 0 else pOut
            w = [St[("kt_rdy", c)], ("sDVE", X["wmb_d"])]
            if fc == 1:
                w.append(("sDVE", X["s_evac"]))
            elif fc >= 2:
                w.append(("sDVE", St[("sadd", c, fc - 2)]))
            em.on("pe", lambda e, fc=fc, bank=bank: e.matmul(
                bank, kt[:, fc * 128:(fc + 1) * 128], T["WmB"][pi].ap(),
                start=True, stop=True), wait=w, inc="sPE")
            dw_d = em.c["sPE"]
            em.on("dve", lambda e, fc=fc, bank=bank: e.tensor_tensor(
                wt[:, fc * ES:(fc + 1) * ES], wt[:, fc * ES:(fc + 1) * ES],
                bank, ALU.add),
                wait=[("sPE", dw_d), ("sDVE", St.get(("sadd", c - 1, fc), 0))],
                inc="sDVE")
            St[("sadd", c, fc)] = em.c["sDVE"]
            if fc % 2 == 0:
                em.on("act", lambda e, fc=fc: e.copy(
                    wtb[:, fc * ES:(fc + 1) * ES], wt[:, fc * ES:(fc + 1) * ES]),
                    wait=[("sDVE", St[("sadd", c, fc)]), ("sPE", X["out_done"]),
                          ("sACT", St.get("wtbwaw_a", 0))], inc="sACT")
                St["wtbwaw_a"] = em.c["sACT"]
                St[("wtbfc", fc)] = ("sACT", em.c["sACT"])
            else:
                em.on("dve", lambda e, fc=fc: e.tensor_copy(
                    wtb[:, fc * ES:(fc + 1) * ES], wt[:, fc * ES:(fc + 1) * ES]),
                    wait=[("sDVE", St[("sadd", c, fc)]), ("sPE", X["out_done"])],
                    inc="sDVE")
                St[("wtbfc", fc)] = ("sDVE", em.c["sDVE"])
        if fcs[-1] == FC - 1:
            St["wtb_rdy"] = em.c["sACT"]
            St["wtb_rdy_d"] = em.c["sDVE"]
            St[("ktread", pi)] = em.c["sPE"]

    def chain_db(c, X):
        pi, pAB = X["pi"], X["pAB"]
        em.on("pe", lambda e: e.matmul(
            pAB[0:1, 0:ES], onesC, T["WmB"][pi].ap(), start=True, stop=True),
            wait=[("sDVE", St.get(("lt", c + 1), St[("lt", c)])),
                  ("sDVE", St["ones_rdy"])], inc="sPE")
        db_d = em.c["sPE"]
        St[("wmbread", pi)] = db_d
        em.on("dve", lambda e: e.tensor_tensor(
            T["bF"].ap(), T["bF"].ap(), pAB[0:1, 0:ES], ALU.add),
            wait=[("sPE", db_d), ("sDVE", St["dbadd"])], inc="sDVE")
        St["dbadd"] = em.c["sDVE"]
        em.on("act", lambda e: e.copy(T["bB"].ap(), T["bF"].ap()),
              wait=[("sDVE", St["dbadd"]), ("sPE", X["out_done"]),
                    ("sACT", St["bB_rdy"])], inc="sACT")
        St["bB_rdy"] = em.c["sACT"]

    def stage_collective(sg, q):
        em.on("pool", lambda e, sg=sg, q=q: e.collective_compute(
            "AllGather", ALU.bypass,
            replica_groups=[[0, 1, 2, 3], [4, 5, 6, 7]],
            ins=[dram["sOut"].ap()[sg * 2048 + q * 512:sg * 2048 + (q + 1) * 512, :]],
            outs=[dram["sGat"].ap()[sg * 8192 + q * 2048:sg * 8192 + (q + 1) * 2048, :]]),
            wait=[(f"qSO{sg}0", 32 * (q + 1)), (f"qSO{sg}1", 32 * (q + 1))],
            inc="cc", n=1)

    EXPQ = [[0, 1], [2, 3], [4, 5], [6, 7], [8]]

    def ffn_stage_units(s):
        q = St["ffnq"]
        base = 16 * s + 17 if s < 3 else 60

        def tag(off, fn):
            cap = NCH + 1 if s < 3 else 63
            q.append((min(base + off, cap) if s < 3 or off <= 5 else NCH + 1, fn))

        for st4 in range(NST):
            tag(st4 // 2, lambda e=None, st4=st4: u_sgln(s, st4))
        tag(2, lambda: u_x1f(s))
        for st4 in range(NST):
            tag(3, lambda st4=st4: u_router(s, st4))
        tag(3, lambda: u_softplus(s))
        for st4 in range(NST):
            tag(4, lambda st4=st4: u_top2(s, st4))
        off = 5
        for gi, grp in enumerate(EXPQ):
            for e8 in grp:
                tag(max(off - 2, 0), lambda s=s, e8=e8: u_wload(s, e8))
                tag(off, lambda s=s, e8=e8, gi=gi: u_layer1(s, e8, gi))
                off += 1
            for st4 in range(NST):
                for half in range(2):
                    tag(off + (st4 // 2), lambda s=s, gi=gi, st4=st4, half=half:
                        u_yacc(s, gi, st4, half))
            off += 2
        for st4 in range(NST):
            tag(off + st4 // 2, lambda s=s, st4=st4: u_ln2(s, st4))

    def u_sgln(s, st4):
        slot = st4 % 2
        sgs = 0
        em.on("pool", lambda e: e.reg_add(T["rCur"], T["rBase"],
                                          s * 4 * 2048 * ES + st4 * 128 * ES))
        em.on("pool", lambda e: e.dma_start(
            out=T["sg"][sgs].ap().rearrange("p (r x) -> p r x", r=4),
            in_=bass.AP(sGatH, T["rCur"], [[ES, 128], [512 * ES, 4], [1, ES]])),
            wait=[("cc", 4 * s + 4)] + St.get("sg_rel", []),
            inc="qSg", n=16)
        sg_rdy = em.c["qSg"]
        x1rel = St.get(("x1tok_rel", slot), [])
        strel = St.get("stats_rel", [])
        em.on("act", lambda e: e.activation(
            T["x1tok"][slot].ap(), T["sg"][sgs].ap(), AF.Square,
            accum_out=stats[:, 1:2]),
            wait=[("qSg", sg_rdy)] + x1rel + strel, inc="sACT")
        sq_d = em.c["sACT"]
        em.on("dve", lambda e: e.reduce_sum(stats[:, 0:1], T["sg"][sgs].ap(), AX.X),
              wait=[("qSg", sg_rdy)] + strel, inc="sDVE")
        em.on("dve", lambda e: e.tensor_scalar_mul(stats[:, 2:3], stats[:, 0:1], 1.0 / D),
              wait=[("sDVE", em.c["sDVE"])], inc="sDVE")
        em.on("dve", lambda e: e.tensor_mul(stats[:, 3:4], stats[:, 2:3], stats[:, 2:3]),
              wait=[("sDVE", em.c["sDVE"])], inc="sDVE")
        em.on("dve", lambda e: e.scalar_tensor_tensor(
            stats[:, 4:5], stats[:, 1:2], 1.0 / D, stats[:, 3:4], ALU.mult, ALU.subtract),
            wait=[("sACT", sq_d), ("sDVE", em.c["sDVE"])], inc="sDVE")
        var_d = em.c["sDVE"]
        em.on("act", lambda e: e.activation(stats[:, 5:6], stats[:, 4:5], AF.Ln,
                                            bias=T["epsT"].ap()),
              wait=[("sDVE", var_d), ("sDVE", St["eps_rdy"])], inc="sACT")
        em.on("act", lambda e: e.activation(stats[:, 5:6], stats[:, 5:6], AF.Exp,
                                            scale=-0.5),
              wait=[("sACT", em.c["sACT"])], inc="sACT")
        rstd_d = em.c["sACT"]
        em.on("dve", lambda e: e.tensor_scalar(
            T["x1tok"][slot].ap(), T["sg"][sgs].ap(), stats[:, 2:3], stats[:, 5:6],
            ALU.subtract, ALU.mult),
            wait=[("sACT", rstd_d), ("sDVE", var_d)], inc="sDVE")
        ap_d = em.c["sDVE"]
        St["sg_rel"] = [("sACT", sq_d), ("sDVE", ap_d)]
        St["stats_rel"] = [("sACT", rstd_d), ("sDVE", ap_d)]
        qxw = "qXwA" if slot == 0 else "qXwB"
        em.on("sync", lambda e: e.dma_start(
            out=x1T[(s * NST + st4) * 128:(s * NST + st4) * 128 + 128, :],
            in_=T["x1tok"][slot].ap()),
            wait=[("sDVE", ap_d)], inc=qxw, n=16)
        St[("x1tok_rel", slot)] = [(qxw, em.c[qxw])]
        St[("x1w_done", s, st4)] = [("qXwA", em.c["qXwA"]), ("qXwB", em.c["qXwB"])]

    def u_x1f(s):
        war = St.get("x1f_war", 0)
        for dc in range(FC):
            em.on("sync", lambda e, dc=dc: e.dma_start(
                out=T["x1f"].ap()[:, dc * GT:(dc + 1) * GT],
                in_=x1T[s * GT:(s + 1) * GT, dc * 128:(dc + 1) * 128],
                transpose=True),
                wait=St[("x1w_done", s, NST - 1)] + [("sPE", war)],
                inc="qXf", n=16)
        St["x1f_rdy"] = em.c["qXf"]

    def u_router(s, st4):
        return _g_router(s, st4)

    def _g_router(s, st4):
        k, rel = claim_bank()
        pb = T["pPj"][k].ap()
        for dc in range(FC):
            w = ([("qXf", St["x1f_rdy"])] + rel) if dc == 0 else []
            em.on("pe", lambda e, dc=dc, st4=st4, pb=pb: e.matmul(
                pb[:, 0:8],
                T["x1f"].ap()[:, dc * GT + st4 * 128: dc * GT + st4 * 128 + 128],
                T["rWTb"].ap()[:, dc * 8:(dc + 1) * 8],
                start=(dc == 0), stop=(dc == FC - 1)),
                wait=w, inc="sPE" if dc == FC - 1 else None)
        rt_d = em.c["sPE"]
        em.on("dve", lambda e, st4=st4, pb=pb: e.tensor_tensor(
            T["LG"].ap()[:, st4 * 8:(st4 + 1) * 8], pb[:, 0:8],
            T["ebBc"].ap(), ALU.add),
            wait=[("sPE", rt_d), ("sDVE", St["ebbc"]),
                  ("sDVE", St.get("lg_free", 0))], inc="sDVE")
        St[("lg", s, st4)] = em.c["sDVE"]
        release_bank(k, [("sDVE", em.c["sDVE"])])
        yield

    def u_softplus(s):
        lg = T["LG"].ap()
        ac = T["ACTs"].ap()
        lgd = St[("lg", s, NST - 1)]
        em.on("act", lambda e: e.activation(ac, lg, AF.Abs),
              wait=[("sDVE", lgd), ("sDVE", St.get("acts_free", 0))], inc="sACT")
        abs_d = em.c["sACT"]
        em.on("act", lambda e: e.activation(ac, ac, AF.Exp, scale=-1.0),
              wait=[("sACT", em.c["sACT"])], inc="sACT")
        em.on("act", lambda e: e.activation(ac, ac, AF.Ln, bias=T["oneT"].ap()),
              wait=[("sACT", em.c["sACT"]), ("sDVE", St["eps_rdy"])], inc="sACT")
        ln_d = em.c["sACT"]
        em.on("dve", lambda e: e.tensor_scalar_max(lg, lg, 0.0),
              wait=[("sACT", abs_d)], inc="sDVE")
        relu_d = em.c["sDVE"]
        em.on("dve", lambda e: e.tensor_tensor(ac, ac, lg, ALU.add),
              wait=[("sACT", ln_d), ("sDVE", relu_d)], inc="sDVE")
        sp_d = em.c["sDVE"]
        St["lg_free"] = sp_d
        em.on("act", lambda e: e.activation(ac, ac, AF.Ln),
              wait=[("sDVE", sp_d)], inc="sACT")
        em.on("act", lambda e: e.activation(ac, ac, AF.Exp, scale=0.5),
              wait=[("sACT", em.c["sACT"])], inc="sACT")
        St["acts_rdy"] = em.c["sACT"]

    def u_top2(s, st4):
        a = T["ACTs"].ap()[:, st4 * 8:(st4 + 1) * 8]
        em.on("dve", lambda e, a=a: e.max(T["mx8"].ap(), a),
              wait=[("sACT", St["acts_rdy"]), ("sDVE", St.get("top2_free", 0))],
              inc="sDVE")
        em.on("dve", lambda e, a=a: e.max_index(T["idx8"].ap(), T["mx8"].ap(), a),
              wait=[("sDVE", em.c["sDVE"])], inc="sDVE")
        em.on("dve", lambda e: e.tensor_copy(T["idxF"].ap(), T["idx8"].ap()[:, 0:2]),
              wait=[("sDVE", em.c["sDVE"])], inc="sDVE")
        em.on("dve", lambda e: e.tensor_scalar(
            T["f1"].ap(), iota8, T["idxF"].ap()[:, 0:1], None, ALU.is_equal),
            wait=[("sDVE", em.c["sDVE"])], inc="sDVE")
        em.on("dve", lambda e: e.tensor_scalar(
            T["f2"].ap(), iota8, T["idxF"].ap()[:, 1:2], None, ALU.is_equal),
            wait=[("sDVE", em.c["sDVE"])], inc="sDVE")
        em.on("dve", lambda e: e.tensor_tensor(
            T["f1"].ap(), T["f1"].ap(), T["f2"].ap(), ALU.add),
            wait=[("sDVE", em.c["sDVE"])], inc="sDVE")
        em.on("dve", lambda e, a=a: e.tensor_tensor(
            T["CTpad"].ap()[:, 0:8], a, T["f1"].ap(), ALU.mult),
            wait=[("sDVE", em.c["sDVE"]), ("sDVE", St["ctpad0"]),
                  ("sPE", St.get("ctp_prev", 0))], inc="sDVE")
        cm_d = em.c["sDVE"]
        St["top2_free"] = cm_d
        if st4 == NST - 1:
            St["acts_free"] = cm_d
        em.on("pe", lambda e: e.transpose(T["pKt"].ap(), T["CTpad"].ap(), identB),
              wait=[("sDVE", cm_d), ("sACT", St.get("ktev_last", 0)),
                    ("qTh", St["qp_theta"])], inc="sPE")
        St["ctp_prev"] = em.c["sPE"]
        em.on("act", lambda e, st4=st4: e.copy(
            T["CT"].ap()[0:8, st4 * 128:(st4 + 1) * 128], T["pKt"].ap()[0:8, :]),
            wait=[("sPE", St["ctp_prev"]), ("sPE", St.get("ct_read_pe", 0))],
            inc="sACT")
        St["ktev_last"] = em.c["sACT"]
        if st4 == NST - 1:
            St["ct_rdy"] = em.c["sACT"]

    def u_wload(s, e8):
        s1 = St.get("w1_slot", 0)
        St["w1_slot"] = 1 - s1
        s2 = St.get("w2_slot", 0)
        St["w2_slot"] = (s2 + 1) % 3
        q1 = "qW1a" if s1 == 0 else "qW1b"
        q2 = ("qW2a", "qW2b", "qW2c")[s2]
        if e8 == 8:
            src1 = dram["sW1"].ap().rearrange("(c p) h -> p c h", p=128)
            src2 = dram["sW2"].ap().rearrange("(c p) d -> p c d", p=128)
        else:
            src1 = dram["rW1"].ap()[e8].rearrange("(c p) h -> p c h", p=128)
            src2 = dram["rW2"].ap()[e8].rearrange("(c p) d -> p c d", p=128)
        em.on("pool", lambda e, src1=src1, s1=s1: e.dma_start(
            out=T["rw1b"][s1].ap().rearrange("p (c h) -> p c h", c=FC),
            in_=src1), wait=[("sPE", St.get(("w1_war", s1), 0))], inc=q1, n=16)
        em.on("pool", lambda e, src2=src2, s2=s2: e.dma_start(
            out=T["rw2b"][s2].ap().rearrange("p (c d) -> p c d", c=4),
            in_=src2), wait=[("sPE", St.get(("w2_war", s2), 0))], inc=q2, n=16)
        St[("w_rdy", s, e8)] = [(q1, em.c[q1]), (q2, em.c[q2])]
        St[("w1_at", s, e8)] = s1
        St[("w2_at", s, e8)] = s2

    def u_layer1(s, e8, gi):
        return _g_layer1(s, e8, gi)

    def _g_layer1(s, e8, gi):
        s1 = St[("w1_at", s, e8)]
        wrdy = St[("w_rdy", s, e8)]
        eslot = EXPQ[gi].index(e8)
        w1 = T["rw1b"][s1].ap()
        shared = (e8 == 8)
        if not shared:
            k, rel = claim_bank()
            pb = T["pPj"][k].ap()
            em.on("pe", lambda e, e8=e8, pb=pb: e.matmul(
                pb, T["selB"].ap()[:, e8 * 128:(e8 + 1) * 128],
                T["CT"].ap()[0:8, :], start=True, stop=True),
                wait=[("sACT", St["ct_rdy"])] + rel, inc="sPE")
            bc_d = em.c["sPE"]
            em.on("act", lambda e, pb=pb: e.copy(T["bcT"].ap(), pb),
                  wait=[("sPE", bc_d), ("sDVE", St.get("bct_free", 0))], inc="sACT")
            bct_rdy = em.c["sACT"]
            release_bank(k, [("sACT", bct_rdy)])
            yield
        else:
            bct_rdy = 0
        hread = St.get("hfA_free", 0)
        scl_d = 0
        for hc in range(4):
            k, rel = claim_bank()
            pb = T["pPj"][k].ap()
            for dc in range(FC):
                w = (list(wrdy) + [("qXf", St["x1f_rdy"])] + rel) if dc == 0 else []
                em.on("pe", lambda e, dc=dc, hc=hc, w1=w1, pb=pb: e.matmul(
                    pb, w1[:, dc * 512 + hc * 128: dc * 512 + hc * 128 + 128],
                    T["x1f"].ap()[:, dc * GT:(dc + 1) * GT],
                    start=(dc == 0), stop=(dc == FC - 1)),
                    wait=w, inc="sPE" if dc == FC - 1 else None)
            hf_d = em.c["sPE"]
            bias = (T["sb1c"].ap()[:, hc:hc + 1] if shared
                    else T["rb1c"].ap()[:, e8 * 4 + hc: e8 * 4 + hc + 1])
            hsl = T["hfA"].ap()[:, (eslot * 4 + hc) * GT:(eslot * 4 + hc + 1) * GT]
            em.on("act", lambda e, hsl=hsl, bias=bias, pb=pb: e.activation(
                hsl, pb, AF.Silu, bias=bias),
                wait=[("sPE", hf_d), ("sPE", hread), ("qC", qs_consts),
                      ("sDVE", St.get(("hf_waw", eslot, hc), 0))], inc="sACT")
            silu_d = em.c["sACT"]
            if not shared:
                em.on("dve", lambda e, hsl=hsl: e.tensor_tensor(
                    hsl, hsl, T["bcT"].ap(), ALU.mult),
                    wait=[("sACT", silu_d), ("sACT", bct_rdy)], inc="sDVE")
                scl_d = em.c["sDVE"]
                St[("hf_waw", eslot, hc)] = scl_d
            else:
                St[("hf_waw", eslot, hc)] = 0
            release_bank(k, [("sACT", silu_d)])
            if hc < 3:
                yield
        St[("hf_rdy", gi, e8)] = (em.c["sACT"], scl_d)
        St[("w1_war", s1)] = em.c["sPE"]
        if not shared:
            St["bct_free"] = scl_d
        if e8 == 8:
            St["x1f_war"] = em.c["sPE"]

    def u_yacc(s, gi, st4, half):
        return _g_yacc(s, gi, st4, half)

    def _g_yacc(s, gi, st4, half):
        grp = EXPQ[gi]
        k, rel = claim_bank()
        pb = T["pPj"][k].ap()
        first = True
        if gi == 0:
            em.on("pe", lambda e, half=half, pb=pb: e.matmul(
                pb, onesR, T["sb2b"].ap()[0:1, half * 512:(half + 1) * 512],
                start=True, stop=False),
                wait=rel + [("sACT", St["ct_rdy"]), ("qTh", St["qp_theta"]),
                            ("sDVE", St["ones_rdy"])])
            em.on("pe", lambda e, st4=st4, half=half, pb=pb: e.matmul(
                pb, T["CT"].ap()[0:8, st4 * 128:(st4 + 1) * 128],
                T["rb2b"].ap()[0:8, half * 512:(half + 1) * 512],
                start=False, stop=False))
            first = False
        for e8 in grp:
            eslot = grp.index(e8)
            s2 = St[("w2_at", s, e8)]
            w2 = T["rw2b"][s2].ap()
            wrdy = St[("hf_rdy", gi, e8)]
            wq = St[("w_rdy", s, e8)][1]
            for hc in range(4):
                lastf = (e8 == grp[-1] and hc == 3)
                hsl = T["hfA"].ap()[:, (eslot * 4 + hc) * GT + st4 * 128:
                                    (eslot * 4 + hc) * GT + st4 * 128 + 128]
                w = []
                if first:
                    w = rel + [("sACT", wrdy[0]), ("sDVE", wrdy[1]), wq]
                elif hc == 0:
                    w = [("sACT", wrdy[0]), ("sDVE", wrdy[1]), wq]
                em.on("pe", lambda e, hsl=hsl, w2=w2, hc=hc, half=half,
                      lastf=lastf, first=first, pb=pb: e.matmul(
                          pb, hsl,
                          w2[:, hc * D + half * 512: hc * D + half * 512 + 512],
                          start=first, stop=lastf),
                      wait=w, inc="sPE" if lastf else None)
                first = False
            if e8 != grp[-1]:
                yield
        y_d = em.c["sPE"]
        ysl = T["yacc"].ap()[:, st4 * D + half * 512: st4 * D + half * 512 + 512]
        if gi == 0:
            em.on("dve", lambda e, ysl=ysl, pb=pb: e.tensor_copy(ysl, pb),
                  wait=[("sPE", y_d)] + St.get(("y_rel", st4, half), []),
                  inc="sDVE")
        else:
            em.on("dve", lambda e, ysl=ysl, pb=pb: e.tensor_tensor(
                ysl, ysl, pb, ALU.add),
                wait=[("sPE", y_d), ("sDVE", St.get(("yw", st4, half), 0))],
                inc="sDVE")
        St[("yw", st4, half)] = em.c["sDVE"]
        release_bank(k, [("sDVE", em.c["sDVE"])])
        if st4 == NST - 1 and half == 1:
            St["hfA_free"] = y_d
            for e8 in grp:
                St[("w2_war", St[("w2_at", s, e8)])] = y_d
            if gi == len(EXPQ) - 1:
                St["ct_read_pe"] = y_d

    def u_ln2(s, st4):
        slot = st4 % 2
        x1rel = St.get(("x1tok_rel", slot), [])
        em.on("sync", lambda e, st4=st4: e.dma_start(
            out=T["x1tok"][slot].ap(),
            in_=x1T[(s * NST + st4) * 128:(s * NST + st4) * 128 + 128, :]),
            wait=x1rel + St[("x1w_done", s, NST - 1)], inc=("qTkA" if slot == 0 else "qTkB"), n=16)
        xr = ("qTkA" if slot == 0 else "qTkB", em.c["qTkA" if slot == 0 else "qTkB"])
        ysl = T["yacc"].ap()[:, st4 * D:(st4 + 1) * D]
        em.on("dve", lambda e, ysl=ysl, slot=slot: e.tensor_tensor(
            ysl, ysl, T["x1tok"][slot].ap(), ALU.add),
            wait=[xr, ("sDVE", St[("yw", st4, 0)]),
                  ("sDVE", St[("yw", st4, 1)])], inc="sDVE")
        s2_d = em.c["sDVE"]
        strel = St.get("stats_rel", [])
        em.on("act", lambda e, ysl=ysl, slot=slot: e.activation(
            T["x1tok"][slot].ap(), ysl, AF.Square, accum_out=stats[:, 1:2]),
            wait=[("sDVE", s2_d)] + strel, inc="sACT")
        sq_d = em.c["sACT"]
        em.on("dve", lambda e, ysl=ysl: e.reduce_sum(stats[:, 0:1], ysl, AX.X),
              wait=[("sDVE", s2_d)] + strel, inc="sDVE")
        em.on("dve", lambda e: e.tensor_scalar_mul(stats[:, 2:3], stats[:, 0:1], 1.0 / D),
              wait=[("sDVE", em.c["sDVE"])], inc="sDVE")
        em.on("dve", lambda e: e.tensor_mul(stats[:, 3:4], stats[:, 2:3], stats[:, 2:3]),
              wait=[("sDVE", em.c["sDVE"])], inc="sDVE")
        em.on("dve", lambda e: e.scalar_tensor_tensor(
            stats[:, 4:5], stats[:, 1:2], 1.0 / D, stats[:, 3:4], ALU.mult, ALU.subtract),
            wait=[("sACT", sq_d), ("sDVE", em.c["sDVE"])], inc="sDVE")
        var_d = em.c["sDVE"]
        em.on("act", lambda e: e.activation(stats[:, 5:6], stats[:, 4:5], AF.Ln,
                                            bias=T["epsT"].ap()),
              wait=[("sDVE", var_d)], inc="sACT")
        em.on("act", lambda e: e.activation(stats[:, 5:6], stats[:, 5:6], AF.Exp,
                                            scale=-0.5),
              wait=[("sACT", em.c["sACT"])], inc="sACT")
        rstd_d = em.c["sACT"]
        oslot = 0
        qo = "qOutA" if st4 % 2 == 0 else "qOutB"
        em.on("dve", lambda e, ysl=ysl, oslot=oslot: e.tensor_scalar(
            T["outT"][oslot].ap(), ysl, stats[:, 2:3], stats[:, 5:6],
            ALU.subtract, ALU.mult),
            wait=[("sACT", rstd_d), ("qOutA", em.c["qOutA"]),
                  ("qOutB", em.c["qOutB"]), ("sDVE", var_d)], inc="sDVE")
        ow = em.c["sDVE"]
        St["stats_rel"] = [("sACT", rstd_d), ("sDVE", ow)]
        St[("y_rel", st4, 0)] = [("sDVE", ow)]
        St[("y_rel", st4, 1)] = [("sDVE", ow)]
        St[("x1tok_rel", slot)] = [("sACT", sq_d), ("sDVE", s2_d)]
        em.on("sync", lambda e, s=s, st4=st4, oslot=oslot: e.dma_start(
            out=dram["out"].ap()[(s * NST + st4) * 128:(s * NST + st4) * 128 + 128, :],
            in_=T["outT"][oslot].ap()),
            wait=[("sDVE", ow)], inc=qo, n=16)

    def pump(c, budget):
        q = St["ffnq"]
        k = 0
        while k < budget:
            gen = St.get("ffngen")
            if gen is None:
                if not (q and q[0][0] <= c):
                    return
                _, fn = q.pop(0)
                r = fn()
                if hasattr(r, "__next__"):
                    St["ffngen"] = r
                    gen = r
                else:
                    k += 1
                    continue
            try:
                next(gen)
                k += 1
            except StopIteration:
                St["ffngen"] = None

    St["x1f_war"] = 0

    for s in range(4):
        ffn_stage_units(s)

    pf_gram(0)
    for _ in pf_steps(0):
        pass
    for c in range(NCH):
        tile_load(c + 2)
        if c % 16 < 4:
            start_block(4 * (c // 16) + 4 + c % 16)
        X = chain_ctx(c)
        chain_r(c, X)
        pf_gram(c + 1)
        chain_outq(c, X)
        pump(c, 3)
        solve = chain_solve(c, X)
        next(solve, None)
        steps = pf_steps(c + 1)
        next(steps, None)
        next(steps, None)
        next(solve, None)
        for fc4 in range(4):
            next(steps, None)
            if c < NCH - 1:
                chain_dw(c, X, [2 * fc4])
            next(steps, None)
            if c < NCH - 1:
                chain_dw(c, X, [2 * fc4 + 1])
            if fc4 % 2 == 1:
                pump(c, 2)
            drip(1)
        for si, _ in enumerate(steps):
            if si % 2 == 0:
                pump(c, 2)
            drip(1)
        if c < NCH - 1:
            chain_db(c, X)
        pump(c, 3)
        drip(1)
        if c % 16 == 5 and c // 16 < 3:
            stage_kq_dma(c // 16 + 1)
        if c % 16 == 6 and c // 16 < 3:
            stage_kq_ag(c // 16 + 3)
        if c % 4 == 3:
            stage_collective(c // 16, (c % 16) // 4)
    drip(10 ** 9)
    pump(NCH + 2, 10 ** 9)

    finals = [("qOutA", em.c["qOutA"]), ("qOutB", em.c["qOutB"])]
    em.on("sync", lambda e: e.nop(), wait=finals)
    em.on("pool", lambda e: e.nop(), wait=finals)


def kernel(L0_attn_in_seq, L0_attn_theta_K, L0_attn_theta_V, L0_attn_theta_Q,
           L0_attn_W0, L0_attn_b0, L0_ffn_routing_W, L0_ffn_expert_bias,
           L0_ffn_sW1, L0_ffn_sb1, L0_ffn_sW2, L0_ffn_sb2, L0_ffn_rW1,
           L0_ffn_rb1, L0_ffn_rW2, L0_ffn_rb2):
    f32 = np.float32
    x = np.asarray(L0_attn_in_seq, f32)
    Tk = np.asarray(L0_attn_theta_K, f32)
    Tv = np.asarray(L0_attn_theta_V, f32)
    Tq = np.asarray(L0_attn_theta_Q, f32)
    W0 = np.asarray(L0_attn_W0, f32)
    b0 = np.asarray(L0_attn_b0, f32)
    rW = np.asarray(L0_ffn_routing_W, f32)
    eb = np.asarray(L0_ffn_expert_bias, f32)
    sb1 = np.asarray(L0_ffn_sb1, f32)
    sb2 = np.asarray(L0_ffn_sb2, f32)
    rb1 = np.asarray(L0_ffn_rb1, f32)
    eta = np.float32(ETA)
    TkTf = np.ascontiguousarray(Tk.T)
    TqTf = np.ascontiguousarray(Tq.T)
    shared = dict(
        sW1=np.ascontiguousarray(np.asarray(L0_ffn_sW1, f32)),
        sb1T=np.ascontiguousarray(sb1.reshape(4, 128).T),
        sW2=np.ascontiguousarray(np.asarray(L0_ffn_sW2, f32)),
        sb2=np.ascontiguousarray(sb2.reshape(1, D)),
        rW1=np.ascontiguousarray(np.asarray(L0_ffn_rW1, f32)),
        rb1T=np.ascontiguousarray(rb1.reshape(8, 4, 128).transpose(2, 0, 1).reshape(128, 32)),
        rW2=np.ascontiguousarray(np.asarray(L0_ffn_rW2, f32)),
        rb2=np.ascontiguousarray(np.asarray(L0_ffn_rb2, f32)),
        rWT=np.ascontiguousarray(rW.T), ebr=np.ascontiguousarray(eb.reshape(1, 8)),
        cIdent=np.eye(128, dtype=f32),
        cMaskL=np.ascontiguousarray(eta * np.tril(np.ones((128, 128), f32), -1)),
        cMaskU=np.ascontiguousarray(eta * np.triu(np.ones((128, 128), f32), 1)),
        cMaskUI=np.ascontiguousarray(np.triu(np.ones((128, 128), f32), 0)),
        cIota=np.ascontiguousarray(np.tile(np.arange(8, dtype=f32), (128, 1))),
        cOnes=np.ones((128, 128), f32),
        cSel=np.ascontiguousarray(
            (np.arange(8)[:, None] == (np.arange(8 * 128) // 128)[None, :]).astype(f32)),
    )
    TvT = np.ascontiguousarray(Tv.T)
    W0T = np.ascontiguousarray(W0.T)
    xTb = [np.ascontiguousarray(x[b].T) for b in range(2)]

    in_maps = []
    for i in range(8):
        b, r = i // 4, i % 4
        sl = slice(r * ES, (r + 1) * ES)
        m = dict(shared)
        m["xT"] = xTb[b]
        m["TkT"] = np.ascontiguousarray(TkTf[:, sl])
        m["TqT"] = np.ascontiguousarray(TqTf[:, sl])
        m["xsl"] = np.ascontiguousarray(x[b][:, sl])
        m["TvTs"] = np.ascontiguousarray(TvT[:, sl])
        m["W0Ts"] = np.ascontiguousarray(W0T[:, sl])
        m["b0s"] = np.ascontiguousarray(b0[sl].reshape(1, ES))
        m["tokOff"] = np.array([[r * 4 * GT * ES]], dtype=np.int32)
        in_maps.append(m)

    nc = build_nc()
    res = run_bass_kernel_spmd(nc, in_maps, core_ids=list(range(8)))
    outp = np.empty((2, S, D), dtype=np.float32)
    for i in range(8):
        b, r = i // 4, i % 4
        o = res.results[i]["out"]
        for s in range(4):
            outp[b, 2048 * s + GT * r: 2048 * s + GT * (r + 1), :] = \
                o[s * GT:(s + 1) * GT, :]
    return outp
